# version 15
# baseline (speedup 1.0000x reference)
import numpy as np
import concourse.bass as bass
import concourse.mybir as mybir
from concourse.bass_utils import run_bass_kernel_spmd

F32 = mybir.dt.float32
BF16 = mybir.dt.bfloat16
AF = mybir.ActivationFunctionType
ALU = mybir.AluOpType
AX = mybir.AxisListType

NCORES = 8
NSEQ = 4
SL = 2048
T = SL // 128
D = 1024
C = D // 128
EPS = 1e-5

ENGS = ("tensor", "vector", "scalar", "gpsimd", "sync")
SECTIONS = []


class P:

    def __init__(self, nc):
        self.nc = nc
        self.q = {e: [] for e in ENGS}
        self._sems = {}

    def sem(self, name):
        self._sems[name] = [self.nc.alloc_semaphore(name), 0]
        return name

    def val(self, name):
        return self._sems[name][1]

    def wait(self, eng, name, v=None):
        h, cur = self._sems[name]
        if v is None:
            v = cur
        assert v <= cur, (name, v, cur)
        if v > 0:
            self.q[eng].append(lambda e, h=h, v=v: e.wait_ge(h, v))

    AUTO = {"vector": "V", "scalar": "S", "gpsimd": "G"}

    def op(self, eng, fn, inc=None, n=1):
        if eng in self.AUTO:
            own = self.AUTO[eng]
            self.wait(eng, own, self.val(own))
            if inc is None:
                inc = own
        if inc is None:
            self.q[eng].append(fn)
            return None
        rec = self._sems[inc]
        rec[1] += n
        h = rec[0]
        self.q[eng].append(lambda e, fn=fn, h=h, n=n: fn(e).then_inc(h, n))
        return rec[1]

    def barrier(self, engines=("tensor", "vector", "scalar", "gpsimd")):
        marks = {}
        for eng in engines:
            sname = "B" + eng[0]
            if sname not in self._sems:
                self.sem(sname)
            marks[eng] = (sname, self.op(eng, lambda e: e.nop(), inc=sname))
        for eng in engines:
            for other in engines:
                if other != eng:
                    self.wait(eng, marks[other][0], marks[other][1])

    def mark(self, name):
        SECTIONS.append((name, self.val("T")))

    def finalize(self):
        with self.nc.Block() as block:
            for eng in ENGS:
                def play(e, q=self.q[eng]):
                    for f in q:
                        f(e)
                getattr(block, eng)(play)



def _cm(w):
    K, N = w.shape
    assert K % 128 == 0
    return np.ascontiguousarray(
        w.reshape(K // 128, 128, N).transpose(1, 0, 2).reshape(128, (K // 128) * N))


def _row(v):
    return np.ascontiguousarray(np.asarray(v, np.float32).reshape(1, -1))



def emit_consts(p, t, f32id=True):
    g = "gpsimd"
    mv = p.op(g, lambda e: e.memset(t["ident_bf"][:], 0.0), inc="G")
    p.wait(g, "G", mv)
    p.op(g, lambda e: e.affine_select(out=t["ident_bf"][:], in_=t["ident_bf"][:],
                                      compare_op=ALU.not_equal, fill=1.0, base=0,
                                      pattern=[[-1, 128]], channel_multiplier=1), inc="G")
    if f32id:
        mv = p.op(g, lambda e: e.memset(t["ident_f32"][:], 0.0), inc="G")
        p.wait(g, "G", mv)
        p.op(g, lambda e: e.affine_select(out=t["ident_f32"][:], in_=t["ident_f32"][:],
                                          compare_op=ALU.not_equal, fill=1.0, base=0,
                                          pattern=[[-1, 128]], channel_multiplier=1), inc="G")
    p.op(g, lambda e: e.memset(t["ones_f32"][:], 1.0), inc="G")
    p.op(g, lambda e: e.memset(t["eps_col"][:], EPS), inc="G")
    if "shift48" in t:
        p.op(g, lambda e: e.memset(t["shift48"][:], -48.0), inc="G")
        p.op(g, lambda e: e.memset(t["shift24"][:], -24.0), inc="G")


def emit_bcast_row(p, t, row_ap, dst_ap, n):
    p.wait("tensor", "V", p.val("V"))
    tv = None
    for j in range(0, n, 512):
        w = min(512, n - j)
        tv = p.op("tensor",
                  lambda e, j=j, w=w: e.matmul(t["pA"][:, j:j + w], t["ones_f32"][:],
                                               row_ap[:, j:j + w], start=True, stop=True),
                  inc="T")
    p.wait("vector", "T", tv)
    return p.op("vector", lambda e: e.tensor_copy(out=dst_ap[:, 0:n], in_=t["pA"][:, 0:n]),
                inc="V")


def emit_load_weight(p, t, dram_ap, dst_bf_ap, ncols, ready=None):
    stage = t["stage"]
    piece = stage.shape[1]
    last_s = t.get("_stage_last")
    for j in range(0, ncols, piece):
        w = min(piece, ncols - j)
        if last_s is not None:
            p.wait("sync", "S", last_s)
        lv = p.op("sync", lambda e, j=j, w=w: e.dma_start(out=stage[:, 0:w],
                                                          in_=dram_ap[:, j:j + w]),
                  inc="Lw", n=16)
        p.wait("scalar", "Lw", lv)
        if ready is not None:
            p.wait("scalar", ready[0], ready[1])
            ready = None
        last_s = p.op("scalar", lambda e, j=j, w=w: e.copy(out=dst_bf_ap[:, j:j + w],
                                                           in_=stage[:, 0:w]), inc="S")
    t["_stage_last"] = last_s
    return ("S", last_s)


def emit_xfm_tile(p, t, x_dram, seq, ti, xfm_ap, also_f32=None):
    row0 = seq * SL + ti * 128
    lv = p.op("sync", lambda e: e.dma_start(out=t["x_t"][:], in_=x_dram[row0:row0 + 128, :]),
              inc="L", n=16)
    p.wait("scalar", "L", lv)
    sv = p.op("scalar", lambda e: e.copy(out=t["xbf"][:], in_=t["x_t"][:]), inc="S")
    p.wait("tensor", "S", sv)
    p.wait("tensor", "V", p.val("V"))
    ptr = t["ptrb"]
    tv = None
    for c in range(C):
        tv = p.op("tensor", lambda e, c=c: e.transpose(ptr[:, 128 * c:128 * c + 128],
                                                       t["xbf"][:, 128 * c:128 * c + 128],
                                                       t["ident_bf"][:]), inc="T")
    p.wait("vector", "T", tv)
    dst = xfm_ap.rearrange("p (c w) -> p c w", w=SL)[:, :, 128 * ti:128 * ti + 128]
    src = ptr.rearrange("p (c q) -> p c q", q=128)
    vv = p.op("vector", lambda e, dst=dst, src=src: e.tensor_copy(out=dst, in_=src), inc="V")
    marks = {"cvt": sv, "evac": vv}
    if also_f32 is not None:
        dst32 = also_f32
        p.wait("tensor", "V", p.val("V"))
        tv2 = None
        for c in range(C):
            tv2 = p.op("tensor", lambda e, c=c: e.transpose(t["pf32"][:, 128 * c:128 * c + 128],
                                                            t["x_t"][:, 128 * c:128 * c + 128],
                                                            t["ident_f32"][:]), inc="T")
        p.wait("vector", "T", tv2)
        marks["f32"] = p.op("vector",
                            lambda e, dst32=dst32: e.tensor_copy(out=dst32, in_=t["pf32"][:]),
                            inc="V")
    return marks


def emit_ln_stats(p, t, src_ap, mu_col, rstd_col, nfree):
    p.wait("scalar", "V", p.val("V"))
    rv = p.op("vector", lambda e: e.reduce_sum(t["red1"][:], src_ap, axis=AX.X), inc="V")
    p.wait("vector", "V", rv)
    p.op("vector", lambda e: e.tensor_scalar(out=mu_col, in0=t["red1"][:],
                                             scalar1=1.0 / nfree, scalar2=None, op0=ALU.mult))
    sv = p.op("scalar", lambda e: e.activation(out=t["sqd"][:], in_=src_ap, func=AF.Square,
                                               accum_out=t["red2"][:]), inc="S")
    p.wait("vector", "S", sv)
    p.op("vector", lambda e: e.tensor_scalar(out=t["red2"][:], in0=t["red2"][:],
                                             scalar1=1.0 / nfree, scalar2=None, op0=ALU.mult))
    p.op("vector", lambda e: e.tensor_tensor(out=t["red3"][:], in0=mu_col, in1=mu_col,
                                             op=ALU.mult))
    vv = p.op("vector", lambda e: e.tensor_tensor(out=t["red2"][:], in0=t["red2"][:],
                                                  in1=t["red3"][:], op=ALU.subtract), inc="V")
    p.wait("scalar", "V", vv)
    p.op("scalar", lambda e: e.activation(out=rstd_col, in_=t["red2"][:], func=AF.Ln,
                                          bias=t["eps_col"][:], scale=1.0))
    sv2 = p.op("scalar", lambda e: e.activation(out=rstd_col, in_=rstd_col, func=AF.Exp,
                                                scale=-0.5), inc="S")
    p.wait("vector", "S", sv2)



def build_launch_A(dump=None):
    nc = bass.Bass("TRN2", target_bir_lowering=False)
    x_dram = nc.dram_tensor("x", [NSEQ * SL, D], F32, kind="ExternalInput").ap()
    y_dram = nc.dram_tensor("y", [NSEQ * SL, D], F32, kind="ExternalOutput").ap()
    wq_d = nc.dram_tensor("wq", [128, C * D], F32, kind="ExternalInput").ap()
    wk_d = nc.dram_tensor("wk", [128, C * D], F32, kind="ExternalInput").ap()
    wv_d = nc.dram_tensor("wv", [128, C * D], F32, kind="ExternalInput").ap()
    wo_d = nc.dram_tensor("wo", [128, C * D], F32, kind="ExternalInput").ap()
    bo_d = nc.dram_tensor("bo", [1, D], F32, kind="ExternalInput").ap()
    gam_d = nc.dram_tensor("gam", [1, D], F32, kind="ExternalInput").ap()
    bet_d = nc.dram_tensor("bet", [1, D], F32, kind="ExternalInput").ap()
    dbg_d = None
    if dump is not None:
        cfg = {"consts": (128, F32), "bc": (D, F32), "xfm1": (C * SL, BF16),
               "xfm": (C * SL, BF16), "ktm": (T * D, BF16), "vtm": (T * D, BF16),
               "ctx": (C * D, BF16)}[dump]
        dbg_d = nc.dram_tensor("dbg", [128, cfg[0]], cfg[1], kind="ExternalOutput").ap()

    t = {}
    A = nc.alloc_sbuf_tensor
    t["stage"] = A("stage", [128, 2048], F32)[:]
    t["W0"] = A("W0", [128, C * D], BF16)[:]
    t["W1"] = A("W1", [128, C * D], BF16)[:]
    t["xfm"] = A("xfm", [128, C * SL], BF16)[:]
    t["ktm"] = A("ktm", [128, T * D], BF16)[:]
    t["vtm"] = A("vtm", [128, T * D], BF16)[:]
    t["ctx"] = A("ctx", [128, C * D], BF16)[:]
    t["kexp_f"] = A("kexp_f", [128, SL], BF16)[:]
    t["x_t"] = A("x_t", [128, D], F32)[:]
    t["xbf"] = A("xbf", [128, D], BF16)[:]
    t["qexp"] = A("qexp", [128, D], BF16)[:]
    t["qfm"] = A("qfm", [128, D], BF16)[:]
    t["attnb"] = A("attnb", [128, D], BF16)[:]
    t["afm"] = A("afm", [128, D], BF16)[:]
    t["r1"] = A("r1", [128, D], F32)[:]
    t["sqd"] = A("sqd", [128, D], F32)[:]
    t["xnew"] = A("xnew", [128, D], F32)[:]
    t["bo_bc"] = A("bo_bc", [128, D], F32)[:]
    t["gam_bc"] = A("gam_bc", [128, D], F32)[:]
    t["bet_bc"] = A("bet_bc", [128, D], F32)[:]
    t["row"] = A("rowbuf", [1, D], F32)[:]
    t["ident_bf"] = A("ident_bf", [128, 128], BF16)[:]
    t["ident_f32"] = A("ident_f32", [128, 128], F32)[:]
    t["ones_f32"] = A("ones_f32", [1, 128], F32)[:]
    t["eps_col"] = A("eps_col", [128, 1], F32)[:]
    t["nmax4"] = A("nmax4", [128, 4], F32)[:]
    t["nmax1"] = A("nmax1", [128, 1], F32)[:]
    t["zc2"] = A("zc2", [128, 2], F32)[:]
    t["zc"] = A("zc", [128, 1], F32)[:]
    t["rzk"] = A("rzk", [128, C], F32)[:]
    t["rzq"] = A("rzq", [128, 1], F32)[:]
    t["mu1"] = A("mu1", [128, 1], F32)[:]
    t["rstd1"] = A("rstd1", [128, 1], F32)[:]
    t["mu2"] = A("mu2", [128, 1], F32)[:]
    t["rstd2"] = A("rstd2", [128, 1], F32)[:]
    t["red1"] = A("red1", [128, 1], F32)[:]
    t["red2"] = A("red2", [128, 1], F32)[:]
    t["red3"] = A("red3", [128, 1], F32)[:]

    AP_ = nc.alloc_psum_tensor
    t["pA"] = AP_("pA", [128, D], F32)[:]
    t["pB"] = AP_("pB", [128, D], F32)[:]
    t["ptrb"] = AP_("ptrb", [128, D], BF16)[:]

    p = P(nc)
    for s in ("L", "Lw", "Lr", "O", "T", "V", "S", "G"):
        p.sem(s)

    emit_consts(p, t)
    gv = p.val("G")
    p.wait("tensor", "G", gv)
    p.wait("scalar", "G", gv)
    p.wait("vector", "G", gv)

    if dump == "consts":
        p.wait("gpsimd", "G", gv)
        p.op("gpsimd", lambda e: e.dma_start(out=dbg_d[:], in_=t["ident_f32"][:]), inc="O", n=16)
        p.wait("gpsimd", "O", p.val("O"))
        p.finalize()
        return nc

    for name, dst in (("bo", "bo_bc"), ("gam", "gam_bc"), ("bet", "bet_bc")):
        d_ap = {"bo": bo_d, "gam": gam_d, "bet": bet_d}[name]
        p.wait("sync", "T", p.val("T"))
        lv = p.op("sync", lambda e, d_ap=d_ap: e.dma_start(out=t["row"][:], in_=d_ap[:]),
                  inc="Lr", n=16)
        p.wait("tensor", "Lr", lv)
        emit_bcast_row(p, t, t["row"], t[dst], D)

    def _dump_and_finish(arena_key):
        p.barrier(engines=("tensor", "vector", "scalar", "gpsimd"))
        src_ap = t[arena_key]
        p.wait("gpsimd", "V", p.val("V"))
        p.wait("gpsimd", "S", p.val("S"))
        p.op("gpsimd", lambda e: e.dma_start(out=dbg_d[:], in_=src_ap), inc="O", n=16)
        p.wait("gpsimd", "O", p.val("O"))
        p.finalize()

    if dump == "bc":
        _dump_and_finish("bo_bc")
        return nc

    wk_ready = emit_load_weight(p, t, wk_d, t["W0"], C * D)
    wv_ready = emit_load_weight(p, t, wv_d, t["W1"], C * D)

    n_store = 0
    for seq in range(NSEQ):
        for ti in range(1 if dump == "xfm1" else T):
            p.wait("sync", "S", p.val("S"))
            p.wait("sync", "V", p.val("V"))
            emit_xfm_tile(p, t, x_dram, seq, ti, t["xfm"])
        if dump in ("xfm", "xfm1"):
            _dump_and_finish("xfm")
            return nc

        p.wait("tensor", wk_ready[0], wk_ready[1])
        for c in range(C):
            p.wait("tensor", "S", p.val("S"))
            p.wait("tensor", "V", p.val("V"))
            tv = None
            for tc in range(4):
                dstp = t["pA"] if tc < 2 else t["pB"]
                off = 512 * (tc % 2)
                for cin in range(C):
                    tv = p.op("tensor", lambda e, c=c, tc=tc, cin=cin, dstp=dstp, off=off:
                              e.matmul(dstp[:, off:off + 512],
                                       t["W0"][:, D * cin + 128 * c: D * cin + 128 * c + 128],
                                       t["xfm"][:, SL * cin + 512 * tc: SL * cin + 512 * tc + 512],
                                       start=(cin == 0), stop=(cin == C - 1)), inc="T")
            p.wait("vector", "T", tv)
            rv = None
            for tc in range(4):
                srcp = t["pA"] if tc < 2 else t["pB"]
                off = 512 * (tc % 2)
                rv = p.op("vector", lambda e, tc=tc, srcp=srcp, off=off: e.reduce_max(
                    t["nmax4"][:, tc:tc + 1], srcp[:, off:off + 512],
                    axis=AX.X, negate=True), inc="V")
            p.wait("vector", "V", rv)
            rv2 = p.op("vector", lambda e: e.tensor_reduce(
                t["nmax1"][:], t["nmax4"][:], axis=AX.X, op=ALU.min), inc="V")
            p.wait("scalar", "V", rv2)
            p.op("scalar", lambda e: e.activation(
                out=t["kexp_f"][:, 0:1024], in_=t["pA"][:], func=AF.Exp,
                bias=t["nmax1"][:], scale=1.0, accum_out=t["zc2"][:, 0:1]))
            sv = p.op("scalar", lambda e: e.activation(
                out=t["kexp_f"][:, 1024:2048], in_=t["pB"][:], func=AF.Exp,
                bias=t["nmax1"][:], scale=1.0, accum_out=t["zc2"][:, 1:2]), inc="S")
            p.wait("vector", "S", sv)
            p.op("vector", lambda e: e.tensor_tensor(out=t["zc"][:], in0=t["zc2"][:, 0:1],
                                                     in1=t["zc2"][:, 1:2], op=ALU.add))
            p.op("vector", lambda e, c=c: e.reciprocal(t["rzk"][:, c:c + 1], t["zc"][:]))
            p.wait("tensor", "S", sv)
            for r in range(2):
                p.wait("tensor", "V", p.val("V"))
                tv = None
                for j in range(8):
                    tv = p.op("tensor", lambda e, r=r, j=j: e.transpose(
                        t["ptrb"][:, 128 * j:128 * j + 128],
                        t["kexp_f"][:, 1024 * r + 128 * j: 1024 * r + 128 * j + 128],
                        t["ident_bf"][:]), inc="T")
                p.wait("vector", "T", tv)
                dst = t["ktm"].rearrange("p (t w) -> p t w", w=D)[:, 8 * r:8 * r + 8,
                                                                 128 * c:128 * c + 128]
                src = t["ptrb"].rearrange("p (q w) -> p q w", w=128)
                p.op("vector", lambda e, dst=dst, src=src: e.tensor_copy(out=dst, in_=src),
                     inc="V")
        if dump == "ktm":
            _dump_and_finish("ktm")
            return nc

        p.wait("tensor", wv_ready[0], wv_ready[1])
        for ti in range(T):
            p.wait("tensor", "V", p.val("V"))
            p.wait("tensor", "S", p.val("S"))
            tv = None
            for half in range(2):
                for cin in range(C):
                    tv = p.op("tensor", lambda e, ti=ti, half=half, cin=cin: e.matmul(
                        t["pA"][:, 512 * half:512 * half + 512],
                        t["xfm"][:, SL * cin + 128 * ti: SL * cin + 128 * ti + 128],
                        t["W1"][:, D * cin + 512 * half: D * cin + 512 * half + 512],
                        start=(cin == 0), stop=(cin == C - 1)), inc="T")
            p.wait("vector", "T", tv)
            p.op("vector", lambda e, ti=ti: e.tensor_copy(
                out=t["vtm"][:, D * ti:D * ti + D], in_=t["pA"][:]), inc="V")
        if dump == "vtm":
            _dump_and_finish("vtm")
            return nc

        wq_ready = emit_load_weight(p, t, wq_d, t["W0"], C * D, ready=("T", p.val("T")))
        wo_ready = emit_load_weight(p, t, wo_d, t["W1"], C * D)

        for c in range(C):
            p.wait("tensor", "V", p.val("V"))
            p.wait("tensor", "S", p.val("S"))
            tv = None
            for half in range(2):
                for ti in range(T):
                    tv = p.op("tensor", lambda e, c=c, half=half, ti=ti: e.matmul(
                        t["pB"][:, 512 * half:512 * half + 512],
                        t["ktm"][:, D * ti + 128 * c: D * ti + 128 * c + 128],
                        t["vtm"][:, D * ti + 512 * half: D * ti + 512 * half + 512],
                        start=(ti == 0), stop=(ti == T - 1)), inc="T")
            p.wait("vector", "T", tv)
            p.op("vector", lambda e, c=c: e.tensor_scalar(
                out=t["ctx"][:, D * c:D * c + D], in0=t["pB"][:],
                scalar1=t["rzk"][:, c:c + 1], scalar2=None, op0=ALU.mult), inc="V")
        if dump == "ctx":
            _dump_and_finish("ctx")
            return nc
        p.wait("tensor", wq_ready[0], wq_ready[1])
        p.wait("tensor", wo_ready[0], wo_ready[1])

        for ti in range(T):
            p.wait("tensor", "S", p.val("S"))
            p.wait("tensor", "V", p.val("V"))
            tv = None
            for half in range(2):
                for cin in range(C):
                    tv = p.op("tensor", lambda e, ti=ti, half=half, cin=cin: e.matmul(
                        t["pA"][:, 512 * half:512 * half + 512],
                        t["xfm"][:, SL * cin + 128 * ti: SL * cin + 128 * ti + 128],
                        t["W0"][:, D * cin + 512 * half: D * cin + 512 * half + 512],
                        start=(cin == 0), stop=(cin == C - 1)), inc="T")
            p.wait("vector", "T", tv)
            rv = p.op("vector", lambda e: e.reduce_max(
                t["nmax1"][:], t["pA"][:], axis=AX.X, negate=True), inc="V")
            p.wait("scalar", "V", rv)
            sv = p.op("scalar", lambda e: e.activation(
                out=t["qexp"][:], in_=t["pA"][:], func=AF.Exp,
                bias=t["nmax1"][:], scale=1.0, accum_out=t["zc"][:]), inc="S")
            p.wait("vector", "S", sv)
            p.op("vector", lambda e: e.reciprocal(t["rzq"][:], t["zc"][:]))
            p.wait("tensor", "S", sv)
            p.wait("tensor", "V", p.val("V"))
            tv = None
            for c in range(C):
                tv = p.op("tensor", lambda e, c=c: e.transpose(
                    t["ptrb"][:, 128 * c:128 * c + 128],
                    t["qexp"][:, 128 * c:128 * c + 128], t["ident_bf"][:]), inc="T")
            p.wait("vector", "T", tv)
            qv = p.op("vector", lambda e: e.tensor_copy(out=t["qfm"][:], in_=t["ptrb"][:]),
                      inc="V")
            p.wait("tensor", "V", qv)
            tv = None
            for half in range(2):
                for c in range(C):
                    tv = p.op("tensor", lambda e, half=half, c=c: e.matmul(
                        t["pB"][:, 512 * half:512 * half + 512],
                        t["qfm"][:, 128 * c:128 * c + 128],
                        t["ctx"][:, D * c + 512 * half: D * c + 512 * half + 512],
                        start=(c == 0), stop=(c == C - 1)), inc="T")
            p.wait("vector", "T", tv)
            av = p.op("vector", lambda e: e.tensor_scalar(
                out=t["attnb"][:], in0=t["pB"][:], scalar1=t["rzq"][:],
                scalar2=float(D) ** -0.5, op0=ALU.mult, op1=ALU.mult), inc="V")
            p.wait("tensor", "V", av)
            tv = None
            for c in range(C):
                tv = p.op("tensor", lambda e, c=c: e.transpose(
                    t["ptrb"][:, 128 * c:128 * c + 128],
                    t["attnb"][:, 128 * c:128 * c + 128], t["ident_bf"][:]), inc="T")
            p.wait("vector", "T", tv)
            fv = p.op("vector", lambda e: e.tensor_copy(out=t["afm"][:], in_=t["ptrb"][:]),
                      inc="V")
            p.wait("tensor", "V", fv)
            tv = None
            for half in range(2):
                for c in range(C):
                    tv = p.op("tensor", lambda e, half=half, c=c: e.matmul(
                        t["pA"][:, 512 * half:512 * half + 512],
                        t["afm"][:, 128 * c:128 * c + 128],
                        t["W1"][:, D * c + 512 * half: D * c + 512 * half + 512],
                        start=(c == 0), stop=(c == C - 1)), inc="T")
            lv = p.op("sync", lambda e, seq=seq, ti=ti: e.dma_start(
                out=t["x_t"][:], in_=x_dram[seq * SL + 128 * ti: seq * SL + 128 * ti + 128, :]),
                inc="L", n=16)
            p.wait("vector", "T", tv)
            p.wait("vector", "L", lv)
            p.op("vector", lambda e: e.tensor_tensor(out=t["r1"][:], in0=t["pA"][:],
                                                     in1=t["bo_bc"][:], op=ALU.add))
            p.op("vector", lambda e: e.tensor_tensor(out=t["r1"][:], in0=t["r1"][:],
                                                     in1=t["x_t"][:], op=ALU.add), inc="V")
            emit_ln_stats(p, t, t["r1"][:], t["mu1"][:], t["rstd1"][:], D)
            p.op("vector", lambda e: e.tensor_scalar(
                out=t["xnew"][:], in0=t["r1"][:], scalar1=t["mu1"][:],
                scalar2=t["rstd1"][:], op0=ALU.subtract, op1=ALU.mult))
            p.op("vector", lambda e: e.tensor_tensor(out=t["xnew"][:], in0=t["xnew"][:],
                                                     in1=t["gam_bc"][:], op=ALU.mult))
            p.op("vector", lambda e: e.tensor_tensor(out=t["xnew"][:], in0=t["xnew"][:],
                                                     in1=t["bet_bc"][:], op=ALU.add))
            p.op("vector", lambda e: e.tensor_tensor(out=t["r1"][:], in0=t["xnew"][:],
                                                     in1=t["x_t"][:], op=ALU.add), inc="V")
            emit_ln_stats(p, t, t["r1"][:], t["mu2"][:], t["rstd2"][:], D)
            p.wait("vector", "O", p.val("O"))
            vv = p.op("vector", lambda e: e.tensor_scalar(
                out=t["xnew"][:], in0=t["r1"][:], scalar1=t["mu2"][:],
                scalar2=t["rstd2"][:], op0=ALU.subtract, op1=ALU.mult), inc="V")
            p.wait("gpsimd", "V", vv)
            p.op("gpsimd", lambda e, seq=seq, ti=ti: e.dma_start(
                out=y_dram[seq * SL + 128 * ti: seq * SL + 128 * ti + 128, :],
                in_=t["xnew"][:]), inc="O", n=16)
            n_store += 1
            p.wait("sync", "V", p.val("V"))
        if seq < NSEQ - 1:
            wk_ready = emit_load_weight(p, t, wk_d, t["W0"], C * D, ready=("T", p.val("T")))
            wv_ready = emit_load_weight(p, t, wv_d, t["W1"], C * D)

    p.wait("gpsimd", "O", 16 * n_store)
    p.finalize()
    return nc


def host_inputs_A(full):
    x = np.asarray(full["L0_attn_x"], dtype=np.float32).reshape(32 * SL, D)
    wq = _cm(np.asarray(full["L0_attn_Wq"], np.float32))
    wk = _cm(np.asarray(full["L0_attn_Wk"], np.float32))
    wv = _cm(np.asarray(full["L0_attn_Wv"], np.float32))
    wo = _cm(np.asarray(full["L0_attn_Wo"], np.float32))
    bo = _row(full["L0_attn_bo"])
    gam = _row(full["L0_attn_gamma"])
    bet = _row(full["L0_attn_beta"])
    maps = []
    for core in range(NCORES):
        xs = np.ascontiguousarray(x[core * NSEQ * SL:(core + 1) * NSEQ * SL])
        maps.append({"x": xs, "wq": wq, "wk": wk, "wv": wv, "wo": wo,
                     "bo": bo, "gam": gam, "bet": bet})
    return maps



def build_launch_B(dump=None):
    nc = bass.Bass("TRN2", target_bir_lowering=False)
    x_dram = nc.dram_tensor("x", [NSEQ * SL, D], F32, kind="ExternalInput").ap()
    y_dram = nc.dram_tensor("y", [NSEQ * SL, D], F32, kind="ExternalOutput").ap()
    dr = {}
    for pfx in ("q", "k", "v"):
        dr[pfx + "_nwT"] = nc.dram_tensor(pfx + "_nwT", [128, C * 15], F32, kind="ExternalInput").ap()
        dr[pfx + "_nb"] = nc.dram_tensor(pfx + "_nb", [1, 15], F32, kind="ExternalInput").ap()
        dr[pfx + "_w1"] = nc.dram_tensor(pfx + "_w1", [128, C * D], F32, kind="ExternalInput").ap()
        dr[pfx + "_b1"] = nc.dram_tensor(pfx + "_b1", [1, D], F32, kind="ExternalInput").ap()
        dr[pfx + "_w2"] = nc.dram_tensor(pfx + "_w2", [128, C * D], F32, kind="ExternalInput").ap()
        dr[pfx + "_b2"] = nc.dram_tensor(pfx + "_b2", [16, D], F32, kind="ExternalInput").ap()
    dbg_d = None
    if dump is not None:
        cfg = {"sel": (3 * T * 128, BF16), "kfm": (C * SL, BF16),
               "vtm": (T * D, BF16)}[dump]
        dbg_d = nc.dram_tensor("dbg", [16 if dump == "sel" else 128, cfg[0]], cfg[1],
                               kind="ExternalOutput").ap()

    t = {}
    A = nc.alloc_sbuf_tensor
    t["stage"] = A("stage", [128, 1024], F32)[:]
    t["xfm"] = A("xfm", [128, C * SL], BF16)[:]
    t["kfm"] = A("kfm", [128, C * SL], BF16)[:]
    t["vtm"] = A("vtm", [128, T * D], BF16)[:]
    t["W1"] = A("W1", [128, C * D], BF16)[:]
    t["W2"] = A("W2", [128, C * D], BF16)[:]
    t["nwT"] = A("nwT", [128, C * 15 * 3], F32)[:]
    t["nbbc"] = A("nbbc", [128, 15 * 3], F32)[:]
    t["b1bc"] = A("b1bc", [128, D], F32)[:]
    t["b2bf"] = A("b2bf", [16, D * 3], BF16)[:]
    t["Ebf"] = A("Ebf", [16, D], BF16)[:]
    t["seltr"] = A("seltr", [16, 3 * T * 128], BF16)[:]
    t["x_t"] = A("x_t", [128, D], F32)[:]
    t["xbf"] = A("xbf", [128, D], BF16)[:]
    t["xtr32"] = A("xtr32", [128, D], F32)[:]
    t["scoreb"] = A("scoreb", [128, 45], F32)[:]
    t["bits"] = A("bits", [128, 4], F32)[:]
    t["nbits"] = A("nbits", [128, 4], F32)[:]
    t["sig"] = A("sig", [128, 1], F32)[:]
    t["tmp1"] = A("tmp1", [128, 1], F32)[:]
    t["p2"] = A("p2", [128, 4], F32)[:]
    t["p3g"] = A("p3g", [128, 8], F32)[:]
    t["tmp8"] = A("tmp8", [128, 8], F32)[:]
    t["sel16"] = A("sel16", [128, 16], F32)[:]
    t["selbf"] = A("selbf", [128, 16], BF16)[:]
    t["hf32"] = A("hf32", [128, D], F32)[:]
    t["Hbf"] = A("Hbf", [128, D], BF16)[:]
    t["maskbf"] = A("maskbf", [128, D], BF16)[:]
    t["Hm"] = A("Hm", [128, D], BF16)[:]
    t["Hfm"] = A("Hfm", [128, D], BF16)[:]
    t["ffout"] = A("ffout", [128, D], BF16)[:]
    t["qfm_t"] = A("qfm_t", [128, D], BF16)[:]
    t["attn_tm"] = A("attn_tm", [128, SL], BF16)[:]
    t["attn_km"] = A("attn_km", [128, SL], BF16)[:]
    t["r1"] = A("r1", [128, D], F32)[:]
    t["xnew"] = A("xnew", [128, D], F32)[:]
    t["row"] = A("rowbuf", [1, D], F32)[:]
    t["ident_bf"] = A("ident_bf", [128, 128], BF16)[:]
    t["ident_f32"] = A("ident_f32", [128, 128], F32)[:]
    t["ones_f32"] = A("ones_f32", [1, 128], F32)[:]
    t["eps_col"] = A("eps_col", [128, 1], F32)[:]
    t["nmax4"] = A("nmax4", [128, 4], F32)[:]
    t["nmax1"] = A("nmax1", [128, 1], F32)[:]
    t["zc2"] = A("zc2", [128, 2], F32)[:]
    t["zc"] = A("zc", [128, 1], F32)[:]
    t["rza"] = A("rza", [128, 1], F32)[:]
    t["mu2"] = A("mu2", [128, 1], F32)[:]
    t["rstd2"] = A("rstd2", [128, 1], F32)[:]
    t["red1"] = A("red1", [128, 1], F32)[:]
    t["red2"] = A("red2", [128, 1], F32)[:]
    t["red3"] = A("red3", [128, 1], F32)[:]

    AP_ = nc.alloc_psum_tensor
    t["sqd"] = t["hf32"]
    t["pA"] = AP_("pA", [128, D], F32)[:]
    t["pB"] = AP_("pB", [128, D], F32)[:]
    t["ptrb"] = AP_("ptrb", [128, D], BF16)[:]
    t["pf32"] = AP_("pf32", [128, D], F32)[:]
    t["ptrk"] = AP_("ptrk", [128, D], BF16)[:]

    p = P(nc)
    for s in ("L", "Lw", "Lr", "O", "T", "V", "S", "G"):
        p.sem(s)

    emit_consts(p, t)
    mv = p.op("gpsimd", lambda e: e.memset(t["mask_bfm"][:], 0.0), inc="G")
    p.wait("gpsimd", "G", mv)
    p.op("gpsimd", lambda e: e.affine_select(out=t["mask_bfm"][:], in_=t["mask_bfm"][:],
                                             compare_op=ALU.is_ge, fill=-10000.0, base=0,
                                             pattern=[[-1, 128]], channel_multiplier=1), inc="G")
    mv = p.op("gpsimd", lambda e: e.memset(t["Ebf"][:], 1.0), inc="G")
    p.wait("gpsimd", "G", mv)
    p.op("gpsimd", lambda e: e.affine_select(out=t["Ebf"][:], in_=t["Ebf"][:],
                                             compare_op=ALU.is_ge, fill=0.0, base=0,
                                             pattern=[[1, D]], channel_multiplier=-64),
         inc="G")
    p.op("gpsimd", lambda e: e.affine_select(out=t["Ebf"][:], in_=t["Ebf"][:],
                                             compare_op=ALU.is_ge, fill=0.0, base=63,
                                             pattern=[[-1, D]], channel_multiplier=64),
         inc="G")
    gv = p.val("G")
    for eng in ("tensor", "scalar", "vector"):
        p.wait(eng, "G", gv)

    for i, pfx in enumerate(("q", "k", "v")):
        lv = p.op("sync", lambda e, pfx=pfx, i=i: e.dma_start(
            out=t["nwT"][:, C * 15 * i: C * 15 * (i + 1)], in_=dr[pfx + "_nwT"][:]),
            inc="Lw", n=16)
        p.wait("sync", "T", p.val("T"))
        lr = p.op("sync", lambda e, pfx=pfx: e.dma_start(out=t["row"][:, 0:15],
                                                         in_=dr[pfx + "_nb"][:]), inc="Lr", n=16)
        p.wait("tensor", "Lr", lr)
        p.wait("tensor", "Lw", lv)
        emit_bcast_row(p, t, t["row"][:, 0:15], t["nbbc"][:, 15 * i:15 * i + 15], 15)
        if t.get("_stage_last") is not None:
            p.wait("sync", "S", t["_stage_last"])
        lb = p.op("sync", lambda e, pfx=pfx: e.dma_start(out=t["stage"][0:16, :],
                                                         in_=dr[pfx + "_b2"][:]), inc="Lw", n=16)
        p.wait("scalar", "Lw", lb)
        t["_stage_last"] = p.op("scalar", lambda e, i=i: e.copy(
            out=t["b2bf"][:, D * i:D * (i + 1)], in_=t["stage"][0:16, :]), inc="S")

    def fff_scores_routing(pidx, ti):
        off = 128 * pidx
        sc = t["scoreb"][:, 15 * pidx:15 * pidx + 15]
        p.wait("tensor", "V", p.val("V"))
        tv = None
        for cin in range(C):
            tv = p.op("tensor", lambda e, cin=cin, off=off, pidx=pidx: e.matmul(
                t["pA"][:, off:off + 15],
                t["xtr32"][:, 128 * cin:128 * cin + 128],
                t["nwT"][:, C * 15 * pidx + 15 * cin: C * 15 * pidx + 15 * cin + 15],
                start=(cin == 0), stop=(cin == C - 1)), inc="T")
        p.wait("vector", "T", tv)
        p.op("vector", lambda e, sc=sc, off=off, pidx=pidx: e.tensor_tensor(
            out=sc, in0=t["pA"][:, off:off + 15],
            in1=t["nbbc"][:, 15 * pidx:15 * pidx + 15], op=ALU.add))
        b = t["bits"]; nb = t["nbits"]
        p.op("vector", lambda e, sc=sc: e.tensor_scalar(
            out=b[:, 0:1], in0=sc[:, 0:1], scalar1=0.0, scalar2=None, op0=ALU.is_ge))
        p.op("vector", lambda e, sc=sc: e.tensor_tensor(
            out=t["tmp1"][:], in0=sc[:, 2:3], in1=sc[:, 1:2], op=ALU.subtract))
        p.op("vector", lambda e, sc=sc: e.scalar_tensor_tensor(
            out=t["sig"][:], in0=t["tmp1"][:], scalar=b[:, 0:1], in1=sc[:, 1:2],
            op0=ALU.mult, op1=ALU.add))
        p.op("vector", lambda e: e.tensor_scalar(
            out=b[:, 1:2], in0=t["sig"][:], scalar1=0.0, scalar2=None, op0=ALU.is_ge))
        p.op("vector", lambda e: e.tensor_scalar(
            out=nb[:, 0:2], in0=b[:, 0:2], scalar1=-1.0, scalar2=1.0,
            op0=ALU.mult, op1=ALU.add))
        p.op("vector", lambda e: e.tensor_tensor(out=t["p2"][:, 0:1], in0=nb[:, 0:1],
                                                 in1=nb[:, 1:2], op=ALU.mult))
        p.op("vector", lambda e: e.tensor_tensor(out=t["p2"][:, 1:2], in0=nb[:, 0:1],
                                                 in1=b[:, 1:2], op=ALU.mult))
        p.op("vector", lambda e: e.tensor_tensor(out=t["p2"][:, 2:3], in0=b[:, 0:1],
                                                 in1=nb[:, 1:2], op=ALU.mult))
        p.op("vector", lambda e: e.tensor_tensor(out=t["p2"][:, 3:4], in0=b[:, 0:1],
                                                 in1=b[:, 1:2], op=ALU.mult))
        p.op("vector", lambda e, sc=sc: e.tensor_tensor(
            out=t["tmp8"][:, 0:4], in0=t["p2"][:], in1=sc[:, 3:7], op=ALU.mult))
        p.op("vector", lambda e: e.reduce_sum(t["sig"][:], t["tmp8"][:, 0:4], axis=AX.X))
        p.op("vector", lambda e: e.tensor_scalar(
            out=b[:, 2:3], in0=t["sig"][:], scalar1=0.0, scalar2=None, op0=ALU.is_ge))
        p.op("vector", lambda e: e.tensor_scalar(
            out=nb[:, 2:3], in0=b[:, 2:3], scalar1=-1.0, scalar2=1.0,
            op0=ALU.mult, op1=ALU.add))
        p.op("vector", lambda e: e.tensor_scalar(
            out=t["p3g"][:, 0:4], in0=t["p2"][:], scalar1=nb[:, 2:3], scalar2=None,
            op0=ALU.mult))
        p.op("vector", lambda e: e.tensor_scalar(
            out=t["p3g"][:, 4:8], in0=t["p2"][:], scalar1=b[:, 2:3], scalar2=None,
            op0=ALU.mult))
        sv1 = sc[:, 7:15].rearrange("p (a b) -> p a b", b=2)[:, :, 0:1]
        sv2 = sc[:, 7:15].rearrange("p (a b) -> p a b", b=2)[:, :, 1:2]
        p.op("vector", lambda e, sv1=sv1: e.tensor_tensor(
            out=t["tmp8"][:, 0:4].rearrange("p (a b) -> p a b", b=1), in0=t["p3g"][:, 0:4].rearrange("p (a b) -> p a b", b=1),
            in1=sv1, op=ALU.mult))
        p.op("vector", lambda e, sv2=sv2: e.tensor_tensor(
            out=t["tmp8"][:, 4:8].rearrange("p (a b) -> p a b", b=1), in0=t["p3g"][:, 4:8].rearrange("p (a b) -> p a b", b=1),
            in1=sv2, op=ALU.mult))
        p.op("vector", lambda e: e.reduce_sum(t["sig"][:], t["tmp8"][:], axis=AX.X))
        p.op("vector", lambda e: e.tensor_scalar(
            out=b[:, 3:4], in0=t["sig"][:], scalar1=0.0, scalar2=None, op0=ALU.is_ge))
        p.op("vector", lambda e: e.tensor_scalar(
            out=nb[:, 3:4], in0=b[:, 3:4], scalar1=-1.0, scalar2=1.0,
            op0=ALU.mult, op1=ALU.add))
        p.op("vector", lambda e: e.tensor_scalar(
            out=t["sel16"][:, 0:8], in0=t["p3g"][:], scalar1=nb[:, 3:4], scalar2=None,
            op0=ALU.mult))
        p.op("vector", lambda e: e.tensor_scalar(
            out=t["sel16"][:, 8:16], in0=t["p3g"][:], scalar1=b[:, 3:4], scalar2=None,
            op0=ALU.mult))
        vv = p.op("vector", lambda e: e.tensor_copy(out=t["selbf"][:], in_=t["sel16"][:]))
        p.wait("tensor", "V", vv)
        p.wait("tensor", "V", p.val("V"))
        tv = p.op("tensor", lambda e: e.transpose(
            t["ptrb"][0:16, 0:128], t["selbf"][:], t["ident_bf"][:]), inc="T")
        p.wait("vector", "T", tv)
        dstc = (pidx * T + ti) * 128
        p.op("vector", lambda e, dstc=dstc: e.tensor_copy(
            out=t["seltr"][:, dstc:dstc + 128], in_=t["ptrb"][0:16, 0:128]), inc="V")

    def fff_apply(pidx, ti, out_kind):
        selc = (pidx * T + ti) * 128
        p.wait("tensor", "V", p.val("V"))
        tv = None
        for half in range(2):
            for cin in range(C):
                tv = p.op("tensor", lambda e, ti=ti, half=half, cin=cin: e.matmul(
                    t["pA"][:, 512 * half:512 * half + 512],
                    t["xfm"][:, SL * cin + 128 * ti: SL * cin + 128 * ti + 128],
                    t["W1"][:, D * cin + 512 * half: D * cin + 512 * half + 512],
                    start=(cin == 0), stop=(cin == C - 1)), inc="T")
        mv = None
        for half in range(2):
            mv = p.op("tensor", lambda e, half=half, selc=selc: e.matmul(
                t["pB"][:, 512 * half:512 * half + 512],
                t["seltr"][:, selc:selc + 128],
                t["Ebf"][:, 512 * half:512 * half + 512],
                start=True, stop=True), inc="T")
        p.wait("vector", "T", mv)
        p.op("vector", lambda e: e.tensor_tensor(out=t["hf32"][:], in0=t["pA"][:],
                                                 in1=t["b1bc"][:], op=ALU.add))
        p.op("vector", lambda e: e.tensor_scalar(out=t["Hbf"][:], in0=t["hf32"][:],
                                                 scalar1=0.0, scalar2=None, op0=ALU.max))
        p.op("vector", lambda e: e.tensor_copy(out=t["maskbf"][:], in_=t["pB"][:]))
        hv = p.op("vector", lambda e: e.tensor_tensor(out=t["Hm"][:], in0=t["Hbf"][:],
                                                      in1=t["maskbf"][:], op=ALU.mult),
                  inc="V")
        p.wait("tensor", "V", hv)
        tv = None
        for c in range(C):
            tv = p.op("tensor", lambda e, c=c: e.transpose(
                t["ptrb"][:, 128 * c:128 * c + 128],
                t["Hm"][:, 128 * c:128 * c + 128], t["ident_bf"][:]), inc="T")
        p.wait("vector", "T", tv)
        fv = p.op("vector", lambda e: e.tensor_copy(out=t["Hfm"][:], in_=t["ptrb"][:]),
                  inc="V")
        p.wait("tensor", "V", fv)
        tv = None
        for half in range(2):
            for c in range(C):
                p.op("tensor", lambda e, half=half, c=c: e.matmul(
                    t["pA"][:, 512 * half:512 * half + 512],
                    t["Hfm"][:, 128 * c:128 * c + 128],
                    t["W2"][:, D * c + 512 * half: D * c + 512 * half + 512],
                    start=(c == 0), stop=False), inc="T")
            tv = p.op("tensor", lambda e, half=half, selc=selc, pidx=pidx: e.matmul(
                t["pA"][:, 512 * half:512 * half + 512],
                t["seltr"][:, selc:selc + 128],
                t["b2bf"][:, D * pidx + 512 * half: D * pidx + 512 * half + 512],
                start=False, stop=True), inc="T")
        p.wait("vector", "T", tv)
        if out_kind == "vtm":
            p.op("vector", lambda e, ti=ti: e.tensor_copy(
                out=t["vtm"][:, D * ti:D * ti + D], in_=t["pA"][:]), inc="V")
        elif out_kind == "q":
            p.op("vector", lambda e: e.tensor_copy(out=t["ffout"][:], in_=t["pA"][:]),
                 inc="V")
        else:
            ev = p.op("vector", lambda e: e.tensor_copy(out=t["ffout"][:], in_=t["pA"][:]),
                      inc="V")
            p.wait("tensor", "V", ev)
            tv = None
            for c in range(C):
                tv = p.op("tensor", lambda e, c=c: e.transpose(
                    t["ptrb"][:, 128 * c:128 * c + 128],
                    t["ffout"][:, 128 * c:128 * c + 128], t["ident_bf"][:]), inc="T")
            p.wait("vector", "T", tv)
            dst = t["kfm"].rearrange("p (c w) -> p c w", w=SL)[:, :, 128 * ti:128 * ti + 128]
            src = t["ptrb"].rearrange("p (c q) -> p c q", q=128)
            p.op("vector", lambda e, dst=dst, src=src: e.tensor_copy(out=dst, in_=src),
                 inc="V")

    def load_fff_weights(pidx, pfx):
        tval = p.val("T")
        rw1 = emit_load_weight(p, t, dr[pfx + "_w1"], t["W1"], C * D,
                               ready=("T", tval))
        rw2 = emit_load_weight(p, t, dr[pfx + "_w2"], t["W2"], C * D,
                               ready=("T", tval))
        p.wait("sync", "T", p.val("T"))
        lr = p.op("sync", lambda e, pfx=pfx: e.dma_start(out=t["row"][:], in_=dr[pfx + "_b1"][:]),
                  inc="Lr", n=16)
        p.wait("tensor", "Lr", lr)
        p.wait("tensor", "S", p.val("S"))
        rb = emit_bcast_row(p, t, t["row"], t["b1bc"], D)
        return rw1, rw2, rb

    p.wait("tensor", "Lw", p.val("Lw"))
    p.wait("tensor", "S", p.val("S"))

    n_store = 0
    for seq in range(NSEQ):
        for ti in range(T):
            p.wait("sync", "S", p.val("S"))
            p.wait("sync", "V", p.val("V"))
            emit_xfm_tile(p, t, x_dram, seq, ti, t["xfm"], also_f32=t["xtr32"][:])
            for pidx in range(3):
                fff_scores_routing(pidx, ti)
        if dump == "sel":
            _dump_b(p, t, dbg_d, "seltr")
            return nc

        rw1, rw2, rb = load_fff_weights(1, "k")
        p.wait("tensor", rw1[0], rw1[1])
        p.wait("tensor", rw2[0], rw2[1])
        p.wait("vector", "V", rb)
        for ti in range(T):
            fff_apply(1, ti, "kfm")
        if dump == "kfm":
            _dump_b(p, t, dbg_d, "kfm")
            return nc

        rw1, rw2, rb = load_fff_weights(2, "v")
        p.wait("tensor", rw1[0], rw1[1])
        p.wait("tensor", rw2[0], rw2[1])
        p.wait("vector", "V", rb)
        for ti in range(T):
            fff_apply(2, ti, "vtm")
        if dump == "vtm":
            _dump_b(p, t, dbg_d, "vtm")
            return nc

        rw1, rw2, rb = load_fff_weights(0, "q")
        p.wait("tensor", rw1[0], rw1[1])
        p.wait("tensor", rw2[0], rw2[1])
        p.wait("vector", "V", rb)
        for ti in range(T):
            fff_apply(0, ti, "q")
            p.wait("tensor", "V", p.val("V"))
            tv = None
            for c in range(C):
                tv = p.op("tensor", lambda e, c=c: e.transpose(
                    t["ptrb"][:, 128 * c:128 * c + 128],
                    t["ffout"][:, 128 * c: 128 * c + 128],
                    t["ident_bf"][:]), inc="T")
            p.wait("vector", "T", tv)
            qv = p.op("vector", lambda e: e.tensor_copy(out=t["qfm_t"][:], in_=t["ptrb"][:]),
                      inc="V")
            p.wait("tensor", "V", qv)
            p.wait("tensor", "S", p.val("S"))
            tv = None
            for tc in range(4):
                dstp = t["pA"] if tc < 2 else t["pB"]
                off = 512 * (tc % 2)
                for c in range(C):
                    tv = p.op("tensor", lambda e, tc=tc, c=c, dstp=dstp, off=off: e.matmul(
                        dstp[:, off:off + 512],
                        t["qfm_t"][:, 128 * c:128 * c + 128],
                        t["kfm"][:, SL * c + 512 * tc: SL * c + 512 * tc + 512],
                        start=(c == 0), stop=(c == C - 1)), inc="T")
            p.wait("vector", "T", tv)
            rv = None
            for tc in range(4):
                srcp = t["pA"] if tc < 2 else t["pB"]
                off = 512 * (tc % 2)
                rv = p.op("vector", lambda e, tc=tc, srcp=srcp, off=off: e.reduce_max(
                    t["nmax4"][:, tc:tc + 1], srcp[:, off:off + 512],
                    axis=AX.X, negate=True), inc="V")
            p.op("vector", lambda e: e.tensor_reduce(
                t["nmax1"][:], t["nmax4"][:], axis=AX.X, op=ALU.min))
            rv2 = p.op("vector", lambda e: e.tensor_scalar(
                out=t["nmax1"][:], in0=t["nmax1"][:], scalar1=float(D) ** -0.5,
                scalar2=None, op0=ALU.mult), inc="V")
            p.wait("scalar", "V", rv2)
            p.op("scalar", lambda e: e.activation(
                out=t["attn_tm"][:, 0:1024], in_=t["pA"][:], func=AF.Exp,
                bias=t["nmax1"][:], scale=float(D) ** -0.5, accum_out=t["zc2"][:, 0:1]))
            sv = p.op("scalar", lambda e: e.activation(
                out=t["attn_tm"][:, 1024:2048], in_=t["pB"][:], func=AF.Exp,
                bias=t["nmax1"][:], scale=float(D) ** -0.5, accum_out=t["zc2"][:, 1:2]),
                inc="S")
            p.wait("vector", "S", sv)
            p.op("vector", lambda e: e.tensor_tensor(out=t["zc"][:], in0=t["zc2"][:, 0:1],
                                                     in1=t["zc2"][:, 1:2], op=ALU.add))
            p.op("vector", lambda e: e.reciprocal(t["rza"][:], t["zc"][:]))
            p.wait("tensor", "S", sv)
            av = None
            for r in range(2):
                p.wait("tensor", "V", p.val("V"))
                tv = None
                for j in range(8):
                    tv = p.op("tensor", lambda e, r=r, j=j: e.transpose(
                        t["ptrk"][:, 128 * j:128 * j + 128],
                        t["attn_tm"][:, 1024 * r + 128 * j: 1024 * r + 128 * j + 128],
                        t["ident_bf"][:]), inc="T")
                p.wait("vector", "T", tv)
                av = p.op("vector", lambda e, r=r: e.tensor_copy(
                    out=t["attn_km"][:, 1024 * r:1024 * r + 1024], in_=t["ptrk"][:]),
                    inc="V")
            p.wait("tensor", "V", av)
            tv = None
            for half in range(2):
                for kt in range(T):
                    tv = p.op("tensor", lambda e, half=half, kt=kt: e.matmul(
                        t["pA"][:, 512 * half:512 * half + 512],
                        t["attn_km"][:, 128 * kt:128 * kt + 128],
                        t["vtm"][:, D * kt + 512 * half: D * kt + 512 * half + 512],
                        start=(kt == 0), stop=(kt == T - 1)), inc="T")
            lv = p.op("sync", lambda e, seq=seq, ti=ti: e.dma_start(
                out=t["x_t"][:], in_=x_dram[seq * SL + 128 * ti: seq * SL + 128 * ti + 128, :]),
                inc="L", n=16)
            p.wait("vector", "T", tv)
            p.wait("vector", "L", lv)
            p.op("vector", lambda e: e.tensor_scalar(
                out=t["r1"][:], in0=t["pA"][:], scalar1=t["rza"][:], scalar2=None,
                op0=ALU.mult))
            p.op("vector", lambda e: e.tensor_tensor(out=t["r1"][:], in0=t["r1"][:],
                                                     in1=t["x_t"][:], op=ALU.add), inc="V")
            emit_ln_stats(p, t, t["r1"][:], t["mu2"][:], t["rstd2"][:], D)
            p.wait("vector", "O", p.val("O"))
            vv = p.op("vector", lambda e: e.tensor_scalar(
                out=t["xnew"][:], in0=t["r1"][:], scalar1=t["mu2"][:],
                scalar2=t["rstd2"][:], op0=ALU.subtract, op1=ALU.mult), inc="V")
            p.wait("gpsimd", "V", vv)
            p.op("gpsimd", lambda e, seq=seq, ti=ti: e.dma_start(
                out=y_dram[seq * SL + 128 * ti: seq * SL + 128 * ti + 128, :],
                in_=t["xnew"][:]), inc="O", n=16)
            n_store += 1
            p.wait("sync", "V", p.val("V"))

    p.wait("gpsimd", "O", 16 * n_store)
    p.finalize()
    return nc


def _dump_b(p, t, dbg_d, key):
    p.barrier(engines=("tensor", "vector", "scalar", "gpsimd"))
    p.wait("gpsimd", "V", p.val("V"))
    p.wait("gpsimd", "S", p.val("S"))
    p.op("gpsimd", lambda e: e.dma_start(out=dbg_d[:], in_=t[key]), inc="O", n=16)
    p.wait("gpsimd", "O", p.val("O"))
    p.finalize()


def host_inputs_B(full, x1):
    maps_common = {}
    for pfx in ("q", "k", "v"):
        nw = np.asarray(full[f"L0_ffn_{pfx}_node_w"], np.float32)
        nb = np.asarray(full[f"L0_ffn_{pfx}_node_b"], np.float32)
        w1 = np.asarray(full[f"L0_ffn_{pfx}_w1"], np.float32)
        b1 = np.asarray(full[f"L0_ffn_{pfx}_b1"], np.float32)
        w2 = np.asarray(full[f"L0_ffn_{pfx}_w2"], np.float32)
        b2 = np.asarray(full[f"L0_ffn_{pfx}_b2"], np.float32)
        w1h = np.concatenate([w1[LEAF_OF_H[h]] for h in range(16)], axis=1)
        b1h = np.concatenate([b1[LEAF_OF_H[h]] for h in range(16)])
        w2h = np.concatenate([w2[LEAF_OF_H[h]] for h in range(16)], axis=0)
        b2h = np.stack([b2[LEAF_OF_H[h]] for h in range(16)], axis=0)
        maps_common[pfx + "_nwT"] = _cm(np.ascontiguousarray(nw.T))
        maps_common[pfx + "_nb"] = _row(nb)
        maps_common[pfx + "_w1"] = _cm(w1h)
        maps_common[pfx + "_b1"] = _row(b1h)
        maps_common[pfx + "_w2"] = _cm(w2h)
        maps_common[pfx + "_b2"] = np.ascontiguousarray(b2h)
    maps = []
    for core in range(NCORES):
        m = dict(maps_common)
        m["x"] = np.ascontiguousarray(x1[core * NSEQ * SL:(core + 1) * NSEQ * SL])
        maps.append(m)
    return maps



def build_launch_C(dump=None):
    nc = bass.Bass("TRN2", target_bir_lowering=False)
    x_dram = nc.dram_tensor("x", [NSEQ * SL, D], F32, kind="ExternalInput").ap()
    y_dram = nc.dram_tensor("y", [NSEQ * SL, D], F32, kind="ExternalOutput").ap()
    wq_d = nc.dram_tensor("wqT", [128, C * D], F32, kind="ExternalInput").ap()
    wk_d = nc.dram_tensor("wkT", [128, C * D], F32, kind="ExternalInput").ap()
    wv_d = nc.dram_tensor("wvT", [128, C * D], F32, kind="ExternalInput").ap()
    wo_d = nc.dram_tensor("woT", [128, C * D], F32, kind="ExternalInput").ap()
    SCALE = 0.125

    t = {}
    A = nc.alloc_sbuf_tensor
    t["stage"] = A("stage", [128, 2048], F32)[:]
    t["W"] = A("W", [128, C * D], BF16)[:]
    t["xfm"] = A("xfm", [128, C * SL], BF16)[:]
    t["qfm"] = A("qfm", [128, C * SL], BF16)[:]
    t["kfm"] = A("kfm", [128, C * SL], BF16)[:]
    t["vtm"] = A("vtm", [128, T * D], BF16)[:]
    t["attn_tm"] = A("attn_tm", [128, SL], BF16)[:]
    t["attn_km"] = A("attn_km", [128, SL], BF16)[:]
    t["out_sb"] = A("out_sb", [128, D], BF16)[:]
    t["ofm"] = A("ofm", [128, D], BF16)[:]
    t["x_t"] = A("x_t", [128, D], F32)[:]
    t["xbf"] = A("xbf", [128, D], BF16)[:]
    t["r1"] = A("r1", [128, D], F32)[:]
    t["sqd"] = A("sqd", [128, D], F32)[:]
    t["xnew"] = A("xnew", [128, D], F32)[:]
    t["mask_bfm"] = A("mask_bfm", [128, 128], BF16)[:]
    t["ident_bf"] = A("ident_bf", [128, 128], BF16)[:]
    t["ident_f32"] = A("ident_f32", [128, 128], F32)[:]
    t["ones_f32"] = A("ones_f32", [1, 128], F32)[:]
    t["eps_col"] = A("eps_col", [128, 1], F32)[:]
    t["nmax4"] = A("nmax4", [128, 4], F32)[:]
    t["nmax1"] = A("nmax1", [128, 1], F32)[:]
    t["zc4"] = A("zc4", [128, 4], F32)[:]
    t["zc"] = A("zc", [128, 1], F32)[:]
    t["rza"] = A("rza", [128, 1], F32)[:]
    t["mu2"] = A("mu2", [128, 1], F32)[:]
    t["rstd2"] = A("rstd2", [128, 1], F32)[:]
    t["red1"] = A("red1", [128, 1], F32)[:]
    t["red2"] = A("red2", [128, 1], F32)[:]
    t["red3"] = A("red3", [128, 1], F32)[:]

    AP_ = nc.alloc_psum_tensor
    t["pA"] = AP_("pA", [128, D], F32)[:]
    t["pB"] = AP_("pB", [128, D], F32)[:]
    t["pO"] = AP_("pO", [128, D], F32)[:]
    t["ptrk"] = AP_("ptrk", [128, D], BF16)[:]
    t["ptrb"] = AP_("ptrb", [128, D], BF16)[:]

    p = P(nc)
    for s in ("L", "Lw", "Lr", "O", "T", "V", "S", "G"):
        p.sem(s)

    emit_consts(p, t)
    mv = p.op("gpsimd", lambda e: e.memset(t["mask_sb"][:], 0.0), inc="G")
    p.wait("gpsimd", "G", mv)
    p.op("gpsimd", lambda e: e.affine_select(out=t["mask_sb"][:], in_=t["mask_sb"][:],
                                             compare_op=ALU.is_ge, fill=-10000.0, base=0,
                                             pattern=[[-1, 128]], channel_multiplier=1),
         inc="G")
    gv = p.val("G")
    for eng in ("tensor", "scalar", "vector"):
        p.wait(eng, "G", gv)

    def produce_fm(w_dram, dst, ready_T):
        rw = emit_load_weight(p, t, w_dram, t["W"], C * D, ready=("T", ready_T))
        p.wait("tensor", rw[0], rw[1])
        for jc in range(C):
            for tc in range(4):
                dstp = t["pA"] if (jc * 4 + tc) % 2 == 0 else t["pB"]
                p.wait("tensor", "V", p.val("V"))
                tv = None
                for cin in range(C):
                    tv = p.op("tensor", lambda e, jc=jc, tc=tc, cin=cin, dstp=dstp: e.matmul(
                        dstp[:, 0:512],
                        t["W"][:, D * cin + 128 * jc: D * cin + 128 * jc + 128],
                        t["xfm"][:, SL * cin + 512 * tc: SL * cin + 512 * tc + 512],
                        start=(cin == 0), stop=(cin == C - 1)), inc="T")
                p.wait("vector", "T", tv)
                p.op("vector", lambda e, jc=jc, tc=tc, dstp=dstp: e.tensor_copy(
                    out=dst[:, SL * jc + 512 * tc: SL * jc + 512 * tc + 512],
                    in_=dstp[:, 0:512]), inc="V")
        return p.val("T")

    def produce_vtm(w_dram, ready_T):
        rw = emit_load_weight(p, t, w_dram, t["W"], C * D, ready=("T", ready_T))
        p.wait("tensor", rw[0], rw[1])
        for ti in range(T):
            p.wait("tensor", "V", p.val("V"))
            tv = None
            for half in range(2):
                for cin in range(C):
                    tv = p.op("tensor", lambda e, ti=ti, half=half, cin=cin: e.matmul(
                        t["pA"][:, 512 * half:512 * half + 512],
                        t["xfm"][:, SL * cin + 128 * ti: SL * cin + 128 * ti + 128],
                        t["W"][:, D * cin + 512 * half: D * cin + 512 * half + 512],
                        start=(cin == 0), stop=(cin == C - 1)), inc="T")
            p.wait("vector", "T", tv)
            p.op("vector", lambda e, ti=ti: e.tensor_copy(
                out=t["vtm"][:, D * ti:D * ti + D], in_=t["pA"][:]), inc="V")
        return p.val("T")

    n_store = 0
    for seq in range(NSEQ):
        for ti in range(T):
            p.wait("sync", "S", p.val("S"))
            p.wait("sync", "V", p.val("V"))
            emit_xfm_tile(p, t, x_dram, seq, ti, t["xfm"])
        tT = produce_fm(wq_d, t["qfm"], p.val("T"))
        tT = produce_fm(wk_d, t["kfm"], tT)
        tT = produce_vtm(wv_d, tT)
        rwo = emit_load_weight(p, t, wo_d, t["W"], C * D, ready=("T", tT))

        for qt in range(T):
            nk = 128 * (qt + 1)
            nkc = (nk + 511) // 512
            for h in range(16):
                jc, hp = h // 2, 64 * (h % 2)
                p.wait("tensor", "S", p.val("S"))
                p.wait("tensor", "V", p.val("V"))
                tv = None
                for tc in range(nkc):
                    w = min(512, nk - 512 * tc)
                    dstp = t["pA"] if tc < 2 else t["pB"]
                    off = 512 * (tc % 2)
                    tv = p.op("tensor", lambda e, qt=qt, tc=tc, w=w, dstp=dstp, off=off,
                              jc=jc, hp=hp: e.matmul(
                        dstp[:, off:off + w],
                        t["qfm"][hp:hp + 64, SL * jc + 128 * qt: SL * jc + 128 * qt + 128],
                        t["kfm"][hp:hp + 64, SL * jc + 512 * tc: SL * jc + 512 * tc + w],
                        start=True, stop=True), inc="T")
                p.wait("vector", "T", tv)
                mtc = (128 * qt) // 512
                moff = 512 * (mtc % 2) + (128 * qt - 512 * mtc)
                mp = t["pA"] if mtc < 2 else t["pB"]
                p.op("vector", lambda e, mp=mp, moff=moff: e.tensor_tensor(
                    out=mp[:, moff:moff + 128], in0=mp[:, moff:moff + 128],
                    in1=t["mask_sb"][:], op=ALU.add))
                for tc in range(nkc):
                    w = min(512, nk - 512 * tc)
                    srcp = t["pA"] if tc < 2 else t["pB"]
                    off = 512 * (tc % 2)
                    p.op("vector", lambda e, tc=tc, w=w, srcp=srcp, off=off: e.reduce_max(
                        t["nmax4"][:, tc:tc + 1], srcp[:, off:off + w],
                        axis=AX.X, negate=True))
                p.op("vector", lambda e, nkc=nkc: e.tensor_reduce(
                    t["nmax1"][:], t["nmax4"][:, 0:nkc], axis=AX.X, op=ALU.min))
                rv2 = p.op("vector", lambda e: e.tensor_scalar(
                    out=t["nmax1"][:], in0=t["nmax1"][:], scalar1=SCALE, scalar2=None,
                    op0=ALU.mult), inc="V")
                p.wait("scalar", "V", rv2)
                sv = None
                for tc in range(nkc):
                    w = min(512, nk - 512 * tc)
                    srcp = t["pA"] if tc < 2 else t["pB"]
                    off = 512 * (tc % 2)
                    sv = p.op("scalar", lambda e, tc=tc, w=w, srcp=srcp, off=off: e.activation(
                        out=t["attn_tm"][:, 512 * tc:512 * tc + w], in_=srcp[:, off:off + w],
                        func=AF.Exp, bias=t["nmax1"][:], scale=SCALE,
                        accum_out=t["zc4"][:, tc:tc + 1]), inc="S")
                p.wait("vector", "S", sv)
                p.op("vector", lambda e, nkc=nkc: e.reduce_sum(
                    t["zc"][:], t["zc4"][:, 0:nkc], axis=AX.X))
                p.op("vector", lambda e: e.reciprocal(t["rza"][:], t["zc"][:]))
                p.wait("tensor", "S", sv)
                nb = qt + 1
                for r in range((nb + 7) // 8):
                    p.wait("tensor", "V", p.val("V"))
                    w8 = min(8, nb - 8 * r)
                    tv = None
                    for j in range(w8):
                        tv = p.op("tensor", lambda e, r=r, j=j: e.transpose(
                            t["ptrk"][:, 128 * j:128 * j + 128],
                            t["attn_tm"][:, 1024 * r + 128 * j: 1024 * r + 128 * j + 128],
                            t["ident_bf"][:]), inc="T")
                    p.wait("vector", "T", tv)
                    p.op("vector", lambda e, r=r, w8=w8: e.tensor_copy(
                        out=t["attn_km"][:, 1024 * r:1024 * r + 128 * w8],
                        in_=t["ptrk"][:, 0:128 * w8]), inc="V")
                p.wait("tensor", "V", p.val("V"))
                tv = None
                for kt in range(qt + 1):
                    tv = p.op("tensor", lambda e, kt=kt, qt=qt, h=h: e.matmul(
                        t["pO"][:, 64 * h:64 * h + 64],
                        t["attn_km"][:, 128 * kt:128 * kt + 128],
                        t["vtm"][:, D * kt + 64 * h: D * kt + 64 * h + 64],
                        start=(kt == 0), stop=(kt == qt)), inc="T")
                p.wait("vector", "T", tv)
                p.op("vector", lambda e, h=h: e.tensor_scalar(
                    out=t["out_sb"][:, 64 * h:64 * h + 64], in0=t["pO"][:, 64 * h:64 * h + 64],
                    scalar1=t["rza"][:], scalar2=None, op0=ALU.mult), inc="V")

            p.wait("tensor", "V", p.val("V"))
            tv = None
            for c in range(C):
                tv = p.op("tensor", lambda e, c=c: e.transpose(
                    t["ptrb"][:, 128 * c:128 * c + 128],
                    t["out_sb"][:, 128 * c:128 * c + 128], t["ident_bf"][:]), inc="T")
            p.wait("vector", "T", tv)
            fv = p.op("vector", lambda e: e.tensor_copy(out=t["ofm"][:], in_=t["ptrb"][:]),
                      inc="V")
            p.wait("tensor", "V", fv)
            if qt == 0:
                p.wait("tensor", rwo[0], rwo[1])
            tv = None
            for half in range(2):
                for c in range(C):
                    tv = p.op("tensor", lambda e, half=half, c=c: e.matmul(
                        t["pA"][:, 512 * half:512 * half + 512],
                        t["ofm"][:, 128 * c:128 * c + 128],
                        t["W"][:, D * c + 512 * half: D * c + 512 * half + 512],
                        start=(c == 0), stop=(c == C - 1)), inc="T")
            lv = p.op("sync", lambda e, seq=seq, qt=qt: e.dma_start(
                out=t["x_t"][:], in_=x_dram[seq * SL + 128 * qt: seq * SL + 128 * qt + 128, :]),
                inc="L", n=16)
            p.wait("vector", "T", tv)
            p.wait("vector", "L", lv)
            p.op("vector", lambda e: e.tensor_tensor(out=t["r1"][:], in0=t["pA"][:],
                                                     in1=t["x_t"][:], op=ALU.add), inc="V")
            emit_ln_stats(p, t, t["r1"][:], t["mu2"][:], t["rstd2"][:], D)
            p.wait("vector", "O", p.val("O"))
            vv = p.op("vector", lambda e: e.tensor_scalar(
                out=t["xnew"][:], in0=t["r1"][:], scalar1=t["mu2"][:],
                scalar2=t["rstd2"][:], op0=ALU.subtract, op1=ALU.mult), inc="V")
            p.wait("gpsimd", "V", vv)
            p.op("gpsimd", lambda e, seq=seq, qt=qt: e.dma_start(
                out=y_dram[seq * SL + 128 * qt: seq * SL + 128 * qt + 128, :],
                in_=t["xnew"][:]), inc="O", n=16)
            n_store += 1
            p.wait("sync", "V", p.val("V"))

    p.wait("gpsimd", "O", 16 * n_store)
    p.finalize()
    return nc


def host_inputs_C(full, x2):
    w_qkv = np.asarray(full["L1_attn_w_qkv"], np.float32)
    w_out = np.asarray(full["L1_attn_w_out"], np.float32)
    m_common = {
        "wqT": _cm(np.ascontiguousarray(w_qkv[0:1024].T)),
        "wkT": _cm(np.ascontiguousarray(w_qkv[1024:2048].T)),
        "wvT": _cm(np.ascontiguousarray(w_qkv[2048:3072].T)),
        "woT": _cm(np.ascontiguousarray(w_out.T)),
    }
    maps = []
    for core in range(NCORES):
        m = dict(m_common)
        m["x"] = np.ascontiguousarray(x2[core * NSEQ * SL:(core + 1) * NSEQ * SL])
        maps.append(m)
    return maps



D_FCS = [(0, 1024), (1024, 1024), (2048, 768)]

def build_launch_D(dump=None):
    nc = bass.Bass("TRN2", target_bir_lowering=False)
    x_dram = nc.dram_tensor("x", [NSEQ * SL, D], F32, kind="ExternalInput").ap()
    dbg_d = None
    if dump is not None:
        dbg_d = nc.dram_tensor("dbg", [128, T * D], F32, kind="ExternalOutput").ap()
    y_dram = nc.dram_tensor("y", [NSEQ * SL, D], F32, kind="ExternalOutput").ap()
    gw_d = nc.dram_tensor("gwT", [128, C * 16], F32, kind="ExternalInput").ap()
    sgw_d = nc.dram_tensor("sgwT", [128, C], F32, kind="ExternalInput").ap()
    FCS = D_FCS
    swg_d = [nc.dram_tensor(f"swgT{i}", [128, C * w], F32, kind="ExternalInput").ap()
             for i, (_, w) in enumerate(FCS)]
    swu_d = [nc.dram_tensor(f"swuT{i}", [128, C * w], F32, kind="ExternalInput").ap()
             for i, (_, w) in enumerate(FCS)]
    swd_d = [nc.dram_tensor(f"swdT{i}", [128, (w // 128) * D], F32, kind="ExternalInput").ap()
             for i, (_, w) in enumerate(FCS)]
    wgu_d = [nc.dram_tensor(f"wgu{e}", [128, C * D], F32, kind="ExternalInput").ap()
             for e in range(16)]
    wdn_d = [nc.dram_tensor(f"wdn{e}", [128, 4 * D], F32, kind="ExternalInput").ap()
             for e in range(16)]

    t = {}
    A = nc.alloc_sbuf_tensor
    t["stage"] = A("stage", [128, 2048], F32)[:]
    t["xfm"] = A("xfm", [128, C * SL], BF16)[:]
    t["acc"] = A("acc", [128, T * D], F32)[:]
    t["Wg"] = A("Wg", [128, C * 1024], BF16)[:]
    t["Wu"] = A("Wu", [128, C * 1024], BF16)[:]
    t["Wd"] = A("Wd", [128, 8 * D], BF16)[:]
    t["gwbf"] = A("gwbf", [128, C * 16], BF16)[:]
    t["sgwbf"] = A("sgwbf", [128, C], BF16)[:]
    t["exps"] = A("exps", [128, T * 16], F32)[:]
    t["combine"] = A("combine", [128, T * 16], F32)[:]
    t["sglog"] = A("sglog", [128, T], F32)[:]
    t["sgate"] = A("sgate", [128, T], F32)[:]
    t["gs_bf"] = A("gs_bf", [128, D], BF16)[:]
    t["U_bf"] = A("U_bf", [128, D], BF16)[:]
    t["Ufm"] = A("Ufm", [128, D], BF16)[:]
    t["x_t"] = A("x_t", [128, D], F32)[:]
    t["xbf"] = A("xbf", [128, D], BF16)[:]
    t["r1"] = A("r1", [128, D], F32)[:]
    t["sqd"] = A("sqd", [128, D], F32)[:]
    t["xnew"] = A("xnew", [128, D], F32)[:]
    t["m1"] = A("m1", [128, 1], F32)[:]
    t["m2"] = A("m2", [128, 1], F32)[:]
    t["mask16"] = A("mask16", [128, 16], F32)[:]
    t["pmsk"] = A("pmsk", [128, 16], F32)[:]
    t["s12"] = A("s12", [128, 1], F32)[:]
    t["rs12"] = A("rs12", [128, 1], F32)[:]
    t["ident_bf"] = A("ident_bf", [128, 128], BF16)[:]
    t["ident_f32"] = A("ident_f32", [128, 128], F32)[:]
    t["ones_f32"] = A("ones_f32", [1, 128], F32)[:]
    t["eps_col"] = A("eps_col", [128, 1], F32)[:]
    t["nmax1"] = A("nmax1", [128, 1], F32)[:]
    t["zc"] = A("zc", [128, 1], F32)[:]
    t["mu2"] = A("mu2", [128, 1], F32)[:]
    t["rstd2"] = A("rstd2", [128, 1], F32)[:]
    t["red1"] = A("red1", [128, 1], F32)[:]
    t["red2"] = A("red2", [128, 1], F32)[:]
    t["red3"] = A("red3", [128, 1], F32)[:]

    AP_ = nc.alloc_psum_tensor
    t["pA"] = AP_("pA", [128, D], F32)[:]
    t["pB"] = AP_("pB", [128, D], F32)[:]
    t["pO"] = AP_("pO", [128, D], F32)[:]
    t["ptrb"] = AP_("ptrb", [128, D], BF16)[:]

    p = P(nc)
    for s in ("L", "Lw", "Lr", "O", "T", "V", "S", "G"):
        p.sem(s)

    emit_consts(p, t)
    gv = p.val("G")
    for eng in ("tensor", "scalar", "vector"):
        p.wait(eng, "G", gv)

    lv = p.op("sync", lambda e: e.dma_start(out=t["stage"][:, 0:C * 16], in_=gw_d[:]),
              inc="Lw", n=16)
    p.wait("scalar", "Lw", lv)
    sv = p.op("scalar", lambda e: e.copy(out=t["gwbf"][:], in_=t["stage"][:, 0:C * 16]), inc="S")
    p.wait("sync", "S", sv)
    lv = p.op("sync", lambda e: e.dma_start(out=t["stage"][:, 0:C], in_=sgw_d[:]), inc="Lw", n=16)
    p.wait("scalar", "Lw", lv)
    t["_stage_last"] = p.op("scalar", lambda e: e.copy(out=t["sgwbf"][:], in_=t["stage"][:, 0:C]),
                            inc="S")
    p.wait("tensor", "S", p.val("S"))

    n_store = 0
    for seq in range(NSEQ):
        for ti in range(T):
            p.wait("sync", "S", p.val("S"))
            p.wait("sync", "V", p.val("V"))
            emit_xfm_tile(p, t, x_dram, seq, ti, t["xfm"])
            p.wait("tensor", "S", p.val("S"))
            p.wait("tensor", "V", p.val("V"))
            tv = None
            for cin in range(C):
                tv = p.op("tensor", lambda e, ti=ti, cin=cin: e.matmul(
                    t["pA"][:, 0:16],
                    t["xfm"][:, SL * cin + 128 * ti: SL * cin + 128 * ti + 128],
                    t["gwbf"][:, 16 * cin:16 * cin + 16],
                    start=(cin == 0), stop=(cin == C - 1)), inc="T")
            for cin in range(C):
                tv = p.op("tensor", lambda e, ti=ti, cin=cin: e.matmul(
                    t["pA"][:, 128:129],
                    t["xfm"][:, SL * cin + 128 * ti: SL * cin + 128 * ti + 128],
                    t["sgwbf"][:, cin:cin + 1],
                    start=(cin == 0), stop=(cin == C - 1)), inc="T")
            p.wait("vector", "T", tv)
            rv = p.op("vector", lambda e: e.reduce_max(
                t["nmax1"][:], t["pA"][:, 0:16], axis=AX.X, negate=True), inc="V")
            p.op("vector", lambda e, ti=ti: e.tensor_copy(
                out=t["sglog"][:, ti:ti + 1], in_=t["pA"][:, 128:129]))
            p.wait("scalar", "V", p.val("V"))
            p.op("scalar", lambda e, ti=ti: e.activation(
                out=t["exps"][:, 16 * ti:16 * ti + 16], in_=t["pA"][:, 0:16], func=AF.Exp,
                bias=t["nmax1"][:], scale=1.0), inc="S")
        p.wait("scalar", "V", p.val("V"))
        p.op("scalar", lambda e: e.activation(out=t["sgate"][:], in_=t["sglog"][:],
                                              func=AF.Sigmoid), inc="S")
        p.wait("vector", "S", p.val("S"))
        for ti in range(T):
            ex = t["exps"][:, 16 * ti:16 * ti + 16]
            cb = t["combine"][:, 16 * ti:16 * ti + 16]
            p.op("vector", lambda e, ex=ex: e.reduce_max(t["m1"][:], ex, axis=AX.X))
            p.op("vector", lambda e, ex=ex: e.tensor_scalar(
                out=t["mask16"][:], in0=ex, scalar1=t["m1"][:], scalar2=None, op0=ALU.is_ge))
            p.op("vector", lambda e, ex=ex: e.scalar_tensor_tensor(
                out=t["pmsk"][:], in0=t["mask16"][:], scalar=-1e30, in1=ex,
                op0=ALU.mult, op1=ALU.add))
            p.op("vector", lambda e: e.reduce_max(t["m2"][:], t["pmsk"][:], axis=AX.X))
            p.op("vector", lambda e, ex=ex: e.tensor_scalar(
                out=t["mask16"][:], in0=ex, scalar1=t["m2"][:], scalar2=None, op0=ALU.is_ge))
            p.op("vector", lambda e: e.tensor_tensor(
                out=t["s12"][:], in0=t["m1"][:], in1=t["m2"][:], op=ALU.add))
            p.op("vector", lambda e: e.reciprocal(t["rs12"][:], t["s12"][:]))
            p.op("vector", lambda e, ex=ex: e.tensor_tensor(
                out=t["pmsk"][:], in0=ex, in1=t["mask16"][:], op=ALU.mult))
            p.op("vector", lambda e, cb=cb: e.tensor_scalar(
                out=cb, in0=t["pmsk"][:], scalar1=t["rs12"][:], scalar2=None, op0=ALU.mult))

        for fci, (f0, wfc) in enumerate(FCS):
            tval = p.val("T")
            rg = emit_load_weight(p, t, swg_d[fci], t["Wg"], C * wfc, ready=("T", tval))
            ru = emit_load_weight(p, t, swu_d[fci], t["Wu"], C * wfc, ready=("T", tval))
            rd = emit_load_weight(p, t, swd_d[fci], t["Wd"], (wfc // 128) * D,
                                  ready=("T", tval))
            p.wait("tensor", rg[0], rg[1])
            p.wait("tensor", ru[0], ru[1])
            p.wait("tensor", rd[0], rd[1])
            for ti in range(T):
                p.wait("tensor", "S", p.val("S"))
                p.wait("tensor", "V", p.val("V"))
                tv = None
                for half in range(wfc // 512 + (1 if wfc % 512 else 0)):
                    w = min(512, wfc - 512 * half)
                    for cin in range(C):
                        tv = p.op("tensor", lambda e, ti=ti, half=half, w=w, cin=cin, wfc=wfc: e.matmul(
                            t["pA"][:, 512 * half:512 * half + w],
                            t["xfm"][:, SL * cin + 128 * ti: SL * cin + 128 * ti + 128],
                            t["Wg"][:, wfc * cin + 512 * half: wfc * cin + 512 * half + w],
                            start=(cin == 0), stop=(cin == C - 1)), inc="T")
                for half in range(wfc // 512 + (1 if wfc % 512 else 0)):
                    w = min(512, wfc - 512 * half)
                    for cin in range(C):
                        tv = p.op("tensor", lambda e, ti=ti, half=half, w=w, cin=cin, wfc=wfc: e.matmul(
                            t["pB"][:, 512 * half:512 * half + w],
                            t["xfm"][:, SL * cin + 128 * ti: SL * cin + 128 * ti + 128],
                            t["Wu"][:, wfc * cin + 512 * half: wfc * cin + 512 * half + w],
                            start=(cin == 0), stop=(cin == C - 1)), inc="T")
                p.wait("scalar", "T", tv)
                sv = p.op("scalar", lambda e, wfc=wfc: e.activation(
                    out=t["gs_bf"][:, 0:wfc], in_=t["pA"][:, 0:wfc], func=AF.Silu), inc="S")
                p.wait("vector", "S", sv)
                uv = p.op("vector", lambda e, wfc=wfc: e.tensor_tensor(
                    out=t["U_bf"][:, 0:wfc], in0=t["gs_bf"][:, 0:wfc],
                    in1=t["pB"][:, 0:wfc], op=ALU.mult), inc="V")
                p.wait("tensor", "V", uv)
                tv = None
                for c2 in range(wfc // 128):
                    tv = p.op("tensor", lambda e, c2=c2: e.transpose(
                        t["ptrb"][:, 128 * c2:128 * c2 + 128],
                        t["U_bf"][:, 128 * c2:128 * c2 + 128], t["ident_bf"][:]), inc="T")
                p.wait("vector", "T", tv)
                fv = p.op("vector", lambda e, wfc=wfc: e.tensor_copy(
                    out=t["Ufm"][:, 0:wfc], in_=t["ptrb"][:, 0:wfc]), inc="V")
                p.wait("tensor", "V", fv)
                tv = None
                for half in range(2):
                    for c2 in range(wfc // 128):
                        tv = p.op("tensor", lambda e, half=half, c2=c2, wfc=wfc: e.matmul(
                            t["pO"][:, 512 * half:512 * half + 512],
                            t["Ufm"][:, 128 * c2:128 * c2 + 128],
                            t["Wd"][:, D * c2 + 512 * half: D * c2 + 512 * half + 512],
                            start=(c2 == 0), stop=(c2 == wfc // 128 - 1)), inc="T")
                p.wait("vector", "T", tv)
                sgc = t["sgate"][:, ti:ti + 1]
                at = t["acc"][:, D * ti:D * ti + D]
                if fci == 0:
                    p.op("vector", lambda e, at=at, sgc=sgc: e.tensor_scalar(
                        out=at, in0=t["pO"][:], scalar1=sgc, scalar2=None, op0=ALU.mult),
                        inc="V")
                else:
                    p.op("vector", lambda e, at=at, sgc=sgc: e.scalar_tensor_tensor(
                        out=at, in0=t["pO"][:], scalar=sgc, in1=at,
                        op0=ALU.mult, op1=ALU.add), inc="V")
                if dump == f"fc{fci}t{ti}":
                    p.barrier(engines=("tensor", "vector", "scalar", "gpsimd"))
                    p.wait("gpsimd", "V", p.val("V"))
                    p.op("gpsimd", lambda e: e.dma_start(out=dbg_d[:], in_=t["acc"]), inc="O", n=16)
                    p.wait("gpsimd", "O", p.val("O"))
                    p.finalize()
                    return nc
            _tistop = None
            if dump is not None and dump.startswith("acct"):
                _tistop = int(dump[4:])
            if dump == f"accfc{fci}" or (fci == len(FCS) - 1 and _tistop is not None and False):
                p.barrier(engines=("tensor", "vector", "scalar", "gpsimd"))
                p.wait("gpsimd", "V", p.val("V"))
                p.op("gpsimd", lambda e: e.dma_start(out=dbg_d[:], in_=t["acc"]), inc="O", n=16)
                p.wait("gpsimd", "O", p.val("O"))
                p.finalize()
                return nc

        if dump == "acc1":
            p.barrier(engines=("tensor", "vector", "scalar", "gpsimd"))
            p.wait("gpsimd", "V", p.val("V"))
            p.op("gpsimd", lambda e: e.dma_start(out=dbg_d[:], in_=t["acc"]), inc="O", n=16)
            p.wait("gpsimd", "O", p.val("O"))
            p.finalize()
            return nc

        for ex in range(16):
            tval = p.val("T")
            rg = emit_load_weight(p, t, wgu_d[ex], t["Wg"], C * D, ready=("T", tval))
            rd = emit_load_weight(p, t, wdn_d[ex], t["Wd"], 4 * D, ready=("T", tval))
            p.wait("tensor", rg[0], rg[1])
            p.wait("tensor", rd[0], rd[1])
            for ti in range(T):
                p.wait("tensor", "S", p.val("S"))
                p.wait("tensor", "V", p.val("V"))
                tv = None
                for half in range(2):
                    for cin in range(C):
                        tv = p.op("tensor", lambda e, ti=ti, half=half, cin=cin: e.matmul(
                            t["pA"][:, 512 * half:512 * half + 512],
                            t["xfm"][:, SL * cin + 128 * ti: SL * cin + 128 * ti + 128],
                            t["Wg"][:, D * cin + 512 * half: D * cin + 512 * half + 512],
                            start=(cin == 0), stop=(cin == C - 1)), inc="T")
                p.wait("scalar", "T", tv)
                sv = p.op("scalar", lambda e: e.activation(
                    out=t["gs_bf"][:, 0:512], in_=t["pA"][:, 0:512], func=AF.Silu), inc="S")
                p.wait("vector", "S", sv)
                p.op("vector", lambda e: e.tensor_tensor(
                    out=t["U_bf"][:, 0:512], in0=t["gs_bf"][:, 0:512],
                    in1=t["pA"][:, 512:1024], op=ALU.mult))
                cbc = t["combine"][:, 16 * ti + ex:16 * ti + ex + 1]
                uv = p.op("vector", lambda e, cbc=cbc: e.tensor_scalar(
                    out=t["U_bf"][:, 0:512], in0=t["U_bf"][:, 0:512], scalar1=cbc,
                    scalar2=None, op0=ALU.mult), inc="V")
                p.wait("tensor", "V", uv)
                tv = None
                for c2 in range(4):
                    tv = p.op("tensor", lambda e, c2=c2: e.transpose(
                        t["ptrb"][:, 128 * c2:128 * c2 + 128],
                        t["U_bf"][:, 128 * c2:128 * c2 + 128], t["ident_bf"][:]), inc="T")
                p.wait("vector", "T", tv)
                fv = p.op("vector", lambda e: e.tensor_copy(
                    out=t["Ufm"][:, 0:512], in_=t["ptrb"][:, 0:512]), inc="V")
                p.wait("tensor", "V", fv)
                tv = None
                for half in range(2):
                    for c2 in range(4):
                        tv = p.op("tensor", lambda e, half=half, c2=c2: e.matmul(
                            t["pO"][:, 512 * half:512 * half + 512],
                            t["Ufm"][:, 128 * c2:128 * c2 + 128],
                            t["Wd"][:, D * c2 + 512 * half: D * c2 + 512 * half + 512],
                            start=(c2 == 0), stop=(c2 == 3)), inc="T")
                p.wait("vector", "T", tv)
                at = t["acc"][:, D * ti:D * ti + D]
                p.op("vector", lambda e, at=at: e.tensor_tensor(
                    out=at, in0=at, in1=t["pO"][:], op=ALU.add), inc="V")

        if dump == "acc2":
            p.barrier(engines=("tensor", "vector", "scalar", "gpsimd"))
            p.wait("gpsimd", "V", p.val("V"))
            p.op("gpsimd", lambda e: e.dma_start(out=dbg_d[:], in_=t["acc"]), inc="O", n=16)
            p.wait("gpsimd", "O", p.val("O"))
            p.finalize()
            return nc

        for ti in range(T):
            lv = p.op("sync", lambda e, seq=seq, ti=ti: e.dma_start(
                out=t["x_t"][:], in_=x_dram[seq * SL + 128 * ti: seq * SL + 128 * ti + 128, :]),
                inc="L", n=16)
            p.wait("vector", "L", lv)
            p.op("vector", lambda e, ti=ti: e.tensor_tensor(
                out=t["r1"][:], in0=t["acc"][:, D * ti:D * ti + D], in1=t["x_t"][:],
                op=ALU.add), inc="V")
            emit_ln_stats(p, t, t["r1"][:], t["mu2"][:], t["rstd2"][:], D)
            p.wait("vector", "O", p.val("O"))
            vv = p.op("vector", lambda e: e.tensor_scalar(
                out=t["xnew"][:], in0=t["r1"][:], scalar1=t["mu2"][:],
                scalar2=t["rstd2"][:], op0=ALU.subtract, op1=ALU.mult), inc="V")
            p.wait("gpsimd", "V", vv)
            p.op("gpsimd", lambda e, seq=seq, ti=ti: e.dma_start(
                out=y_dram[seq * SL + 128 * ti: seq * SL + 128 * ti + 128, :],
                in_=t["xnew"][:]), inc="O", n=16)
            n_store += 1
            p.wait("sync", "V", p.val("V"))

    p.wait("gpsimd", "O", 16 * n_store)
    p.finalize()
    return nc


def host_inputs_D(full, x3):
    gw = np.asarray(full["L1_ffn_gate_w"], np.float32)
    sgw = np.asarray(full["L1_ffn_shared_gate_w"], np.float32)
    swg = np.asarray(full["L1_ffn_shared_w_gate"], np.float32)
    swu = np.asarray(full["L1_ffn_shared_w_up"], np.float32)
    swd = np.asarray(full["L1_ffn_shared_w_down"], np.float32)
    wgu = np.asarray(full["L1_ffn_w_gate_up"], np.float32)
    wdn = np.asarray(full["L1_ffn_w_down"], np.float32)
    FCS = [(0, 1024), (1024, 1024), (2048, 768)]
    m_common = {"gwT": _cm(np.ascontiguousarray(gw.T)),
                "sgwT": _cm(np.ascontiguousarray(sgw.T))}
    swgT = np.ascontiguousarray(swg.T)
    swuT = np.ascontiguousarray(swu.T)
    swdT = np.ascontiguousarray(swd.T)
    for i, (f0, w) in enumerate(FCS):
        m_common[f"swgT{i}"] = _cm(np.ascontiguousarray(swgT[:, f0:f0 + w]))
        m_common[f"swuT{i}"] = _cm(np.ascontiguousarray(swuT[:, f0:f0 + w]))
        m_common[f"swdT{i}"] = _cm(np.ascontiguousarray(swdT[f0:f0 + w, :]))
    for e in range(16):
        m_common[f"wgu{e}"] = _cm(wgu[e])
        m_common[f"wdn{e}"] = _cm(wdn[e])
    maps = []
    for core in range(NCORES):
        m = dict(m_common)
        m["x"] = np.ascontiguousarray(x3[core * NSEQ * SL:(core + 1) * NSEQ * SL])
        maps.append(m)
    return maps



import contextlib


def build_fused():
    nc = bass.Bass("TRN2", target_bir_lowering=False)
    x_dram = nc.dram_tensor("x", [NSEQ * SL, D], F32, kind="ExternalInput").ap()
    y_dram = nc.dram_tensor("y", [NSEQ * SL, D], F32, kind="ExternalOutput").ap()
    wq_d = nc.dram_tensor("wq", [128, C * D], F32, kind="ExternalInput").ap()
    wk_d = nc.dram_tensor("wk", [128, C * D], F32, kind="ExternalInput").ap()
    wv_d = nc.dram_tensor("wv", [128, C * D], F32, kind="ExternalInput").ap()
    wo_d = nc.dram_tensor("wo", [128, C * D], F32, kind="ExternalInput").ap()
    bo_d = nc.dram_tensor("bo", [1, D], F32, kind="ExternalInput").ap()
    gam_d = nc.dram_tensor("gam", [1, D], F32, kind="ExternalInput").ap()
    bet_d = nc.dram_tensor("bet", [1, D], F32, kind="ExternalInput").ap()
    dr = {}
    for pfx in ("q", "k", "v"):
        dr[pfx + "_nwT"] = nc.dram_tensor(pfx + "_nwT", [128, C * 15], F32, kind="ExternalInput").ap()
        dr[pfx + "_nb"] = nc.dram_tensor(pfx + "_nb", [1, 15], F32, kind="ExternalInput").ap()
        dr[pfx + "_w1"] = nc.dram_tensor(pfx + "_w1", [128, C * D], F32, kind="ExternalInput").ap()
        dr[pfx + "_b1"] = nc.dram_tensor(pfx + "_b1", [1, D], F32, kind="ExternalInput").ap()
        dr[pfx + "_w2"] = nc.dram_tensor(pfx + "_w2", [128, C * D], F32, kind="ExternalInput").ap()
        dr[pfx + "_b2"] = nc.dram_tensor(pfx + "_b2", [16, D], F32, kind="ExternalInput").ap()
    wqT_d = nc.dram_tensor("wqT", [128, C * D], F32, kind="ExternalInput").ap()
    wkT_d = nc.dram_tensor("wkT", [128, C * D], F32, kind="ExternalInput").ap()
    wvT_d = nc.dram_tensor("wvT", [128, C * D], F32, kind="ExternalInput").ap()
    woT_d = nc.dram_tensor("woT", [128, C * D], F32, kind="ExternalInput").ap()
    gw_d = nc.dram_tensor("gwT", [128, C * 16], F32, kind="ExternalInput").ap()
    sgw_d = nc.dram_tensor("sgwT", [128, C], F32, kind="ExternalInput").ap()
    FCS = [(0, 1024), (1024, 1024), (2048, 768)]
    swg_d = [nc.dram_tensor(f"swgT{i}", [128, C * w], F32, kind="ExternalInput").ap()
             for i, (_, w) in enumerate(FCS)]
    swu_d = [nc.dram_tensor(f"swuT{i}", [128, C * w], F32, kind="ExternalInput").ap()
             for i, (_, w) in enumerate(FCS)]
    swd_d = [nc.dram_tensor(f"swdT{i}", [128, (w // 128) * D], F32, kind="ExternalInput").ap()
             for i, (_, w) in enumerate(FCS)]
    wgu_d = [nc.dram_tensor(f"wgu{e}", [128, C * D], F32, kind="ExternalInput").ap()
             for e in range(16)]
    wdn_d = [nc.dram_tensor(f"wdn{e}", [128, 4 * D], F32, kind="ExternalInput").ap()
             for e in range(16)]

    t = {}
    A = nc.alloc_sbuf_tensor
    t["x_bf"] = A("x_bf", [128, T * D], BF16)[:]
    t["xfm"] = A("xfm", [128, C * SL], BF16)[:]
    t["stage"] = A("stage", [128, 1024], F32)[:]
    t["W0"] = A("W0", [128, C * D], BF16)[:]
    t["W1"] = A("W1", [128, C * D], BF16)[:]
    t["row"] = A("rowbuf", [1, D], F32)[:]
    t["ident_bf"] = A("ident_bf", [128, 128], BF16)[:]
    t["ones_f32"] = A("ones_f32", [1, 128], F32)[:]
    t["eps_col"] = A("eps_col", [128, 1], F32)[:]
    t["shift48"] = A("shift48", [128, 1], F32)[:]
    t["shift24"] = A("shift24", [128, 1], F32)[:]
    t["mask_bfm"] = A("mask_bfm", [128, 128], BF16)[:]
    t["Ebf"] = A("Ebf", [16, D], BF16)[:]
    t["nmax4"] = A("nmax4", [128, 4], F32)[:]
    t["nmax1"] = A("nmax1", [128, 1], F32)[:]
    t["zc2"] = A("zc2", [128, 4], F32)[:]
    t["zc"] = A("zc", [128, 1], F32)[:]
    t["rzk"] = A("rzk", [128, C], F32)[:]
    t["rzq"] = A("rzq", [128, 1], F32)[:]
    t["rza"] = A("rza", [128, 1], F32)[:]
    t["mu1"] = A("mu1", [128, 1], F32)[:]
    t["rstd1"] = A("rstd1", [128, 1], F32)[:]
    t["mu2"] = A("mu2", [128, 1], F32)[:]
    t["rstd2"] = A("rstd2", [128, 1], F32)[:]
    t["red1"] = A("red1", [128, 1], F32)[:]
    t["red2"] = A("red2", [128, 1], F32)[:]
    t["red3"] = A("red3", [128, 1], F32)[:]
    t["zc4"] = t["zc2"]

    AP_ = nc.alloc_psum_tensor
    t["pA"] = AP_("pA", [128, D], F32)[:]
    t["pB"] = AP_("pB", [128, D], F32)[:]
    t["pO"] = AP_("pO", [128, D], F32)[:]
    t["ptrb"] = AP_("ptrb", [128, D], BF16)[:]
    t["ptrk"] = AP_("ptrk", [128, D], BF16)[:]

    p = P(nc)
    for s in ("L", "Lw", "Lr", "O", "T", "V", "S", "G", "Ls"):
        p.sem(s)

    FCS_W = [w for (_, w) in FCS]
    wspec = [("wq", wq_d, 128, C * D), ("wk", wk_d, 128, C * D),
             ("wv", wv_d, 128, C * D), ("wo", wo_d, 128, C * D),
             ("wqT", wqT_d, 128, C * D), ("wkT", wkT_d, 128, C * D),
             ("wvT", wvT_d, 128, C * D), ("woT", woT_d, 128, C * D),
             ("gwT", gw_d, 128, C * 16), ("sgwT", sgw_d, 128, C)]
    for pfx in ("q", "k", "v"):
        wspec += [(pfx + "_w1", dr[pfx + "_w1"], 128, C * D),
                  (pfx + "_w2", dr[pfx + "_w2"], 128, C * D),
                  (pfx + "_b2", dr[pfx + "_b2"], 16, D),
                  (pfx + "_nwT", dr[pfx + "_nwT"], 128, C * 15)]
    for i in range(3):
        wspec += [(f"swgT{i}", swg_d[i], 128, C * FCS_W[i]),
                  (f"swuT{i}", swu_d[i], 128, C * FCS_W[i]),
                  (f"swdT{i}", swd_d[i], 128, (FCS_W[i] // 128) * D)]
    for e_ in range(16):
        wspec += [(f"wgu{e_}", wgu_d[e_], 128, C * D), (f"wdn{e_}", wdn_d[e_], 128, 4 * D)]
    scr = {}
    for nm, _, npart, ncols in wspec:
        scr[nm] = nc.dram_tensor("s_" + nm, [npart, ncols], BF16, kind="Internal").ap()

    emit_consts(p, t, f32id=False)
    mv = p.op("gpsimd", lambda e: e.memset(t["mask_bfm"][:], 0.0), inc="G")
    p.wait("gpsimd", "G", mv)
    p.op("gpsimd", lambda e: e.affine_select(out=t["mask_bfm"][:], in_=t["mask_bfm"][:],
                                             compare_op=ALU.is_ge, fill=-10000.0, base=0,
                                             pattern=[[-1, 128]], channel_multiplier=1), inc="G")
    mv = p.op("gpsimd", lambda e: e.memset(t["Ebf"][:], 1.0), inc="G")
    p.wait("gpsimd", "G", mv)
    p.op("gpsimd", lambda e: e.affine_select(out=t["Ebf"][:], in_=t["Ebf"][:],
                                             compare_op=ALU.is_ge, fill=0.0, base=0,
                                             pattern=[[1, D]], channel_multiplier=-64), inc="G")
    p.op("gpsimd", lambda e: e.affine_select(out=t["Ebf"][:], in_=t["Ebf"][:],
                                             compare_op=ALU.is_ge, fill=0.0, base=63,
                                             pattern=[[-1, D]], channel_multiplier=64), inc="G")
    gv = p.val("G")
    for eng in ("tensor", "scalar", "vector"):
        p.wait(eng, "G", gv)

    def emit_phase0():
        t["_scrmark"] = {}
        with contextlib.ExitStack() as st_:
            stf = [st_.enter_context(nc.sbuf_tensor(f"ph0f{i}", [128, 2048], F32))[:]
                   for i in range(2)]
            stb = [st_.enter_context(nc.sbuf_tensor(f"ph0b{i}", [128, 2048], BF16))[:]
                   for i in range(2)]
            cvt_mark = [0, 0]
            out_mark = [0, 0]
            k = 0
            for nm, src_ap, npart, ncols in wspec:
                for j in range(0, ncols, 2048):
                    w = min(2048, ncols - j)
                    par = k % 2
                    k += 1
                    if out_mark[par]:
                        p.wait("gpsimd", "Ls", out_mark[par])
                    p.op("gpsimd", lambda e, src_ap=src_ap, npart=npart, j=j, w=w, par=par:
                         e.dma_start(out=stf[par][0:npart, 0:w],
                                     in_=src_ap[:, j:j + w]), inc="Lw", n=16)
                    p.wait("gpsimd", "Lw", p.val("Lw"))
                    cvt_mark[par] = p.op("gpsimd", lambda e, npart=npart, w=w, par=par:
                                         e.tensor_copy(out=stb[par][0:npart, 0:w],
                                                       in_=stf[par][0:npart, 0:w]), inc="G")
                    out_mark[par] = p.op("gpsimd", lambda e, nm=nm, npart=npart, j=j, w=w,
                                         par=par: e.dma_start(
                        out=scr[nm][:, j:j + w], in_=stb[par][0:npart, 0:w]), inc="Ls", n=16)
                t["_scrmark"][nm] = out_mark[(k - 1) % 2]

    def wload(nm, dst_ap, ncols, war_T=None):
        p.wait("sync", "Ls", t["_scrmark"][nm])
        if war_T:
            p.wait("sync", "T", war_T)
        return p.op("sync", lambda e, nm=nm, ncols=ncols, dst_ap=dst_ap: e.dma_start(
            out=dst_ap[:, 0:ncols], in_=scr[nm][:, 0:ncols]), inc="Lw", n=16)

    def load_x(seq):
        for ti in range(T):
            if t.get("_stage_last") is not None:
                p.wait("sync", "S", t["_stage_last"])
            lv = p.op("sync", lambda e, seq=seq, ti=ti: e.dma_start(
                out=t["stage"][:], in_=x_dram[seq * SL + 128 * ti: seq * SL + 128 * ti + 128, :]),
                inc="L", n=16)
            p.wait("scalar", "L", lv)
            p.wait("scalar", "V", p.val("V"))
            t["_stage_last"] = p.op("scalar", lambda e, ti=ti: e.copy(
                out=t["x_bf"][:, D * ti:D * ti + D], in_=t["stage"][:]), inc="S")

    def build_xfm():
        p.wait("tensor", "S", p.val("S"))
        for ti in range(T):
            p.wait("tensor", "V", p.val("V"))
            tv = None
            for c in range(C):
                tv = p.op("tensor", lambda e, ti=ti, c=c: e.transpose(
                    t["ptrb"][:, 128 * c:128 * c + 128],
                    t["x_bf"][:, D * ti + 128 * c: D * ti + 128 * c + 128],
                    t["ident_bf"][:]), inc="T")
            p.wait("vector", "T", tv)
            dst = t["xfm"].rearrange("p (c w) -> p c w", w=SL)[:, :, 128 * ti:128 * ti + 128]
            src = t["ptrb"].rearrange("p (c q) -> p c q", q=128)
            p.op("vector", lambda e, dst=dst, src=src: e.tensor_copy(out=dst, in_=src), inc="V")

    def xbf_tile(ti):
        return t["x_bf"][:, D * ti:D * ti + D]

    def bcast_to(row_src_d, dst_ap, n, bf=True):
        p.wait("sync", "T", p.val("T"))
        lv = p.op("sync", lambda e: e.dma_start(out=t["row"][:, 0:n], in_=row_src_d[:]),
                  inc="Lr", n=16)
        p.wait("tensor", "Lr", lv)
        p.wait("tensor", "V", p.val("V"))
        tv = None
        for j in range(0, n, 512):
            w = min(512, n - j)
            tv = p.op("tensor", lambda e, j=j, w=w: e.matmul(
                t["pA"][:, j:j + w], t["ones_f32"][:], t["row"][:, j:j + w],
                start=True, stop=True), inc="T")
        p.wait("vector", "T", tv)
        return p.op("vector", lambda e: e.tensor_copy(out=dst_ap[:, 0:n], in_=t["pA"][:, 0:n]),
                    inc="V")

    def emit_A(seq):
        p.mark(f'A{seq}')
        with contextlib.ExitStack() as st_:
            a = {}
            for nm, shape, dt in (("ktm", [128, T * D], BF16), ("vtm", [128, T * D], BF16),
                                  ("ctx", [128, C * D], BF16), ("kexp_f", [128, SL], BF16),
                                  ("qexp", [128, D], BF16), ("qfm", [128, D], BF16),
                                  ("afm", [128, D], BF16),
                                  ("r1", [128, D], F32),
                                  ("bo_bc", [128, D], BF16), ("gam_bc", [128, D], BF16),
                                  ("bet_bc", [128, D], BF16)):
                a[nm] = st_.enter_context(nc.sbuf_tensor(f"A{seq}_{nm}", shape, dt))[:]
            a["attnb"] = a["qexp"]
            p.barrier()
            for dram, dst in ((bo_d, "bo_bc"), (gam_d, "gam_bc"), (bet_d, "bet_bc")):
                bcast_to(dram, a[dst], D)
            wk_ready = wload("wk", t["W0"], C * D, war_T=p.val("T"))
            wv_ready = wload("wv", t["W1"], C * D)
            build_xfm()
            p.mark(f'A{seq}.K')
            p.wait("tensor", "Lw", wk_ready)
            for c in range(C):
                p.wait("tensor", "S", p.val("S"))
                p.wait("tensor", "V", p.val("V"))
                tv = None
                for tc in range(4):
                    dstp = t["pA"] if tc < 2 else t["pB"]
                    off = 512 * (tc % 2)
                    for cin in range(C):
                        tv = p.op("tensor", lambda e, c=c, tc=tc, cin=cin, dstp=dstp, off=off:
                                  e.matmul(dstp[:, off:off + 512],
                                           t["W0"][:, D * cin + 128 * c: D * cin + 128 * c + 128],
                                           t["xfm"][:, SL * cin + 512 * tc: SL * cin + 512 * tc + 512],
                                           start=(cin == 0), stop=(cin == C - 1)), inc="T")
                p.wait("scalar", "T", tv)
                p.op("scalar", lambda e, a=a: e.activation(
                    out=a["kexp_f"][:, 0:1024], in_=t["pA"][:], func=AF.Exp,
                    bias=t["shift24"][:], scale=1.0, accum_out=t["zc2"][:, 0:1]))
                sv = p.op("scalar", lambda e, a=a: e.activation(
                    out=a["kexp_f"][:, 1024:2048], in_=t["pB"][:], func=AF.Exp,
                    bias=t["shift24"][:], scale=1.0, accum_out=t["zc2"][:, 1:2]), inc="S")
                p.wait("vector", "S", sv)
                p.op("vector", lambda e: e.tensor_tensor(out=t["zc"][:], in0=t["zc2"][:, 0:1],
                                                         in1=t["zc2"][:, 1:2], op=ALU.add))
                p.op("vector", lambda e, c=c: e.reciprocal(t["rzk"][:, c:c + 1], t["zc"][:]))
                p.wait("tensor", "S", sv)
                for r in range(2):
                    p.wait("tensor", "V", p.val("V"))
                    tv = None
                    for j in range(8):
                        tv = p.op("tensor", lambda e, a=a, r=r, j=j: e.transpose(
                            t["ptrb"][:, 128 * j:128 * j + 128],
                            a["kexp_f"][:, 1024 * r + 128 * j: 1024 * r + 128 * j + 128],
                            t["ident_bf"][:]), inc="T")
                    p.wait("vector", "T", tv)
                    dst = a["ktm"].rearrange("p (t w) -> p t w", w=D)[:, 8 * r:8 * r + 8,
                                                                     128 * c:128 * c + 128]
                    src = t["ptrb"].rearrange("p (q w) -> p q w", w=128)
                    p.op("vector", lambda e, dst=dst, src=src: e.tensor_copy(out=dst, in_=src),
                         inc="V")
            wq_ready = wload("wq", t["W0"], C * D, war_T=p.val("T"))
            p.mark(f'A{seq}.V')
            p.wait("tensor", "Lw", wv_ready)
            for ti in range(T):
                p.wait("tensor", "V", p.val("V"))
                p.wait("tensor", "S", p.val("S"))
                tv = None
                for half in range(2):
                    for cin in range(C):
                        tv = p.op("tensor", lambda e, ti=ti, half=half, cin=cin: e.matmul(
                            t["pA"][:, 512 * half:512 * half + 512],
                            t["xfm"][:, SL * cin + 128 * ti: SL * cin + 128 * ti + 128],
                            t["W1"][:, D * cin + 512 * half: D * cin + 512 * half + 512],
                            start=(cin == 0), stop=(cin == C - 1)), inc="T")
                p.wait("vector", "T", tv)
                p.op("vector", lambda e, a=a, ti=ti: e.tensor_copy(
                    out=a["vtm"][:, D * ti:D * ti + D], in_=t["pA"][:]), inc="V")
            wo_ready = wload("wo", t["W1"], C * D, war_T=p.val("T"))
            p.mark(f'A{seq}.ctx')
            for c in range(C):
                p.wait("tensor", "V", p.val("V"))
                p.wait("tensor", "S", p.val("S"))
                tv = None
                for half in range(2):
                    for ti in range(T):
                        tv = p.op("tensor", lambda e, a=a, c=c, half=half, ti=ti: e.matmul(
                            t["pB"][:, 512 * half:512 * half + 512],
                            a["ktm"][:, D * ti + 128 * c: D * ti + 128 * c + 128],
                            a["vtm"][:, D * ti + 512 * half: D * ti + 512 * half + 512],
                            start=(ti == 0), stop=(ti == T - 1)), inc="T")
                p.wait("vector", "T", tv)
                p.op("vector", lambda e, a=a, c=c: e.tensor_scalar(
                    out=a["ctx"][:, D * c:D * c + D], in0=t["pB"][:],
                    scalar1=t["rzk"][:, c:c + 1], scalar2=None, op0=ALU.mult), inc="V")
            p.wait("tensor", "Lw", wq_ready)
            p.wait("tensor", "Lw", wo_ready)
            p.mark(f'A{seq}.A4')
            for ti in range(T):
                p.wait("tensor", "S", p.val("S"))
                p.wait("tensor", "V", p.val("V"))
                tv = None
                for half in range(2):
                    for cin in range(C):
                        tv = p.op("tensor", lambda e, ti=ti, half=half, cin=cin: e.matmul(
                            t["pA"][:, 512 * half:512 * half + 512],
                            t["xfm"][:, SL * cin + 128 * ti: SL * cin + 128 * ti + 128],
                            t["W0"][:, D * cin + 512 * half: D * cin + 512 * half + 512],
                            start=(cin == 0), stop=(cin == C - 1)), inc="T")
                p.wait("scalar", "T", tv)
                sv = p.op("scalar", lambda e, a=a: e.activation(
                    out=a["qexp"][:], in_=t["pA"][:], func=AF.Exp,
                    bias=t["shift24"][:], scale=1.0, accum_out=t["zc"][:]), inc="S")
                p.wait("vector", "S", sv)
                p.op("vector", lambda e: e.reciprocal(t["rzq"][:], t["zc"][:]))
                p.wait("tensor", "S", sv)
                p.wait("tensor", "V", p.val("V"))
                tv = None
                for c in range(C):
                    tv = p.op("tensor", lambda e, a=a, c=c: e.transpose(
                        t["ptrb"][:, 128 * c:128 * c + 128],
                        a["qexp"][:, 128 * c:128 * c + 128], t["ident_bf"][:]), inc="T")
                p.wait("vector", "T", tv)
                qv = p.op("vector", lambda e, a=a: e.tensor_copy(out=a["qfm"][:], in_=t["ptrb"][:]),
                          inc="V")
                p.wait("tensor", "V", qv)
                tv = None
                for half in range(2):
                    for c in range(C):
                        tv = p.op("tensor", lambda e, a=a, half=half, c=c: e.matmul(
                            t["pB"][:, 512 * half:512 * half + 512],
                            a["qfm"][:, 128 * c:128 * c + 128],
                            a["ctx"][:, D * c + 512 * half: D * c + 512 * half + 512],
                            start=(c == 0), stop=(c == C - 1)), inc="T")
                p.wait("vector", "T", tv)
                av = p.op("vector", lambda e, a=a: e.tensor_scalar(
                    out=a["attnb"][:], in0=t["pB"][:], scalar1=t["rzq"][:],
                    scalar2=float(D) ** -0.5, op0=ALU.mult, op1=ALU.mult), inc="V")
                p.wait("tensor", "V", av)
                tv = None
                for c in range(C):
                    tv = p.op("tensor", lambda e, a=a, c=c: e.transpose(
                        t["ptrb"][:, 128 * c:128 * c + 128],
                        a["attnb"][:, 128 * c:128 * c + 128], t["ident_bf"][:]), inc="T")
                p.wait("vector", "T", tv)
                fv = p.op("vector", lambda e, a=a: e.tensor_copy(out=a["afm"][:], in_=t["ptrb"][:]),
                          inc="V")
                p.wait("tensor", "V", fv)
                tv = None
                for half in range(2):
                    for c in range(C):
                        tv = p.op("tensor", lambda e, a=a, half=half, c=c: e.matmul(
                            t["pA"][:, 512 * half:512 * half + 512],
                            a["afm"][:, 128 * c:128 * c + 128],
                            t["W1"][:, D * c + 512 * half: D * c + 512 * half + 512],
                            start=(c == 0), stop=(c == C - 1)), inc="T")
                p.wait("vector", "T", tv)
                p.op("vector", lambda e, a=a: e.tensor_tensor(out=a["r1"][:], in0=t["pA"][:],
                                                              in1=a["bo_bc"][:], op=ALU.add))
                p.op("vector", lambda e, a=a, ti=ti: e.tensor_tensor(
                    out=a["r1"][:], in0=a["r1"][:], in1=xbf_tile(ti), op=ALU.add), inc="V")
                emit_ln_stats2(p, t, a["r1"][:], t["mu1"][:], t["rstd1"][:], D, a["afm"][:])
                p.op("vector", lambda e, a=a: e.tensor_scalar(
                    out=a["attnb"][:], in0=a["r1"][:], scalar1=t["mu1"][:],
                    scalar2=t["rstd1"][:], op0=ALU.subtract, op1=ALU.mult))
                p.op("vector", lambda e, a=a: e.tensor_tensor(out=a["attnb"][:], in0=a["attnb"][:],
                                                              in1=a["gam_bc"][:], op=ALU.mult))
                p.op("vector", lambda e, a=a: e.tensor_tensor(out=a["attnb"][:], in0=a["attnb"][:],
                                                              in1=a["bet_bc"][:], op=ALU.add))
                p.op("vector", lambda e, a=a, ti=ti: e.tensor_tensor(
                    out=a["r1"][:], in0=a["attnb"][:], in1=xbf_tile(ti), op=ALU.add), inc="V")
                emit_ln_stats2(p, t, a["r1"][:], t["mu2"][:], t["rstd2"][:], D, a["afm"][:])
                p.op("vector", lambda e, a=a, ti=ti: e.tensor_scalar(
                    out=xbf_tile(ti), in0=a["r1"][:], scalar1=t["mu2"][:],
                    scalar2=t["rstd2"][:], op0=ALU.subtract, op1=ALU.mult), inc="V")
            p.barrier()

    def emit_B(seq):
        p.mark(f'B{seq}')
        with contextlib.ExitStack() as st_:
            b = {}
            for nm, shape, dt in (("kfm", [128, C * SL], BF16), ("vtm", [128, T * D], BF16),
                                  ("seltr", [16, T * 128], BF16), ("b2bf", [16, D], BF16),
                                  ("nwT", [128, 3 * C * 15], BF16), ("nbbc", [128, 45], F32),
                                  ("b1bc", [128, D], BF16), ("attn_tm", [128, SL], BF16),
                                  ("attn_km", [128, SL], BF16), ("Hbf", [128, D], BF16),
                                  ("maskbf", [128, D], BF16), ("Hm", [128, D], BF16),
                                  ("Hfm", [128, D], BF16), ("ffout", [128, D], BF16),
                                  ("qfm_t", [128, D], BF16), ("r1", [128, D], F32),
                                  ("scoreb", [128, 15], F32), ("bits", [128, 4], F32),
                                  ("nbits", [128, 4], F32), ("sig", [128, 1], F32),
                                  ("tmp1", [128, 1], F32), ("p2", [128, 4], F32),
                                  ("p3g", [128, 8], F32), ("tmp8", [128, 8], F32),
                                  ("sel16", [128, 16], F32), ("selbf", [128, 16], BF16)):
                b[nm] = st_.enter_context(nc.sbuf_tensor(f"B{seq}_{nm}", shape, dt))[:]
            p.barrier()
            build_xfm()
            nw_ready = None
            p.wait("sync", "V", p.val("V"))
            p.wait("sync", "S", p.val("S"))
            for i, pfx in enumerate(("q", "k", "v")):
                nw_ready = wload(pfx + "_nwT", b["nwT"][:, C * 15 * i:C * 15 * (i + 1)],
                                 C * 15, war_T=(p.val("T") if i == 0 else None))
                bcast_to(dr[pfx + "_nb"], b["nbbc"][:, 15 * i:], 15)
            p.wait("tensor", "Lw", nw_ready)

            def fff_scores_routing(b, pidx, ti):
                sc = b["scoreb"]
                p.wait("tensor", "V", p.val("V"))
                p.wait("tensor", "S", p.val("S"))
                tv = None
                for cin in range(C):
                    tv = p.op("tensor", lambda e, b=b, cin=cin, ti=ti, pidx=pidx: e.matmul(
                        t["pO"][:, 0:15],
                        t["xfm"][:, SL * cin + 128 * ti: SL * cin + 128 * ti + 128],
                        b["nwT"][:, C * 15 * pidx + 15 * cin: C * 15 * pidx + 15 * cin + 15],
                        start=(cin == 0), stop=(cin == C - 1)), inc="T")
                p.wait("vector", "T", tv)
                p.op("vector", lambda e, b=b, pidx=pidx: e.tensor_tensor(
                    out=sc, in0=t["pO"][:, 0:15],
                    in1=b["nbbc"][:, 15 * pidx:15 * pidx + 15], op=ALU.add))
                bb = b["bits"]; nb = b["nbits"]
                p.op("vector", lambda e, sc=sc, bb=bb: e.tensor_scalar(
                    out=bb[:, 0:1], in0=sc[:, 0:1], scalar1=0.0, scalar2=None, op0=ALU.is_ge))
                p.op("vector", lambda e, sc=sc, b=b: e.tensor_tensor(
                    out=b["tmp1"][:], in0=sc[:, 2:3], in1=sc[:, 1:2], op=ALU.subtract))
                p.op("vector", lambda e, sc=sc, b=b, bb=bb: e.scalar_tensor_tensor(
                    out=b["sig"][:], in0=b["tmp1"][:], scalar=bb[:, 0:1], in1=sc[:, 1:2],
                    op0=ALU.mult, op1=ALU.add))
                p.op("vector", lambda e, b=b, bb=bb: e.tensor_scalar(
                    out=bb[:, 1:2], in0=b["sig"][:], scalar1=0.0, scalar2=None, op0=ALU.is_ge))
                p.op("vector", lambda e, bb=bb, nb=nb: e.tensor_scalar(
                    out=nb[:, 0:2], in0=bb[:, 0:2], scalar1=-1.0, scalar2=1.0,
                    op0=ALU.mult, op1=ALU.add))
                p.op("vector", lambda e, b=b, bb=bb, nb=nb: e.tensor_tensor(
                    out=b["p2"][:, 0:1], in0=nb[:, 0:1], in1=nb[:, 1:2], op=ALU.mult))
                p.op("vector", lambda e, b=b, bb=bb, nb=nb: e.tensor_tensor(
                    out=b["p2"][:, 1:2], in0=nb[:, 0:1], in1=bb[:, 1:2], op=ALU.mult))
                p.op("vector", lambda e, b=b, bb=bb, nb=nb: e.tensor_tensor(
                    out=b["p2"][:, 2:3], in0=bb[:, 0:1], in1=nb[:, 1:2], op=ALU.mult))
                p.op("vector", lambda e, b=b, bb=bb: e.tensor_tensor(
                    out=b["p2"][:, 3:4], in0=bb[:, 0:1], in1=bb[:, 1:2], op=ALU.mult))
                p.op("vector", lambda e, sc=sc, b=b: e.tensor_tensor(
                    out=b["tmp8"][:, 0:4], in0=b["p2"][:], in1=sc[:, 3:7], op=ALU.mult))
                p.op("vector", lambda e, b=b: e.reduce_sum(b["sig"][:], b["tmp8"][:, 0:4],
                                                           axis=AX.X))
                p.op("vector", lambda e, b=b, bb=bb: e.tensor_scalar(
                    out=bb[:, 2:3], in0=b["sig"][:], scalar1=0.0, scalar2=None, op0=ALU.is_ge))
                p.op("vector", lambda e, bb=bb, nb=nb: e.tensor_scalar(
                    out=nb[:, 2:3], in0=bb[:, 2:3], scalar1=-1.0, scalar2=1.0,
                    op0=ALU.mult, op1=ALU.add))
                p.op("vector", lambda e, b=b, nb=nb: e.tensor_scalar(
                    out=b["p3g"][:, 0:4], in0=b["p2"][:], scalar1=nb[:, 2:3], scalar2=None,
                    op0=ALU.mult))
                p.op("vector", lambda e, b=b, bb=bb: e.tensor_scalar(
                    out=b["p3g"][:, 4:8], in0=b["p2"][:], scalar1=bb[:, 2:3], scalar2=None,
                    op0=ALU.mult))
                sv1 = sc[:, 7:15].rearrange("p (a b) -> p a b", b=2)[:, :, 0:1]
                sv2 = sc[:, 7:15].rearrange("p (a b) -> p a b", b=2)[:, :, 1:2]
                p.op("vector", lambda e, b=b, sv1=sv1: e.tensor_tensor(
                    out=b["tmp8"][:, 0:4].rearrange("p (a b) -> p a b", b=1),
                    in0=b["p3g"][:, 0:4].rearrange("p (a b) -> p a b", b=1),
                    in1=sv1, op=ALU.mult))
                p.op("vector", lambda e, b=b, sv2=sv2: e.tensor_tensor(
                    out=b["tmp8"][:, 4:8].rearrange("p (a b) -> p a b", b=1),
                    in0=b["p3g"][:, 4:8].rearrange("p (a b) -> p a b", b=1),
                    in1=sv2, op=ALU.mult))
                p.op("vector", lambda e, b=b: e.reduce_sum(b["sig"][:], b["tmp8"][:], axis=AX.X))
                p.op("vector", lambda e, b=b, bb=bb: e.tensor_scalar(
                    out=bb[:, 3:4], in0=b["sig"][:], scalar1=0.0, scalar2=None, op0=ALU.is_ge))
                p.op("vector", lambda e, bb=bb, nb=nb: e.tensor_scalar(
                    out=nb[:, 3:4], in0=bb[:, 3:4], scalar1=-1.0, scalar2=1.0,
                    op0=ALU.mult, op1=ALU.add))
                p.op("vector", lambda e, b=b, nb=nb: e.tensor_scalar(
                    out=b["sel16"][:, 0:8], in0=b["p3g"][:], scalar1=nb[:, 3:4], scalar2=None,
                    op0=ALU.mult))
                p.op("vector", lambda e, b=b, bb=bb: e.tensor_scalar(
                    out=b["sel16"][:, 8:16], in0=b["p3g"][:], scalar1=bb[:, 3:4], scalar2=None,
                    op0=ALU.mult))
                vv = p.op("vector", lambda e, b=b: e.tensor_copy(out=b["selbf"][:],
                                                                 in_=b["sel16"][:]))
                p.wait("tensor", "V", p.val("V"))
                tv = p.op("tensor", lambda e, b=b: e.transpose(
                    t["ptrb"][0:16, 0:128], b["selbf"][:], t["ident_bf"][:]), inc="T")
                p.wait("vector", "T", tv)
                p.op("vector", lambda e, b=b, ti=ti: e.tensor_copy(
                    out=b["seltr"][:, 128 * ti:128 * ti + 128], in_=t["ptrb"][0:16, 0:128]),
                    inc="V")

            def fff_apply(b, pidx, ti, out_kind):
                selc = 128 * ti
                p.wait("tensor", "V", p.val("V"))
                tv = None
                for half in range(2):
                    for cin in range(C):
                        tv = p.op("tensor", lambda e, ti=ti, half=half, cin=cin: e.matmul(
                            t["pA"][:, 512 * half:512 * half + 512],
                            t["xfm"][:, SL * cin + 128 * ti: SL * cin + 128 * ti + 128],
                            t["W0"][:, D * cin + 512 * half: D * cin + 512 * half + 512],
                            start=(cin == 0), stop=(cin == C - 1)), inc="T")
                mv = None
                for half in range(2):
                    mv = p.op("tensor", lambda e, b=b, half=half, selc=selc: e.matmul(
                        t["pB"][:, 512 * half:512 * half + 512],
                        b["seltr"][:, selc:selc + 128],
                        t["Ebf"][:, 512 * half:512 * half + 512],
                        start=True, stop=True), inc="T")
                p.wait("vector", "T", mv)
                p.op("vector", lambda e, b=b: e.tensor_tensor(out=b["Hbf"][:], in0=t["pA"][:],
                                                              in1=b["b1bc"][:], op=ALU.add))
                p.op("vector", lambda e, b=b: e.tensor_scalar(out=b["Hbf"][:], in0=b["Hbf"][:],
                                                              scalar1=0.0, scalar2=None,
                                                              op0=ALU.max))
                p.op("vector", lambda e, b=b: e.tensor_copy(out=b["maskbf"][:], in_=t["pB"][:]))
                hv = p.op("vector", lambda e, b=b: e.tensor_tensor(
                    out=b["Hm"][:], in0=b["Hbf"][:], in1=b["maskbf"][:], op=ALU.mult), inc="V")
                p.wait("tensor", "V", hv)
                tv = None
                for c in range(C):
                    tv = p.op("tensor", lambda e, b=b, c=c: e.transpose(
                        t["ptrb"][:, 128 * c:128 * c + 128],
                        b["Hm"][:, 128 * c:128 * c + 128], t["ident_bf"][:]), inc="T")
                p.wait("vector", "T", tv)
                fv = p.op("vector", lambda e, b=b: e.tensor_copy(out=b["Hfm"][:], in_=t["ptrb"][:]),
                          inc="V")
                p.wait("tensor", "V", fv)
                tv = None
                for half in range(2):
                    for c in range(C):
                        p.op("tensor", lambda e, b=b, half=half, c=c: e.matmul(
                            t["pA"][:, 512 * half:512 * half + 512],
                            b["Hfm"][:, 128 * c:128 * c + 128],
                            t["W1"][:, D * c + 512 * half: D * c + 512 * half + 512],
                            start=(c == 0), stop=False), inc="T")
                    tv = p.op("tensor", lambda e, b=b, half=half, selc=selc: e.matmul(
                        t["pA"][:, 512 * half:512 * half + 512],
                        b["seltr"][:, selc:selc + 128],
                        b["b2bf"][:, 512 * half:512 * half + 512],
                        start=False, stop=True), inc="T")
                p.wait("vector", "T", tv)
                if out_kind == "vtm":
                    p.op("vector", lambda e, b=b, ti=ti: e.tensor_copy(
                        out=b["vtm"][:, D * ti:D * ti + D], in_=t["pA"][:]), inc="V")
                elif out_kind == "q":
                    p.op("vector", lambda e, b=b: e.tensor_copy(out=b["ffout"][:], in_=t["pA"][:]),
                         inc="V")
                else:
                    ev = p.op("vector", lambda e, b=b: e.tensor_copy(out=b["ffout"][:],
                                                                     in_=t["pA"][:]), inc="V")
                    p.wait("tensor", "V", ev)
                    tv = None
                    for c in range(C):
                        tv = p.op("tensor", lambda e, b=b, c=c: e.transpose(
                            t["ptrb"][:, 128 * c:128 * c + 128],
                            b["ffout"][:, 128 * c:128 * c + 128], t["ident_bf"][:]), inc="T")
                    p.wait("vector", "T", tv)
                    dst = b["kfm"].rearrange("p (c w) -> p c w", w=SL)[:, :, 128 * ti:128 * ti + 128]
                    src = t["ptrb"].rearrange("p (c q) -> p c q", q=128)
                    p.op("vector", lambda e, dst=dst, src=src: e.tensor_copy(out=dst, in_=src),
                         inc="V")

            def load_fff_weights(pfx):
                tval = p.val("T")
                p.wait("sync", "V", p.val("V"))
                p.wait("sync", "S", p.val("S"))
                rw1 = wload(pfx + "_w1", t["W0"], C * D, war_T=tval)
                rw2 = wload(pfx + "_w2", t["W1"], C * D)
                rb2 = wload(pfx + "_b2", b["b2bf"], D)
                rb = bcast_to(dr[pfx + "_b1"], b["b1bc"], D)
                p.wait("tensor", "Lw", rb2)
                p.wait("vector", "V", rb)

            p.mark(f'B{seq}.kv')
            for pidx, pfx, kind in ((1, "k", "kfm"), (2, "v", "vtm")):
                load_fff_weights(pfx)
                for ti in range(T):
                    fff_scores_routing(b, pidx, ti)
                    fff_apply(b, pidx, ti, kind)
            p.mark(f'B{seq}.qattn')
            load_fff_weights("q")
            for ti in range(T):
                fff_scores_routing(b, 0, ti)
                fff_apply(b, 0, ti, "q")
                p.wait("tensor", "V", p.val("V"))
                tv = None
                for c in range(C):
                    tv = p.op("tensor", lambda e, b=b, c=c: e.transpose(
                        t["ptrb"][:, 128 * c:128 * c + 128],
                        b["ffout"][:, 128 * c: 128 * c + 128], t["ident_bf"][:]), inc="T")
                p.wait("vector", "T", tv)
                qv = p.op("vector", lambda e, b=b: e.tensor_copy(out=b["qfm_t"][:],
                                                                 in_=t["ptrb"][:]), inc="V")
                p.wait("tensor", "V", qv)
                p.wait("tensor", "S", p.val("S"))
                tv = None
                for tc in range(4):
                    dstp = t["pA"] if tc < 2 else t["pB"]
                    off = 512 * (tc % 2)
                    for c in range(C):
                        tv = p.op("tensor", lambda e, b=b, tc=tc, c=c, dstp=dstp, off=off:
                                  e.matmul(dstp[:, off:off + 512],
                                           b["qfm_t"][:, 128 * c:128 * c + 128],
                                           b["kfm"][:, SL * c + 512 * tc: SL * c + 512 * tc + 512],
                                           start=(c == 0), stop=(c == C - 1)), inc="T")
                p.wait("scalar", "T", tv)
                p.op("scalar", lambda e, b=b: e.activation(
                    out=b["attn_tm"][:, 0:1024], in_=t["pA"][:], func=AF.Exp,
                    bias=t["shift48"][:], scale=float(D) ** -0.5, accum_out=t["zc2"][:, 0:1]))
                sv = p.op("scalar", lambda e, b=b: e.activation(
                    out=b["attn_tm"][:, 1024:2048], in_=t["pB"][:], func=AF.Exp,
                    bias=t["shift48"][:], scale=float(D) ** -0.5, accum_out=t["zc2"][:, 1:2]),
                    inc="S")
                p.wait("vector", "S", sv)
                p.op("vector", lambda e: e.tensor_tensor(out=t["zc"][:], in0=t["zc2"][:, 0:1],
                                                         in1=t["zc2"][:, 1:2], op=ALU.add))
                p.op("vector", lambda e: e.reciprocal(t["rza"][:], t["zc"][:]))
                p.wait("tensor", "S", sv)
                for r in range(2):
                    p.wait("tensor", "V", p.val("V"))
                    tv = None
                    for j in range(8):
                        tv = p.op("tensor", lambda e, b=b, r=r, j=j: e.transpose(
                            t["ptrk"][:, 128 * j:128 * j + 128],
                            b["attn_tm"][:, 1024 * r + 128 * j: 1024 * r + 128 * j + 128],
                            t["ident_bf"][:]), inc="T")
                    p.wait("vector", "T", tv)
                    p.op("vector", lambda e, b=b, r=r: e.tensor_copy(
                        out=b["attn_km"][:, 1024 * r:1024 * r + 1024], in_=t["ptrk"][:]), inc="V")
                p.wait("tensor", "V", p.val("V"))
                tv = None
                for half in range(2):
                    for kt in range(T):
                        tv = p.op("tensor", lambda e, b=b, half=half, kt=kt: e.matmul(
                            t["pO"][:, 512 * half:512 * half + 512],
                            b["attn_km"][:, 128 * kt:128 * kt + 128],
                            b["vtm"][:, D * kt + 512 * half: D * kt + 512 * half + 512],
                            start=(kt == 0), stop=(kt == T - 1)), inc="T")
                p.wait("vector", "T", tv)
                p.op("vector", lambda e, b=b: e.tensor_scalar(
                    out=b["r1"][:], in0=t["pO"][:], scalar1=t["rza"][:], scalar2=None,
                    op0=ALU.mult))
                p.op("vector", lambda e, b=b, ti=ti: e.tensor_tensor(
                    out=b["r1"][:], in0=b["r1"][:], in1=xbf_tile(ti), op=ALU.add), inc="V")
                emit_ln_stats2(p, t, b["r1"][:], t["mu2"][:], t["rstd2"][:], D, b["Hbf"][:])
                p.op("vector", lambda e, b=b, ti=ti: e.tensor_scalar(
                    out=xbf_tile(ti), in0=b["r1"][:], scalar1=t["mu2"][:],
                    scalar2=t["rstd2"][:], op0=ALU.subtract, op1=ALU.mult), inc="V")
            p.barrier()

    def emit_C(seq):
        p.mark(f'C{seq}')
        with contextlib.ExitStack() as st_:
            cvars = {}
            for nm, shape, dt in (("kfm", [128, C * SL], BF16), ("vtm", [128, T * D], BF16),
                                  ("qtmp", [128, D], BF16), ("qfm_t", [128, D], BF16),
                                  ("attn_tm", [128, SL], BF16), ("attn_km", [128, SL], BF16),
                                  ("attn_tm2", [128, SL], BF16), ("attn_km2", [128, SL], BF16),
                                  ("out_sb", [128, D], BF16), ("ofm", [128, D], BF16),
                                  ("rza16", [128, 16], F32),
                                  ("r1", [128, D], F32)):
                cvars[nm] = st_.enter_context(nc.sbuf_tensor(f"C{seq}_{nm}", shape, dt))[:]
            cv = cvars
            p.barrier()
            build_xfm()
            SCALE = 0.125
            rq = wload("wqT", t["W0"], C * D, war_T=p.val("T"))
            rk = wload("wkT", t["W1"], C * D)
            p.wait("tensor", "Lw", rk)
            for jc in range(C):
                for tc in range(4):
                    dstp = t["pA"] if (jc * 4 + tc) % 2 == 0 else t["pB"]
                    p.wait("tensor", "V", p.val("V"))
                    tv = None
                    for cin in range(C):
                        tv = p.op("tensor", lambda e, jc=jc, tc=tc, cin=cin, dstp=dstp: e.matmul(
                            dstp[:, 0:512],
                            t["W1"][:, D * cin + 128 * jc: D * cin + 128 * jc + 128],
                            t["xfm"][:, SL * cin + 512 * tc: SL * cin + 512 * tc + 512],
                            start=(cin == 0), stop=(cin == C - 1)), inc="T")
                    p.wait("vector", "T", tv)
                    p.op("vector", lambda e, cv=cv, jc=jc, tc=tc, dstp=dstp: e.tensor_copy(
                        out=cv["kfm"][:, SL * jc + 512 * tc: SL * jc + 512 * tc + 512],
                        in_=dstp[:, 0:512]), inc="V")
            rv_ = wload("wvT", t["W1"], C * D, war_T=p.val("T"))
            p.wait("tensor", "Lw", rv_)
            for ti in range(T):
                p.wait("tensor", "V", p.val("V"))
                tv = None
                for half in range(2):
                    for cin in range(C):
                        tv = p.op("tensor", lambda e, ti=ti, half=half, cin=cin: e.matmul(
                            t["pA"][:, 512 * half:512 * half + 512],
                            t["xfm"][:, SL * cin + 128 * ti: SL * cin + 128 * ti + 128],
                            t["W1"][:, D * cin + 512 * half: D * cin + 512 * half + 512],
                            start=(cin == 0), stop=(cin == C - 1)), inc="T")
                p.wait("vector", "T", tv)
                p.op("vector", lambda e, cv=cv, ti=ti: e.tensor_copy(
                    out=cv["vtm"][:, D * ti:D * ti + D], in_=t["pA"][:]), inc="V")
            ro = wload("woT", t["W1"], C * D, war_T=p.val("T"))
            p.wait("tensor", "Lw", rq)
            p.wait("tensor", "Lw", ro)
            p.mark(f'C{seq}.qt')
            for qt in range(T):
                p.wait("tensor", "V", p.val("V"))
                tv = None
                for half in range(2):
                    for cin in range(C):
                        tv = p.op("tensor", lambda e, qt=qt, half=half, cin=cin: e.matmul(
                            t["pO"][:, 512 * half:512 * half + 512],
                            t["xfm"][:, SL * cin + 128 * qt: SL * cin + 128 * qt + 128],
                            t["W0"][:, D * cin + 512 * half: D * cin + 512 * half + 512],
                            start=(cin == 0), stop=(cin == C - 1)), inc="T")
                p.wait("vector", "T", tv)
                ev = p.op("vector", lambda e, cv=cv: e.tensor_copy(out=cv["qtmp"][:],
                                                                   in_=t["pO"][:]), inc="V")
                p.wait("tensor", "V", ev)
                tv = None
                for c in range(C):
                    tv = p.op("tensor", lambda e, cv=cv, c=c: e.transpose(
                        t["ptrb"][:, 128 * c:128 * c + 128],
                        cv["qtmp"][:, 128 * c:128 * c + 128], t["ident_bf"][:]), inc="T")
                p.wait("vector", "T", tv)
                qv = p.op("vector", lambda e, cv=cv: e.tensor_copy(out=cv["qfm_t"][:],
                                                                   in_=t["ptrb"][:]), inc="V")
                p.wait("tensor", "V", qv)
                nk = 128 * (qt + 1)
                nkc = (nk + 511) // 512
                fmarks = {}

                def c3_front(h):
                    jc, hp = h // 2, 64 * (h % 2)
                    atm = cv["attn_tm"] if h % 2 == 0 else cv["attn_tm2"]
                    p.wait("tensor", "S", p.val("S"))
                    p.wait("tensor", "V", p.val("V"))
                    mtc = (128 * qt) // 512
                    tv = None
                    for tc in range(nkc):
                        w = min(512, nk - 512 * tc)
                        dstp = t["pA"] if tc < 2 else t["pB"]
                        off = 512 * (tc % 2)
                        tv = p.op("tensor", lambda e, cv=cv, qt=qt, tc=tc, w=w, dstp=dstp,
                                  off=off, jc=jc, hp=hp, mtc=mtc: e.matmul(
                            dstp[:, off:off + w],
                            cv["qfm_t"][hp:hp + 64, 128 * jc:128 * jc + 128],
                            cv["kfm"][hp:hp + 64, SL * jc + 512 * tc: SL * jc + 512 * tc + w],
                            start=True, stop=(tc != mtc)), inc="T")
                        if tc == mtc:
                            moff = 512 * (mtc % 2) + (128 * qt - 512 * mtc)
                            tv = p.op("tensor", lambda e, dstp=dstp, moff=moff: e.matmul(
                                dstp[:, moff:moff + 128], t["ident_bf"][:],
                                t["mask_bfm"][:], start=False, stop=True), inc="T")
                    p.wait("scalar", "T", tv)
                    if ("tr", h - 2) in fmarks:
                        p.wait("scalar", "T", fmarks[("tr", h - 2)])
                    sv = None
                    for tc in range(nkc):
                        w = min(512, nk - 512 * tc)
                        srcp = t["pA"] if tc < 2 else t["pB"]
                        off = 512 * (tc % 2)
                        sv = p.op("scalar", lambda e, atm=atm, tc=tc, w=w, srcp=srcp, off=off:
                                  e.activation(
                            out=atm[:, 512 * tc:512 * tc + w], in_=srcp[:, off:off + w],
                            func=AF.Exp, bias=t["shift48"][:], scale=SCALE,
                            accum_out=t["zc4"][:, tc:tc + 1]), inc="S")
                    fmarks[("exp", h)] = sv
                    p.wait("vector", "S", sv)
                    p.op("vector", lambda e, nkc=nkc: e.reduce_sum(
                        t["zc"][:], t["zc4"][:, 0:nkc], axis=AX.X))
                    fmarks[("z", h)] = p.op("vector", lambda e, cv=cv, h=h: e.reciprocal(
                        cv["rza16"][:, h:h + 1], t["zc"][:]), inc="V")

                def c3_back(h):
                    atm = cv["attn_tm"] if h % 2 == 0 else cv["attn_tm2"]
                    akm = cv["attn_km"] if h % 2 == 0 else cv["attn_km2"]
                    p.wait("tensor", "S", fmarks[("exp", h)])
                    nb_ = qt + 1
                    tv = None
                    for r in range((nb_ + 7) // 8):
                        p.wait("tensor", "V", p.val("V"))
                        w8 = min(8, nb_ - 8 * r)
                        tv = None
                        for j in range(w8):
                            tv = p.op("tensor", lambda e, atm=atm, r=r, j=j: e.transpose(
                                t["ptrk"][:, 128 * j:128 * j + 128],
                                atm[:, 1024 * r + 128 * j: 1024 * r + 128 * j + 128],
                                t["ident_bf"][:]), inc="T")
                        p.wait("vector", "T", tv)
                        if ("outmm", h - 2) in fmarks:
                            p.wait("vector", "T", fmarks[("outmm", h - 2)])
                        p.op("vector", lambda e, akm=akm, r=r, w8=w8: e.tensor_copy(
                            out=akm[:, 1024 * r:1024 * r + 128 * w8],
                            in_=t["ptrk"][:, 0:128 * w8]), inc="V")
                    fmarks[("tr", h)] = tv
                    p.wait("tensor", "V", p.val("V"))
                    tv = None
                    for kt in range(qt + 1):
                        tv = p.op("tensor", lambda e, cv=cv, akm=akm, kt=kt, qt=qt, h=h: e.matmul(
                            t["pO"][:, 64 * h:64 * h + 64],
                            akm[:, 128 * kt:128 * kt + 128],
                            cv["vtm"][:, D * kt + 64 * h: D * kt + 64 * h + 64],
                            start=(kt == 0), stop=(kt == qt)), inc="T")
                    fmarks[("outmm", h)] = tv
                    p.wait("scalar", "T", tv)
                    p.op("scalar", lambda e, cv=cv, h=h: e.activation(
                        out=cv["out_sb"][:, 64 * h:64 * h + 64],
                        in_=t["pO"][:, 64 * h:64 * h + 64], func=AF.Copy,
                        scale=cv["rza16"][:, h:h + 1]), inc="S")

                c3_front(0)
                for h in range(1, 16):
                    c3_front(h)
                    c3_back(h - 1)
                c3_back(15)
                p.wait("tensor", "V", p.val("V"))
                p.wait("tensor", "S", p.val("S"))
                tv = None
                for c in range(C):
                    tv = p.op("tensor", lambda e, cv=cv, c=c: e.transpose(
                        t["ptrb"][:, 128 * c:128 * c + 128],
                        cv["out_sb"][:, 128 * c:128 * c + 128], t["ident_bf"][:]), inc="T")
                p.wait("vector", "T", tv)
                fv = p.op("vector", lambda e, cv=cv: e.tensor_copy(out=cv["ofm"][:],
                                                                   in_=t["ptrb"][:]), inc="V")
                p.wait("tensor", "V", fv)
                tv = None
                for half in range(2):
                    for c in range(C):
                        tv = p.op("tensor", lambda e, cv=cv, half=half, c=c: e.matmul(
                            t["pA"][:, 512 * half:512 * half + 512],
                            cv["ofm"][:, 128 * c:128 * c + 128],
                            t["W1"][:, D * c + 512 * half: D * c + 512 * half + 512],
                            start=(c == 0), stop=(c == C - 1)), inc="T")
                p.wait("vector", "T", tv)
                p.op("vector", lambda e, cv=cv, qt=qt: e.tensor_tensor(
                    out=cv["r1"][:], in0=t["pA"][:], in1=xbf_tile(qt), op=ALU.add), inc="V")
                emit_ln_stats2(p, t, cv["r1"][:], t["mu2"][:], t["rstd2"][:], D, cv["out_sb"][:])
                p.op("vector", lambda e, cv=cv, qt=qt: e.tensor_scalar(
                    out=xbf_tile(qt), in0=cv["r1"][:], scalar1=t["mu2"][:],
                    scalar2=t["rstd2"][:], op0=ALU.subtract, op1=ALU.mult), inc="V")
            p.barrier()

    def emit_D(seq, n_store_ref):
        p.mark(f'D{seq}')
        with contextlib.ExitStack() as st_:
            d = {}
            for nm, shape, dt in (("acc", [128, T * D], F32), ("Wd", [128, 8 * D], BF16),
                                  ("gwbf", [128, C * 16], BF16), ("sgwbf", [128, C], BF16),
                                  ("exps", [128, T * 16], F32), ("combine", [128, T * 16], F32),
                                  ("sglog", [128, T], F32), ("sgate", [128, T], F32),
                                  ("gs_bf", [128, D], BF16), ("U_bf", [128, D], BF16),
                                  ("U_bf2", [128, D], BF16), ("Ufm2", [128, D], BF16),
                                  ("Ufm", [128, D], BF16), ("r1", [128, D], F32),
                                  ("m1", [128, 1], F32), ("m2", [128, 1], F32),
                                  ("mask16", [128, 16], F32), ("pmsk", [128, 16], F32),
                                  ("s12", [128, 1], F32), ("rs12", [128, 1], F32)):
                d[nm] = st_.enter_context(nc.sbuf_tensor(f"D{seq}_{nm}", shape, dt))[:]
            p.barrier()
            build_xfm()
            p.wait("sync", "V", p.val("V"))
            p.wait("sync", "S", p.val("S"))
            wload("gwT", d["gwbf"], C * 16, war_T=p.val("T"))
            rsg = wload("sgwT", d["sgwbf"], C)
            p.wait("tensor", "Lw", rsg)
            for ti in range(T):
                p.wait("tensor", "S", p.val("S"))
                p.wait("tensor", "V", p.val("V"))
                tv = None
                for cin in range(C):
                    tv = p.op("tensor", lambda e, d=d, ti=ti, cin=cin: e.matmul(
                        t["pA"][:, 0:16],
                        t["xfm"][:, SL * cin + 128 * ti: SL * cin + 128 * ti + 128],
                        d["gwbf"][:, 16 * cin:16 * cin + 16],
                        start=(cin == 0), stop=(cin == C - 1)), inc="T")
                for cin in range(C):
                    tv = p.op("tensor", lambda e, d=d, ti=ti, cin=cin: e.matmul(
                        t["pA"][:, 128:129],
                        t["xfm"][:, SL * cin + 128 * ti: SL * cin + 128 * ti + 128],
                        d["sgwbf"][:, cin:cin + 1],
                        start=(cin == 0), stop=(cin == C - 1)), inc="T")
                p.wait("vector", "T", tv)
                rv = p.op("vector", lambda e: e.reduce_max(
                    t["nmax1"][:], t["pA"][:, 0:16], axis=AX.X, negate=True), inc="V")
                p.op("vector", lambda e, d=d, ti=ti: e.tensor_copy(
                    out=d["sglog"][:, ti:ti + 1], in_=t["pA"][:, 128:129]))
                p.wait("scalar", "V", p.val("V"))
                p.op("scalar", lambda e, d=d, ti=ti: e.activation(
                    out=d["exps"][:, 16 * ti:16 * ti + 16], in_=t["pA"][:, 0:16], func=AF.Exp,
                    bias=t["nmax1"][:], scale=1.0), inc="S")
            p.wait("scalar", "V", p.val("V"))
            p.op("scalar", lambda e, d=d: e.activation(out=d["sgate"][:], in_=d["sglog"][:],
                                                       func=AF.Sigmoid), inc="S")
            p.wait("vector", "S", p.val("S"))
            for ti in range(T):
                ex = d["exps"][:, 16 * ti:16 * ti + 16]
                cb = d["combine"][:, 16 * ti:16 * ti + 16]
                p.op("vector", lambda e, ex=ex, d=d: e.reduce_max(d["m1"][:], ex, axis=AX.X))
                p.op("vector", lambda e, ex=ex, d=d: e.tensor_scalar(
                    out=d["mask16"][:], in0=ex, scalar1=d["m1"][:], scalar2=None, op0=ALU.is_ge))
                p.op("vector", lambda e, ex=ex, d=d: e.scalar_tensor_tensor(
                    out=d["pmsk"][:], in0=d["mask16"][:], scalar=-1e30, in1=ex,
                    op0=ALU.mult, op1=ALU.add))
                p.op("vector", lambda e, d=d: e.reduce_max(d["m2"][:], d["pmsk"][:], axis=AX.X))
                p.op("vector", lambda e, ex=ex, d=d: e.tensor_scalar(
                    out=d["mask16"][:], in0=ex, scalar1=d["m2"][:], scalar2=None, op0=ALU.is_ge))
                p.op("vector", lambda e, d=d: e.tensor_tensor(
                    out=d["s12"][:], in0=d["m1"][:], in1=d["m2"][:], op=ALU.add))
                p.op("vector", lambda e, d=d: e.reciprocal(d["rs12"][:], d["s12"][:]))
                p.op("vector", lambda e, ex=ex, d=d: e.tensor_tensor(
                    out=d["pmsk"][:], in0=ex, in1=d["mask16"][:], op=ALU.mult))
                p.op("vector", lambda e, cb=cb, d=d: e.tensor_scalar(
                    out=cb, in0=d["pmsk"][:], scalar1=d["rs12"][:], scalar2=None, op0=ALU.mult))
            p.mark(f'D{seq}.D1')
            for fci, (f0, wfc) in enumerate(FCS):
                tval = p.val("T")
                rg = wload(f"swgT{fci}", t["W0"], C * wfc, war_T=tval)
                ru = wload(f"swuT{fci}", t["W1"], C * wfc)
                rd = wload(f"swdT{fci}", d["Wd"], (wfc // 128) * D)
                p.wait("tensor", "Lw", rd)
                nh = wfc // 512 + (1 if wfc % 512 else 0)
                nc2 = wfc // 128
                for ti in range(T):
                    p.wait("tensor", "S", p.val("S"))
                    p.wait("tensor", "V", p.val("V"))
                    tv = None
                    for half in range(nh):
                        w = min(512, wfc - 512 * half)
                        for cin in range(C):
                            tv = p.op("tensor", lambda e, ti=ti, half=half, w=w, cin=cin,
                                      wfc=wfc: e.matmul(
                                t["pA"][:, 512 * half:512 * half + w],
                                t["xfm"][:, SL * cin + 128 * ti: SL * cin + 128 * ti + 128],
                                t["W0"][:, wfc * cin + 512 * half: wfc * cin + 512 * half + w],
                                start=(cin == 0), stop=(cin == C - 1)), inc="T")
                    for half in range(nh):
                        w = min(512, wfc - 512 * half)
                        for cin in range(C):
                            tv = p.op("tensor", lambda e, ti=ti, half=half, w=w, cin=cin,
                                      wfc=wfc: e.matmul(
                                t["pB"][:, 512 * half:512 * half + w],
                                t["xfm"][:, SL * cin + 128 * ti: SL * cin + 128 * ti + 128],
                                t["W1"][:, wfc * cin + 512 * half: wfc * cin + 512 * half + w],
                                start=(cin == 0), stop=(cin == C - 1)), inc="T")
                    p.wait("scalar", "T", tv)
                    sv = p.op("scalar", lambda e, d=d, wfc=wfc: e.activation(
                        out=d["gs_bf"][:, 0:wfc], in_=t["pA"][:, 0:wfc], func=AF.Silu), inc="S")
                    p.wait("vector", "S", sv)
                    uv = p.op("vector", lambda e, d=d, wfc=wfc: e.tensor_tensor(
                        out=d["U_bf"][:, 0:wfc], in0=d["gs_bf"][:, 0:wfc],
                        in1=t["pB"][:, 0:wfc], op=ALU.mult), inc="V")
                    p.wait("tensor", "V", uv)
                    tv = None
                    for c2 in range(nc2):
                        tv = p.op("tensor", lambda e, d=d, c2=c2: e.transpose(
                            t["ptrb"][:, 128 * c2:128 * c2 + 128],
                            d["U_bf"][:, 128 * c2:128 * c2 + 128], t["ident_bf"][:]), inc="T")
                    p.wait("vector", "T", tv)
                    fv = p.op("vector", lambda e, d=d, wfc=wfc: e.tensor_copy(
                        out=d["Ufm"][:, 0:wfc], in_=t["ptrb"][:, 0:wfc]), inc="V")
                    p.wait("tensor", "V", fv)
                    tv = None
                    for half in range(2):
                        for c2 in range(nc2):
                            tv = p.op("tensor", lambda e, d=d, half=half, c2=c2, wfc=wfc,
                                      nc2=nc2: e.matmul(
                                t["pO"][:, 512 * half:512 * half + 512],
                                d["Ufm"][:, 128 * c2:128 * c2 + 128],
                                d["Wd"][:, D * c2 + 512 * half: D * c2 + 512 * half + 512],
                                start=(c2 == 0), stop=(c2 == nc2 - 1)), inc="T")
                    p.wait("vector", "T", tv)
                    sgc = d["sgate"][:, ti:ti + 1]
                    at = d["acc"][:, D * ti:D * ti + D]
                    if fci == 0:
                        p.op("vector", lambda e, at=at, sgc=sgc: e.tensor_scalar(
                            out=at, in0=t["pO"][:], scalar1=sgc, scalar2=None, op0=ALU.mult),
                            inc="V")
                    else:
                        p.op("vector", lambda e, at=at, sgc=sgc: e.scalar_tensor_tensor(
                            out=at, in0=t["pO"][:], scalar=sgc, in1=at,
                            op0=ALU.mult, op1=ALU.add), inc="V")
            p.mark(f'D{seq}.D2')
            wg_ar = {0: t["W0"], 1: t["W1"]}
            wd_ar = {0: d["Wd"][:, 0:4 * D], 1: d["Wd"][:, 4 * D:8 * D]}
            t_done = {0: p.val("T"), 1: p.val("T")}
            eloads = {}

            def load_expert(ex_, ready_T):
                par = ex_ % 2
                rg = wload(f"wgu{ex_}", wg_ar[par], C * D, war_T=ready_T)
                rd = wload(f"wdn{ex_}", wd_ar[par], 4 * D)
                return rg, rd

            eloads[0] = load_expert(0, t_done[0])
            eloads[1] = load_expert(1, t_done[1])
            for ex_ in range(16):
                par = ex_ % 2
                wg, wd = wg_ar[par], wd_ar[par]
                rg, rd = eloads[ex_]
                p.wait("tensor", "Lw", rd)
                marks = {}

                def emit_gu2(ti, wg=wg):
                    ppar = ti % 2
                    pg = t["pA"] if ppar == 0 else t["pB"]
                    if ti >= 2:
                        p.wait("tensor", "S", marks[("silu", ti - 2)])
                        p.wait("tensor", "V", marks[("u", ti - 2)])
                    tv = None
                    for half in range(2):
                        for cin in range(C):
                            tv = p.op("tensor", lambda e, ti=ti, half=half, cin=cin, wg=wg,
                                      pg=pg: e.matmul(
                                pg[:, 512 * half:512 * half + 512],
                                t["xfm"][:, SL * cin + 128 * ti: SL * cin + 128 * ti + 128],
                                wg[:, D * cin + 512 * half: D * cin + 512 * half + 512],
                                start=(cin == 0), stop=(cin == C - 1)), inc="T")
                    marks[("gu", ti)] = p.val("T")

                p.wait("tensor", "V", p.val("V"))
                emit_gu2(0)
                for ti in range(T):
                    ppar = ti % 2
                    pg = t["pA"] if ppar == 0 else t["pB"]
                    if ti + 1 < T:
                        emit_gu2(ti + 1)
                    p.wait("scalar", "T", marks[("gu", ti)])
                    if ti >= 1:
                        p.wait("scalar", "V", marks[("u", ti - 1)])
                    marks[("silu", ti)] = p.op("scalar", lambda e, d=d, pg=pg: e.activation(
                        out=d["gs_bf"][:, 0:512], in_=pg[:, 0:512], func=AF.Silu), inc="S")
                    p.wait("vector", "S", marks[("silu", ti)])
                    ub = d["U_bf"] if ppar == 0 else d["U_bf2"]
                    cbc = d["combine"][:, 16 * ti + ex_:16 * ti + ex_ + 1]
                    p.op("vector", lambda e, d=d, pg=pg, ub=ub: e.tensor_tensor(
                        out=ub[:, 0:512], in0=d["gs_bf"][:, 0:512],
                        in1=pg[:, 512:1024], op=ALU.mult))
                    marks[("u", ti)] = p.op("vector", lambda e, cbc=cbc, ub=ub: e.tensor_scalar(
                        out=ub[:, 0:512], in0=ub[:, 0:512], scalar1=cbc,
                        scalar2=None, op0=ALU.mult), inc="V")
                    p.wait("tensor", "V", marks[("u", ti)])
                    if ti >= 1:
                        p.wait("tensor", "V", marks[("ufm", ti - 1)])
                    tv = None
                    for c2 in range(4):
                        tv = p.op("tensor", lambda e, c2=c2, ub=ub: e.transpose(
                            t["ptrb"][:, 128 * c2:128 * c2 + 128],
                            ub[:, 128 * c2:128 * c2 + 128], t["ident_bf"][:]), inc="T")
                    p.wait("vector", "T", tv)
                    uf = d["Ufm"] if ppar == 0 else d["Ufm2"]
                    marks[("ufm", ti)] = p.op("vector", lambda e, uf=uf: e.tensor_copy(
                        out=uf[:, 0:512], in_=t["ptrb"][:, 0:512]), inc="V")
                    p.wait("tensor", "V", marks[("ufm", ti)])
                    tv = None
                    for half in range(2):
                        for c2 in range(4):
                            tv = p.op("tensor", lambda e, d=d, half=half, c2=c2, wd=wd,
                                      uf=uf: e.matmul(
                                t["pO"][:, 512 * half:512 * half + 512],
                                uf[:, 128 * c2:128 * c2 + 128],
                                wd[:, D * c2 + 512 * half: D * c2 + 512 * half + 512],
                                start=(c2 == 0), stop=(c2 == 3)), inc="T")
                    p.wait("vector", "T", tv)
                    at = d["acc"][:, D * ti:D * ti + D]
                    p.op("vector", lambda e, at=at: e.tensor_tensor(
                        out=at, in0=at, in1=t["pO"][:], op=ALU.add), inc="V")
                t_done[par] = p.val("T")
                if ex_ + 2 < 16:
                    eloads[ex_ + 2] = load_expert(ex_ + 2, t_done[par])
            p.mark(f'D{seq}.D3')
            for ti in range(T):
                p.wait("vector", "O", p.val("O"))
                rm = p.op("vector", lambda e, d=d, ti=ti: e.tensor_tensor(
                    out=d["r1"][:], in0=d["acc"][:, D * ti:D * ti + D], in1=xbf_tile(ti),
                    op=ALU.add), inc="V")
                if seq + 1 < NSEQ:
                    if t.get("_stage_last") is not None:
                        p.wait("sync", "S", t["_stage_last"])
                    lv = p.op("sync", lambda e, seq=seq, ti=ti: e.dma_start(
                        out=t["stage"][:],
                        in_=x_dram[(seq + 1) * SL + 128 * ti: (seq + 1) * SL + 128 * ti + 128, :]),
                        inc="L", n=16)
                    p.wait("scalar", "L", lv)
                    p.wait("scalar", "V", rm)
                    t["_stage_last"] = p.op("scalar", lambda e, ti=ti: e.copy(
                        out=t["x_bf"][:, D * ti:D * ti + D], in_=t["stage"][:]), inc="S")
                emit_ln_stats2(p, t, d["r1"][:], t["mu2"][:], t["rstd2"][:], D, d["gs_bf"][:])
                vv = p.op("vector", lambda e, d=d: e.tensor_scalar(
                    out=d["r1"][:], in0=d["r1"][:], scalar1=t["mu2"][:],
                    scalar2=t["rstd2"][:], op0=ALU.subtract, op1=ALU.mult), inc="V")
                p.wait("gpsimd", "V", vv)
                p.op("gpsimd", lambda e, d=d, seq=seq, ti=ti: e.dma_start(
                    out=y_dram[seq * SL + 128 * ti: seq * SL + 128 * ti + 128, :],
                    in_=d["r1"][:]), inc="O", n=16)
                n_store_ref[0] += 1
            p.barrier()

    emit_phase0()

    n_store = [0]
    for seq in range(NSEQ):
        if seq == 0:
            load_x(seq)
        emit_A(seq)
        emit_B(seq)
        emit_C(seq)
        emit_D(seq, n_store)

    p.wait("gpsimd", "O", 16 * n_store[0])
    p.finalize()
    return nc


def emit_ln_stats2(p, t, src_ap, mu_col, rstd_col, nfree, sq_dump):
    p.wait("scalar", "V", p.val("V"))
    rv = p.op("vector", lambda e: e.reduce_sum(t["red1"][:], src_ap, axis=AX.X), inc="V")
    p.wait("vector", "V", rv)
    p.op("vector", lambda e: e.tensor_scalar(out=mu_col, in0=t["red1"][:],
                                             scalar1=1.0 / nfree, scalar2=None, op0=ALU.mult))
    sv = p.op("scalar", lambda e: e.activation(out=sq_dump, in_=src_ap, func=AF.Square,
                                               accum_out=t["red2"][:]), inc="S")
    p.wait("vector", "S", sv)
    p.op("vector", lambda e: e.tensor_scalar(out=t["red2"][:], in0=t["red2"][:],
                                             scalar1=1.0 / nfree, scalar2=None, op0=ALU.mult))
    p.op("vector", lambda e: e.tensor_tensor(out=t["red3"][:], in0=mu_col, in1=mu_col,
                                             op=ALU.mult))
    vv = p.op("vector", lambda e: e.tensor_tensor(out=t["red2"][:], in0=t["red2"][:],
                                                  in1=t["red3"][:], op=ALU.subtract), inc="V")
    p.wait("scalar", "V", vv)
    p.op("scalar", lambda e: e.activation(out=rstd_col, in_=t["red2"][:], func=AF.Ln,
                                          bias=t["eps_col"][:], scale=1.0))
    sv2 = p.op("scalar", lambda e: e.activation(out=rstd_col, in_=rstd_col, func=AF.Exp,
                                                scale=-0.5), inc="S")
    p.wait("vector", "S", sv2)


def host_inputs_fused(full):
    x = np.asarray(full["L0_attn_x"], dtype=np.float32).reshape(32 * SL, D)
    m = {}
    m.update({k: v for k, v in host_inputs_A(full)[0].items() if k != "x"})
    m.update({k: v for k, v in host_inputs_B(full, x)[0].items() if k != "x"})
    m.update({k: v for k, v in host_inputs_C(full, x)[0].items() if k != "x"})
    m.update({k: v for k, v in host_inputs_D(full, x)[0].items() if k != "x"})
    maps = []
    for core in range(NCORES):
        mm = dict(m)
        mm["x"] = np.ascontiguousarray(x[core * NSEQ * SL:(core + 1) * NSEQ * SL])
        maps.append(mm)
    return maps


LEAF_OF_H = []
for _h in range(16):
    _b3, _b2, _b0, _b1 = (_h >> 3) & 1, (_h >> 2) & 1, (_h >> 1) & 1, _h & 1
    LEAF_OF_H.append(8 * _b0 + 4 * _b1 + 2 * _b2 + _b3)


FUSED = True


def kernel(**inputs):
    cores = list(range(NCORES))
    if FUSED:
        res = run_bass_kernel_spmd(build_fused(), host_inputs_fused(inputs), core_ids=cores)
        return np.concatenate([r["y"] for r in res.results], axis=0).reshape(32, SL, D)
    resA = run_bass_kernel_spmd(build_launch_A(), host_inputs_A(inputs), core_ids=cores)
    x1 = np.concatenate([r["y"] for r in resA.results], axis=0)
    resB = run_bass_kernel_spmd(build_launch_B(), host_inputs_B(inputs, x1), core_ids=cores)
    x2 = np.concatenate([r["y"] for r in resB.results], axis=0)
    resC = run_bass_kernel_spmd(build_launch_C(), host_inputs_C(inputs, x2), core_ids=cores)
    x3 = np.concatenate([r["y"] for r in resC.results], axis=0)
    resD = run_bass_kernel_spmd(build_launch_D(), host_inputs_D(inputs, x3), core_ids=cores)
    x4 = np.concatenate([r["y"] for r in resD.results], axis=0)
    return x4.reshape(32, SL, D)


# version 16 (final)
# speedup vs baseline: 1.0518x; 1.0518x over previous
import numpy as np
import concourse.bass as bass
import concourse.mybir as mybir
from concourse.bass_utils import run_bass_kernel_spmd

F32 = mybir.dt.float32
BF16 = mybir.dt.bfloat16
AF = mybir.ActivationFunctionType
ALU = mybir.AluOpType
AX = mybir.AxisListType

NCORES = 8
NSEQ = 4
SL = 2048
T = SL // 128
D = 1024
C = D // 128
EPS = 1e-5

ENGS = ("tensor", "vector", "scalar", "gpsimd", "sync")
SECTIONS = []


class P:

    def __init__(self, nc):
        self.nc = nc
        self.q = {e: [] for e in ENGS}
        self._sems = {}

    def sem(self, name):
        self._sems[name] = [self.nc.alloc_semaphore(name), 0]
        return name

    def val(self, name):
        return self._sems[name][1]

    def wait(self, eng, name, v=None):
        h, cur = self._sems[name]
        if v is None:
            v = cur
        assert v <= cur, (name, v, cur)
        if v > 0:
            self.q[eng].append(lambda e, h=h, v=v: e.wait_ge(h, v))

    AUTO = {"vector": "V", "scalar": "S", "gpsimd": "G"}

    def op(self, eng, fn, inc=None, n=1):
        if eng in self.AUTO:
            own = self.AUTO[eng]
            self.wait(eng, own, self.val(own))
            if inc is None:
                inc = own
        if inc is None:
            self.q[eng].append(fn)
            return None
        rec = self._sems[inc]
        rec[1] += n
        h = rec[0]
        self.q[eng].append(lambda e, fn=fn, h=h, n=n: fn(e).then_inc(h, n))
        return rec[1]

    def barrier(self, engines=("tensor", "vector", "scalar", "gpsimd")):
        marks = {}
        for eng in engines:
            sname = "B" + eng[0]
            if sname not in self._sems:
                self.sem(sname)
            marks[eng] = (sname, self.op(eng, lambda e: e.nop(), inc=sname))
        for eng in engines:
            for other in engines:
                if other != eng:
                    self.wait(eng, marks[other][0], marks[other][1])

    def mark(self, name):
        SECTIONS.append((name, self.val("T")))

    def finalize(self):
        with self.nc.Block() as block:
            for eng in ENGS:
                def play(e, q=self.q[eng]):
                    for f in q:
                        f(e)
                getattr(block, eng)(play)



def _cm(w):
    K, N = w.shape
    assert K % 128 == 0
    return np.ascontiguousarray(
        w.reshape(K // 128, 128, N).transpose(1, 0, 2).reshape(128, (K // 128) * N))


def _row(v):
    return np.ascontiguousarray(np.asarray(v, np.float32).reshape(1, -1))



def emit_consts(p, t, f32id=True):
    g = "gpsimd"
    mv = p.op(g, lambda e: e.memset(t["ident_bf"][:], 0.0), inc="G")
    p.wait(g, "G", mv)
    p.op(g, lambda e: e.affine_select(out=t["ident_bf"][:], in_=t["ident_bf"][:],
                                      compare_op=ALU.not_equal, fill=1.0, base=0,
                                      pattern=[[-1, 128]], channel_multiplier=1), inc="G")
    if f32id:
        mv = p.op(g, lambda e: e.memset(t["ident_f32"][:], 0.0), inc="G")
        p.wait(g, "G", mv)
        p.op(g, lambda e: e.affine_select(out=t["ident_f32"][:], in_=t["ident_f32"][:],
                                          compare_op=ALU.not_equal, fill=1.0, base=0,
                                          pattern=[[-1, 128]], channel_multiplier=1), inc="G")
    p.op(g, lambda e: e.memset(t["ones_f32"][:], 1.0), inc="G")
    p.op(g, lambda e: e.memset(t["eps_col"][:], EPS), inc="G")
    if "shift48" in t:
        p.op(g, lambda e: e.memset(t["shift48"][:], -48.0), inc="G")
        p.op(g, lambda e: e.memset(t["shift24"][:], -24.0), inc="G")


def emit_bcast_row(p, t, row_ap, dst_ap, n):
    p.wait("tensor", "V", p.val("V"))
    tv = None
    for j in range(0, n, 512):
        w = min(512, n - j)
        tv = p.op("tensor",
                  lambda e, j=j, w=w: e.matmul(t["pA"][:, j:j + w], t["ones_f32"][:],
                                               row_ap[:, j:j + w], start=True, stop=True),
                  inc="T")
    p.wait("vector", "T", tv)
    return p.op("vector", lambda e: e.tensor_copy(out=dst_ap[:, 0:n], in_=t["pA"][:, 0:n]),
                inc="V")


def emit_load_weight(p, t, dram_ap, dst_bf_ap, ncols, ready=None):
    stage = t["stage"]
    piece = stage.shape[1]
    last_s = t.get("_stage_last")
    for j in range(0, ncols, piece):
        w = min(piece, ncols - j)
        if last_s is not None:
            p.wait("sync", "S", last_s)
        lv = p.op("sync", lambda e, j=j, w=w: e.dma_start(out=stage[:, 0:w],
                                                          in_=dram_ap[:, j:j + w]),
                  inc="Lw", n=16)
        p.wait("scalar", "Lw", lv)
        if ready is not None:
            p.wait("scalar", ready[0], ready[1])
            ready = None
        last_s = p.op("scalar", lambda e, j=j, w=w: e.copy(out=dst_bf_ap[:, j:j + w],
                                                           in_=stage[:, 0:w]), inc="S")
    t["_stage_last"] = last_s
    return ("S", last_s)


def emit_xfm_tile(p, t, x_dram, seq, ti, xfm_ap, also_f32=None):
    row0 = seq * SL + ti * 128
    lv = p.op("sync", lambda e: e.dma_start(out=t["x_t"][:], in_=x_dram[row0:row0 + 128, :]),
              inc="L", n=16)
    p.wait("scalar", "L", lv)
    sv = p.op("scalar", lambda e: e.copy(out=t["xbf"][:], in_=t["x_t"][:]), inc="S")
    p.wait("tensor", "S", sv)
    p.wait("tensor", "V", p.val("V"))
    ptr = t["ptrb"]
    tv = None
    for c in range(C):
        tv = p.op("tensor", lambda e, c=c: e.transpose(ptr[:, 128 * c:128 * c + 128],
                                                       t["xbf"][:, 128 * c:128 * c + 128],
                                                       t["ident_bf"][:]), inc="T")
    p.wait("vector", "T", tv)
    dst = xfm_ap.rearrange("p (c w) -> p c w", w=SL)[:, :, 128 * ti:128 * ti + 128]
    src = ptr.rearrange("p (c q) -> p c q", q=128)
    vv = p.op("vector", lambda e, dst=dst, src=src: e.tensor_copy(out=dst, in_=src), inc="V")
    marks = {"cvt": sv, "evac": vv}
    if also_f32 is not None:
        dst32 = also_f32
        p.wait("tensor", "V", p.val("V"))
        tv2 = None
        for c in range(C):
            tv2 = p.op("tensor", lambda e, c=c: e.transpose(t["pf32"][:, 128 * c:128 * c + 128],
                                                            t["x_t"][:, 128 * c:128 * c + 128],
                                                            t["ident_f32"][:]), inc="T")
        p.wait("vector", "T", tv2)
        marks["f32"] = p.op("vector",
                            lambda e, dst32=dst32: e.tensor_copy(out=dst32, in_=t["pf32"][:]),
                            inc="V")
    return marks


def emit_ln_stats(p, t, src_ap, mu_col, rstd_col, nfree):
    p.wait("scalar", "V", p.val("V"))
    rv = p.op("vector", lambda e: e.reduce_sum(t["red1"][:], src_ap, axis=AX.X), inc="V")
    p.wait("vector", "V", rv)
    p.op("vector", lambda e: e.tensor_scalar(out=mu_col, in0=t["red1"][:],
                                             scalar1=1.0 / nfree, scalar2=None, op0=ALU.mult))
    sv = p.op("scalar", lambda e: e.activation(out=t["sqd"][:], in_=src_ap, func=AF.Square,
                                               accum_out=t["red2"][:]), inc="S")
    p.wait("vector", "S", sv)
    p.op("vector", lambda e: e.tensor_scalar(out=t["red2"][:], in0=t["red2"][:],
                                             scalar1=1.0 / nfree, scalar2=None, op0=ALU.mult))
    p.op("vector", lambda e: e.tensor_tensor(out=t["red3"][:], in0=mu_col, in1=mu_col,
                                             op=ALU.mult))
    vv = p.op("vector", lambda e: e.tensor_tensor(out=t["red2"][:], in0=t["red2"][:],
                                                  in1=t["red3"][:], op=ALU.subtract), inc="V")
    p.wait("scalar", "V", vv)
    p.op("scalar", lambda e: e.activation(out=rstd_col, in_=t["red2"][:], func=AF.Ln,
                                          bias=t["eps_col"][:], scale=1.0))
    sv2 = p.op("scalar", lambda e: e.activation(out=rstd_col, in_=rstd_col, func=AF.Exp,
                                                scale=-0.5), inc="S")
    p.wait("vector", "S", sv2)



def build_launch_A(dump=None):
    nc = bass.Bass("TRN2", target_bir_lowering=False)
    x_dram = nc.dram_tensor("x", [NSEQ * SL, D], F32, kind="ExternalInput").ap()
    y_dram = nc.dram_tensor("y", [NSEQ * SL, D], F32, kind="ExternalOutput").ap()
    wq_d = nc.dram_tensor("wq", [128, C * D], F32, kind="ExternalInput").ap()
    wk_d = nc.dram_tensor("wk", [128, C * D], F32, kind="ExternalInput").ap()
    wv_d = nc.dram_tensor("wv", [128, C * D], F32, kind="ExternalInput").ap()
    wo_d = nc.dram_tensor("wo", [128, C * D], F32, kind="ExternalInput").ap()
    bo_d = nc.dram_tensor("bo", [1, D], F32, kind="ExternalInput").ap()
    gam_d = nc.dram_tensor("gam", [1, D], F32, kind="ExternalInput").ap()
    bet_d = nc.dram_tensor("bet", [1, D], F32, kind="ExternalInput").ap()
    dbg_d = None
    if dump is not None:
        cfg = {"consts": (128, F32), "bc": (D, F32), "xfm1": (C * SL, BF16),
               "xfm": (C * SL, BF16), "ktm": (T * D, BF16), "vtm": (T * D, BF16),
               "ctx": (C * D, BF16)}[dump]
        dbg_d = nc.dram_tensor("dbg", [128, cfg[0]], cfg[1], kind="ExternalOutput").ap()

    t = {}
    A = nc.alloc_sbuf_tensor
    t["stage"] = A("stage", [128, 2048], F32)[:]
    t["W0"] = A("W0", [128, C * D], BF16)[:]
    t["W1"] = A("W1", [128, C * D], BF16)[:]
    t["xfm"] = A("xfm", [128, C * SL], BF16)[:]
    t["ktm"] = A("ktm", [128, T * D], BF16)[:]
    t["vtm"] = A("vtm", [128, T * D], BF16)[:]
    t["ctx"] = A("ctx", [128, C * D], BF16)[:]
    t["kexp_f"] = A("kexp_f", [128, SL], BF16)[:]
    t["x_t"] = A("x_t", [128, D], F32)[:]
    t["xbf"] = A("xbf", [128, D], BF16)[:]
    t["qexp"] = A("qexp", [128, D], BF16)[:]
    t["qfm"] = A("qfm", [128, D], BF16)[:]
    t["attnb"] = A("attnb", [128, D], BF16)[:]
    t["afm"] = A("afm", [128, D], BF16)[:]
    t["r1"] = A("r1", [128, D], F32)[:]
    t["sqd"] = A("sqd", [128, D], F32)[:]
    t["xnew"] = A("xnew", [128, D], F32)[:]
    t["bo_bc"] = A("bo_bc", [128, D], F32)[:]
    t["gam_bc"] = A("gam_bc", [128, D], F32)[:]
    t["bet_bc"] = A("bet_bc", [128, D], F32)[:]
    t["row"] = A("rowbuf", [1, D], F32)[:]
    t["ident_bf"] = A("ident_bf", [128, 128], BF16)[:]
    t["ident_f32"] = A("ident_f32", [128, 128], F32)[:]
    t["ones_f32"] = A("ones_f32", [1, 128], F32)[:]
    t["eps_col"] = A("eps_col", [128, 1], F32)[:]
    t["nmax4"] = A("nmax4", [128, 4], F32)[:]
    t["nmax1"] = A("nmax1", [128, 1], F32)[:]
    t["zc2"] = A("zc2", [128, 2], F32)[:]
    t["zc"] = A("zc", [128, 1], F32)[:]
    t["rzk"] = A("rzk", [128, C], F32)[:]
    t["rzq"] = A("rzq", [128, 1], F32)[:]
    t["mu1"] = A("mu1", [128, 1], F32)[:]
    t["rstd1"] = A("rstd1", [128, 1], F32)[:]
    t["mu2"] = A("mu2", [128, 1], F32)[:]
    t["rstd2"] = A("rstd2", [128, 1], F32)[:]
    t["red1"] = A("red1", [128, 1], F32)[:]
    t["red2"] = A("red2", [128, 1], F32)[:]
    t["red3"] = A("red3", [128, 1], F32)[:]

    AP_ = nc.alloc_psum_tensor
    t["pA"] = AP_("pA", [128, D], F32)[:]
    t["pB"] = AP_("pB", [128, D], F32)[:]
    t["ptrb"] = AP_("ptrb", [128, D], BF16)[:]

    p = P(nc)
    for s in ("L", "Lw", "Lr", "O", "T", "V", "S", "G"):
        p.sem(s)

    emit_consts(p, t)
    gv = p.val("G")
    p.wait("tensor", "G", gv)
    p.wait("scalar", "G", gv)
    p.wait("vector", "G", gv)

    if dump == "consts":
        p.wait("gpsimd", "G", gv)
        p.op("gpsimd", lambda e: e.dma_start(out=dbg_d[:], in_=t["ident_f32"][:]), inc="O", n=16)
        p.wait("gpsimd", "O", p.val("O"))
        p.finalize()
        return nc

    for name, dst in (("bo", "bo_bc"), ("gam", "gam_bc"), ("bet", "bet_bc")):
        d_ap = {"bo": bo_d, "gam": gam_d, "bet": bet_d}[name]
        p.wait("sync", "T", p.val("T"))
        lv = p.op("sync", lambda e, d_ap=d_ap: e.dma_start(out=t["row"][:], in_=d_ap[:]),
                  inc="Lr", n=16)
        p.wait("tensor", "Lr", lv)
        emit_bcast_row(p, t, t["row"], t[dst], D)

    def _dump_and_finish(arena_key):
        p.barrier(engines=("tensor", "vector", "scalar", "gpsimd"))
        src_ap = t[arena_key]
        p.wait("gpsimd", "V", p.val("V"))
        p.wait("gpsimd", "S", p.val("S"))
        p.op("gpsimd", lambda e: e.dma_start(out=dbg_d[:], in_=src_ap), inc="O", n=16)
        p.wait("gpsimd", "O", p.val("O"))
        p.finalize()

    if dump == "bc":
        _dump_and_finish("bo_bc")
        return nc

    wk_ready = emit_load_weight(p, t, wk_d, t["W0"], C * D)
    wv_ready = emit_load_weight(p, t, wv_d, t["W1"], C * D)

    n_store = 0
    for seq in range(NSEQ):
        for ti in range(1 if dump == "xfm1" else T):
            p.wait("sync", "S", p.val("S"))
            p.wait("sync", "V", p.val("V"))
            emit_xfm_tile(p, t, x_dram, seq, ti, t["xfm"])
        if dump in ("xfm", "xfm1"):
            _dump_and_finish("xfm")
            return nc

        p.wait("tensor", wk_ready[0], wk_ready[1])
        for c in range(C):
            p.wait("tensor", "S", p.val("S"))
            p.wait("tensor", "V", p.val("V"))
            tv = None
            for tc in range(4):
                dstp = t["pA"] if tc < 2 else t["pB"]
                off = 512 * (tc % 2)
                for cin in range(C):
                    tv = p.op("tensor", lambda e, c=c, tc=tc, cin=cin, dstp=dstp, off=off:
                              e.matmul(dstp[:, off:off + 512],
                                       t["W0"][:, D * cin + 128 * c: D * cin + 128 * c + 128],
                                       t["xfm"][:, SL * cin + 512 * tc: SL * cin + 512 * tc + 512],
                                       start=(cin == 0), stop=(cin == C - 1)), inc="T")
            p.wait("vector", "T", tv)
            rv = None
            for tc in range(4):
                srcp = t["pA"] if tc < 2 else t["pB"]
                off = 512 * (tc % 2)
                rv = p.op("vector", lambda e, tc=tc, srcp=srcp, off=off: e.reduce_max(
                    t["nmax4"][:, tc:tc + 1], srcp[:, off:off + 512],
                    axis=AX.X, negate=True), inc="V")
            p.wait("vector", "V", rv)
            rv2 = p.op("vector", lambda e: e.tensor_reduce(
                t["nmax1"][:], t["nmax4"][:], axis=AX.X, op=ALU.min), inc="V")
            p.wait("scalar", "V", rv2)
            p.op("scalar", lambda e: e.activation(
                out=t["kexp_f"][:, 0:1024], in_=t["pA"][:], func=AF.Exp,
                bias=t["nmax1"][:], scale=1.0, accum_out=t["zc2"][:, 0:1]))
            sv = p.op("scalar", lambda e: e.activation(
                out=t["kexp_f"][:, 1024:2048], in_=t["pB"][:], func=AF.Exp,
                bias=t["nmax1"][:], scale=1.0, accum_out=t["zc2"][:, 1:2]), inc="S")
            p.wait("vector", "S", sv)
            p.op("vector", lambda e: e.tensor_tensor(out=t["zc"][:], in0=t["zc2"][:, 0:1],
                                                     in1=t["zc2"][:, 1:2], op=ALU.add))
            p.op("vector", lambda e, c=c: e.reciprocal(t["rzk"][:, c:c + 1], t["zc"][:]))
            p.wait("tensor", "S", sv)
            for r in range(2):
                p.wait("tensor", "V", p.val("V"))
                tv = None
                for j in range(8):
                    tv = p.op("tensor", lambda e, r=r, j=j: e.transpose(
                        t["ptrb"][:, 128 * j:128 * j + 128],
                        t["kexp_f"][:, 1024 * r + 128 * j: 1024 * r + 128 * j + 128],
                        t["ident_bf"][:]), inc="T")
                p.wait("vector", "T", tv)
                dst = t["ktm"].rearrange("p (t w) -> p t w", w=D)[:, 8 * r:8 * r + 8,
                                                                 128 * c:128 * c + 128]
                src = t["ptrb"].rearrange("p (q w) -> p q w", w=128)
                p.op("vector", lambda e, dst=dst, src=src: e.tensor_copy(out=dst, in_=src),
                     inc="V")
        if dump == "ktm":
            _dump_and_finish("ktm")
            return nc

        p.wait("tensor", wv_ready[0], wv_ready[1])
        for ti in range(T):
            p.wait("tensor", "V", p.val("V"))
            p.wait("tensor", "S", p.val("S"))
            tv = None
            for half in range(2):
                for cin in range(C):
                    tv = p.op("tensor", lambda e, ti=ti, half=half, cin=cin: e.matmul(
                        t["pA"][:, 512 * half:512 * half + 512],
                        t["xfm"][:, SL * cin + 128 * ti: SL * cin + 128 * ti + 128],
                        t["W1"][:, D * cin + 512 * half: D * cin + 512 * half + 512],
                        start=(cin == 0), stop=(cin == C - 1)), inc="T")
            p.wait("vector", "T", tv)
            p.op("vector", lambda e, ti=ti: e.tensor_copy(
                out=t["vtm"][:, D * ti:D * ti + D], in_=t["pA"][:]), inc="V")
        if dump == "vtm":
            _dump_and_finish("vtm")
            return nc

        wq_ready = emit_load_weight(p, t, wq_d, t["W0"], C * D, ready=("T", p.val("T")))
        wo_ready = emit_load_weight(p, t, wo_d, t["W1"], C * D)

        for c in range(C):
            p.wait("tensor", "V", p.val("V"))
            p.wait("tensor", "S", p.val("S"))
            tv = None
            for half in range(2):
                for ti in range(T):
                    tv = p.op("tensor", lambda e, c=c, half=half, ti=ti: e.matmul(
                        t["pB"][:, 512 * half:512 * half + 512],
                        t["ktm"][:, D * ti + 128 * c: D * ti + 128 * c + 128],
                        t["vtm"][:, D * ti + 512 * half: D * ti + 512 * half + 512],
                        start=(ti == 0), stop=(ti == T - 1)), inc="T")
            p.wait("vector", "T", tv)
            p.op("vector", lambda e, c=c: e.tensor_scalar(
                out=t["ctx"][:, D * c:D * c + D], in0=t["pB"][:],
                scalar1=t["rzk"][:, c:c + 1], scalar2=None, op0=ALU.mult), inc="V")
        if dump == "ctx":
            _dump_and_finish("ctx")
            return nc
        p.wait("tensor", wq_ready[0], wq_ready[1])
        p.wait("tensor", wo_ready[0], wo_ready[1])

        for ti in range(T):
            p.wait("tensor", "S", p.val("S"))
            p.wait("tensor", "V", p.val("V"))
            tv = None
            for half in range(2):
                for cin in range(C):
                    tv = p.op("tensor", lambda e, ti=ti, half=half, cin=cin: e.matmul(
                        t["pA"][:, 512 * half:512 * half + 512],
                        t["xfm"][:, SL * cin + 128 * ti: SL * cin + 128 * ti + 128],
                        t["W0"][:, D * cin + 512 * half: D * cin + 512 * half + 512],
                        start=(cin == 0), stop=(cin == C - 1)), inc="T")
            p.wait("vector", "T", tv)
            rv = p.op("vector", lambda e: e.reduce_max(
                t["nmax1"][:], t["pA"][:], axis=AX.X, negate=True), inc="V")
            p.wait("scalar", "V", rv)
            sv = p.op("scalar", lambda e: e.activation(
                out=t["qexp"][:], in_=t["pA"][:], func=AF.Exp,
                bias=t["nmax1"][:], scale=1.0, accum_out=t["zc"][:]), inc="S")
            p.wait("vector", "S", sv)
            p.op("vector", lambda e: e.reciprocal(t["rzq"][:], t["zc"][:]))
            p.wait("tensor", "S", sv)
            p.wait("tensor", "V", p.val("V"))
            tv = None
            for c in range(C):
                tv = p.op("tensor", lambda e, c=c: e.transpose(
                    t["ptrb"][:, 128 * c:128 * c + 128],
                    t["qexp"][:, 128 * c:128 * c + 128], t["ident_bf"][:]), inc="T")
            p.wait("vector", "T", tv)
            qv = p.op("vector", lambda e: e.tensor_copy(out=t["qfm"][:], in_=t["ptrb"][:]),
                      inc="V")
            p.wait("tensor", "V", qv)
            tv = None
            for half in range(2):
                for c in range(C):
                    tv = p.op("tensor", lambda e, half=half, c=c: e.matmul(
                        t["pB"][:, 512 * half:512 * half + 512],
                        t["qfm"][:, 128 * c:128 * c + 128],
                        t["ctx"][:, D * c + 512 * half: D * c + 512 * half + 512],
                        start=(c == 0), stop=(c == C - 1)), inc="T")
            p.wait("vector", "T", tv)
            av = p.op("vector", lambda e: e.tensor_scalar(
                out=t["attnb"][:], in0=t["pB"][:], scalar1=t["rzq"][:],
                scalar2=float(D) ** -0.5, op0=ALU.mult, op1=ALU.mult), inc="V")
            p.wait("tensor", "V", av)
            tv = None
            for c in range(C):
                tv = p.op("tensor", lambda e, c=c: e.transpose(
                    t["ptrb"][:, 128 * c:128 * c + 128],
                    t["attnb"][:, 128 * c:128 * c + 128], t["ident_bf"][:]), inc="T")
            p.wait("vector", "T", tv)
            fv = p.op("vector", lambda e: e.tensor_copy(out=t["afm"][:], in_=t["ptrb"][:]),
                      inc="V")
            p.wait("tensor", "V", fv)
            tv = None
            for half in range(2):
                for c in range(C):
                    tv = p.op("tensor", lambda e, half=half, c=c: e.matmul(
                        t["pA"][:, 512 * half:512 * half + 512],
                        t["afm"][:, 128 * c:128 * c + 128],
                        t["W1"][:, D * c + 512 * half: D * c + 512 * half + 512],
                        start=(c == 0), stop=(c == C - 1)), inc="T")
            lv = p.op("sync", lambda e, seq=seq, ti=ti: e.dma_start(
                out=t["x_t"][:], in_=x_dram[seq * SL + 128 * ti: seq * SL + 128 * ti + 128, :]),
                inc="L", n=16)
            p.wait("vector", "T", tv)
            p.wait("vector", "L", lv)
            p.op("vector", lambda e: e.tensor_tensor(out=t["r1"][:], in0=t["pA"][:],
                                                     in1=t["bo_bc"][:], op=ALU.add))
            p.op("vector", lambda e: e.tensor_tensor(out=t["r1"][:], in0=t["r1"][:],
                                                     in1=t["x_t"][:], op=ALU.add), inc="V")
            emit_ln_stats(p, t, t["r1"][:], t["mu1"][:], t["rstd1"][:], D)
            p.op("vector", lambda e: e.tensor_scalar(
                out=t["xnew"][:], in0=t["r1"][:], scalar1=t["mu1"][:],
                scalar2=t["rstd1"][:], op0=ALU.subtract, op1=ALU.mult))
            p.op("vector", lambda e: e.tensor_tensor(out=t["xnew"][:], in0=t["xnew"][:],
                                                     in1=t["gam_bc"][:], op=ALU.mult))
            p.op("vector", lambda e: e.tensor_tensor(out=t["xnew"][:], in0=t["xnew"][:],
                                                     in1=t["bet_bc"][:], op=ALU.add))
            p.op("vector", lambda e: e.tensor_tensor(out=t["r1"][:], in0=t["xnew"][:],
                                                     in1=t["x_t"][:], op=ALU.add), inc="V")
            emit_ln_stats(p, t, t["r1"][:], t["mu2"][:], t["rstd2"][:], D)
            p.wait("vector", "O", p.val("O"))
            vv = p.op("vector", lambda e: e.tensor_scalar(
                out=t["xnew"][:], in0=t["r1"][:], scalar1=t["mu2"][:],
                scalar2=t["rstd2"][:], op0=ALU.subtract, op1=ALU.mult), inc="V")
            p.wait("gpsimd", "V", vv)
            p.op("gpsimd", lambda e, seq=seq, ti=ti: e.dma_start(
                out=y_dram[seq * SL + 128 * ti: seq * SL + 128 * ti + 128, :],
                in_=t["xnew"][:]), inc="O", n=16)
            n_store += 1
            p.wait("sync", "V", p.val("V"))
        if seq < NSEQ - 1:
            wk_ready = emit_load_weight(p, t, wk_d, t["W0"], C * D, ready=("T", p.val("T")))
            wv_ready = emit_load_weight(p, t, wv_d, t["W1"], C * D)

    p.wait("gpsimd", "O", 16 * n_store)
    p.finalize()
    return nc


def host_inputs_A(full):
    x = np.asarray(full["L0_attn_x"], dtype=np.float32).reshape(32 * SL, D)
    wq = _cm(np.asarray(full["L0_attn_Wq"], np.float32))
    wk = _cm(np.asarray(full["L0_attn_Wk"], np.float32))
    wv = _cm(np.asarray(full["L0_attn_Wv"], np.float32))
    wo = _cm(np.asarray(full["L0_attn_Wo"], np.float32))
    bo = _row(full["L0_attn_bo"])
    gam = _row(full["L0_attn_gamma"])
    bet = _row(full["L0_attn_beta"])
    maps = []
    for core in range(NCORES):
        xs = np.ascontiguousarray(x[core * NSEQ * SL:(core + 1) * NSEQ * SL])
        maps.append({"x": xs, "wq": wq, "wk": wk, "wv": wv, "wo": wo,
                     "bo": bo, "gam": gam, "bet": bet})
    return maps



def build_launch_B(dump=None):
    nc = bass.Bass("TRN2", target_bir_lowering=False)
    x_dram = nc.dram_tensor("x", [NSEQ * SL, D], F32, kind="ExternalInput").ap()
    y_dram = nc.dram_tensor("y", [NSEQ * SL, D], F32, kind="ExternalOutput").ap()
    dr = {}
    for pfx in ("q", "k", "v"):
        dr[pfx + "_nwT"] = nc.dram_tensor(pfx + "_nwT", [128, C * 15], F32, kind="ExternalInput").ap()
        dr[pfx + "_nb"] = nc.dram_tensor(pfx + "_nb", [1, 15], F32, kind="ExternalInput").ap()
        dr[pfx + "_w1"] = nc.dram_tensor(pfx + "_w1", [128, C * D], F32, kind="ExternalInput").ap()
        dr[pfx + "_b1"] = nc.dram_tensor(pfx + "_b1", [1, D], F32, kind="ExternalInput").ap()
        dr[pfx + "_w2"] = nc.dram_tensor(pfx + "_w2", [128, C * D], F32, kind="ExternalInput").ap()
        dr[pfx + "_b2"] = nc.dram_tensor(pfx + "_b2", [16, D], F32, kind="ExternalInput").ap()
    dbg_d = None
    if dump is not None:
        cfg = {"sel": (3 * T * 128, BF16), "kfm": (C * SL, BF16),
               "vtm": (T * D, BF16)}[dump]
        dbg_d = nc.dram_tensor("dbg", [16 if dump == "sel" else 128, cfg[0]], cfg[1],
                               kind="ExternalOutput").ap()

    t = {}
    A = nc.alloc_sbuf_tensor
    t["stage"] = A("stage", [128, 1024], F32)[:]
    t["xfm"] = A("xfm", [128, C * SL], BF16)[:]
    t["kfm"] = A("kfm", [128, C * SL], BF16)[:]
    t["vtm"] = A("vtm", [128, T * D], BF16)[:]
    t["W1"] = A("W1", [128, C * D], BF16)[:]
    t["W2"] = A("W2", [128, C * D], BF16)[:]
    t["nwT"] = A("nwT", [128, C * 15 * 3], F32)[:]
    t["nbbc"] = A("nbbc", [128, 15 * 3], F32)[:]
    t["b1bc"] = A("b1bc", [128, D], F32)[:]
    t["b2bf"] = A("b2bf", [16, D * 3], BF16)[:]
    t["Ebf"] = A("Ebf", [16, D], BF16)[:]
    t["seltr"] = A("seltr", [16, 3 * T * 128], BF16)[:]
    t["x_t"] = A("x_t", [128, D], F32)[:]
    t["xbf"] = A("xbf", [128, D], BF16)[:]
    t["xtr32"] = A("xtr32", [128, D], F32)[:]
    t["scoreb"] = A("scoreb", [128, 45], F32)[:]
    t["bits"] = A("bits", [128, 4], F32)[:]
    t["nbits"] = A("nbits", [128, 4], F32)[:]
    t["sig"] = A("sig", [128, 1], F32)[:]
    t["tmp1"] = A("tmp1", [128, 1], F32)[:]
    t["p2"] = A("p2", [128, 4], F32)[:]
    t["p3g"] = A("p3g", [128, 8], F32)[:]
    t["tmp8"] = A("tmp8", [128, 8], F32)[:]
    t["sel16"] = A("sel16", [128, 16], F32)[:]
    t["selbf"] = A("selbf", [128, 16], BF16)[:]
    t["hf32"] = A("hf32", [128, D], F32)[:]
    t["Hbf"] = A("Hbf", [128, D], BF16)[:]
    t["maskbf"] = A("maskbf", [128, D], BF16)[:]
    t["Hm"] = A("Hm", [128, D], BF16)[:]
    t["Hfm"] = A("Hfm", [128, D], BF16)[:]
    t["ffout"] = A("ffout", [128, D], BF16)[:]
    t["qfm_t"] = A("qfm_t", [128, D], BF16)[:]
    t["attn_tm"] = A("attn_tm", [128, SL], BF16)[:]
    t["attn_km"] = A("attn_km", [128, SL], BF16)[:]
    t["r1"] = A("r1", [128, D], F32)[:]
    t["xnew"] = A("xnew", [128, D], F32)[:]
    t["row"] = A("rowbuf", [1, D], F32)[:]
    t["ident_bf"] = A("ident_bf", [128, 128], BF16)[:]
    t["ident_f32"] = A("ident_f32", [128, 128], F32)[:]
    t["ones_f32"] = A("ones_f32", [1, 128], F32)[:]
    t["eps_col"] = A("eps_col", [128, 1], F32)[:]
    t["nmax4"] = A("nmax4", [128, 4], F32)[:]
    t["nmax1"] = A("nmax1", [128, 1], F32)[:]
    t["zc2"] = A("zc2", [128, 2], F32)[:]
    t["zc"] = A("zc", [128, 1], F32)[:]
    t["rza"] = A("rza", [128, 1], F32)[:]
    t["mu2"] = A("mu2", [128, 1], F32)[:]
    t["rstd2"] = A("rstd2", [128, 1], F32)[:]
    t["red1"] = A("red1", [128, 1], F32)[:]
    t["red2"] = A("red2", [128, 1], F32)[:]
    t["red3"] = A("red3", [128, 1], F32)[:]

    AP_ = nc.alloc_psum_tensor
    t["sqd"] = t["hf32"]
    t["pA"] = AP_("pA", [128, D], F32)[:]
    t["pB"] = AP_("pB", [128, D], F32)[:]
    t["ptrb"] = AP_("ptrb", [128, D], BF16)[:]
    t["pf32"] = AP_("pf32", [128, D], F32)[:]
    t["ptrk"] = AP_("ptrk", [128, D], BF16)[:]

    p = P(nc)
    for s in ("L", "Lw", "Lr", "O", "T", "V", "S", "G"):
        p.sem(s)

    emit_consts(p, t)
    mv = p.op("gpsimd", lambda e: e.memset(t["mask_bfm"][:], 0.0), inc="G")
    p.wait("gpsimd", "G", mv)
    p.op("gpsimd", lambda e: e.affine_select(out=t["mask_bfm"][:], in_=t["mask_bfm"][:],
                                             compare_op=ALU.is_ge, fill=-10000.0, base=0,
                                             pattern=[[-1, 128]], channel_multiplier=1), inc="G")
    mv = p.op("gpsimd", lambda e: e.memset(t["Ebf"][:], 1.0), inc="G")
    p.wait("gpsimd", "G", mv)
    p.op("gpsimd", lambda e: e.affine_select(out=t["Ebf"][:], in_=t["Ebf"][:],
                                             compare_op=ALU.is_ge, fill=0.0, base=0,
                                             pattern=[[1, D]], channel_multiplier=-64),
         inc="G")
    p.op("gpsimd", lambda e: e.affine_select(out=t["Ebf"][:], in_=t["Ebf"][:],
                                             compare_op=ALU.is_ge, fill=0.0, base=63,
                                             pattern=[[-1, D]], channel_multiplier=64),
         inc="G")
    gv = p.val("G")
    for eng in ("tensor", "scalar", "vector"):
        p.wait(eng, "G", gv)

    for i, pfx in enumerate(("q", "k", "v")):
        lv = p.op("sync", lambda e, pfx=pfx, i=i: e.dma_start(
            out=t["nwT"][:, C * 15 * i: C * 15 * (i + 1)], in_=dr[pfx + "_nwT"][:]),
            inc="Lw", n=16)
        p.wait("sync", "T", p.val("T"))
        lr = p.op("sync", lambda e, pfx=pfx: e.dma_start(out=t["row"][:, 0:15],
                                                         in_=dr[pfx + "_nb"][:]), inc="Lr", n=16)
        p.wait("tensor", "Lr", lr)
        p.wait("tensor", "Lw", lv)
        emit_bcast_row(p, t, t["row"][:, 0:15], t["nbbc"][:, 15 * i:15 * i + 15], 15)
        if t.get("_stage_last") is not None:
            p.wait("sync", "S", t["_stage_last"])
        lb = p.op("sync", lambda e, pfx=pfx: e.dma_start(out=t["stage"][0:16, :],
                                                         in_=dr[pfx + "_b2"][:]), inc="Lw", n=16)
        p.wait("scalar", "Lw", lb)
        t["_stage_last"] = p.op("scalar", lambda e, i=i: e.copy(
            out=t["b2bf"][:, D * i:D * (i + 1)], in_=t["stage"][0:16, :]), inc="S")

    def fff_scores_routing(pidx, ti):
        off = 128 * pidx
        sc = t["scoreb"][:, 15 * pidx:15 * pidx + 15]
        p.wait("tensor", "V", p.val("V"))
        tv = None
        for cin in range(C):
            tv = p.op("tensor", lambda e, cin=cin, off=off, pidx=pidx: e.matmul(
                t["pA"][:, off:off + 15],
                t["xtr32"][:, 128 * cin:128 * cin + 128],
                t["nwT"][:, C * 15 * pidx + 15 * cin: C * 15 * pidx + 15 * cin + 15],
                start=(cin == 0), stop=(cin == C - 1)), inc="T")
        p.wait("vector", "T", tv)
        p.op("vector", lambda e, sc=sc, off=off, pidx=pidx: e.tensor_tensor(
            out=sc, in0=t["pA"][:, off:off + 15],
            in1=t["nbbc"][:, 15 * pidx:15 * pidx + 15], op=ALU.add))
        b = t["bits"]; nb = t["nbits"]
        p.op("vector", lambda e, sc=sc: e.tensor_scalar(
            out=b[:, 0:1], in0=sc[:, 0:1], scalar1=0.0, scalar2=None, op0=ALU.is_ge))
        p.op("vector", lambda e, sc=sc: e.tensor_tensor(
            out=t["tmp1"][:], in0=sc[:, 2:3], in1=sc[:, 1:2], op=ALU.subtract))
        p.op("vector", lambda e, sc=sc: e.scalar_tensor_tensor(
            out=t["sig"][:], in0=t["tmp1"][:], scalar=b[:, 0:1], in1=sc[:, 1:2],
            op0=ALU.mult, op1=ALU.add))
        p.op("vector", lambda e: e.tensor_scalar(
            out=b[:, 1:2], in0=t["sig"][:], scalar1=0.0, scalar2=None, op0=ALU.is_ge))
        p.op("vector", lambda e: e.tensor_scalar(
            out=nb[:, 0:2], in0=b[:, 0:2], scalar1=-1.0, scalar2=1.0,
            op0=ALU.mult, op1=ALU.add))
        p.op("vector", lambda e: e.tensor_tensor(out=t["p2"][:, 0:1], in0=nb[:, 0:1],
                                                 in1=nb[:, 1:2], op=ALU.mult))
        p.op("vector", lambda e: e.tensor_tensor(out=t["p2"][:, 1:2], in0=nb[:, 0:1],
                                                 in1=b[:, 1:2], op=ALU.mult))
        p.op("vector", lambda e: e.tensor_tensor(out=t["p2"][:, 2:3], in0=b[:, 0:1],
                                                 in1=nb[:, 1:2], op=ALU.mult))
        p.op("vector", lambda e: e.tensor_tensor(out=t["p2"][:, 3:4], in0=b[:, 0:1],
                                                 in1=b[:, 1:2], op=ALU.mult))
        p.op("vector", lambda e, sc=sc: e.tensor_tensor(
            out=t["tmp8"][:, 0:4], in0=t["p2"][:], in1=sc[:, 3:7], op=ALU.mult))
        p.op("vector", lambda e: e.reduce_sum(t["sig"][:], t["tmp8"][:, 0:4], axis=AX.X))
        p.op("vector", lambda e: e.tensor_scalar(
            out=b[:, 2:3], in0=t["sig"][:], scalar1=0.0, scalar2=None, op0=ALU.is_ge))
        p.op("vector", lambda e: e.tensor_scalar(
            out=nb[:, 2:3], in0=b[:, 2:3], scalar1=-1.0, scalar2=1.0,
            op0=ALU.mult, op1=ALU.add))
        p.op("vector", lambda e: e.tensor_scalar(
            out=t["p3g"][:, 0:4], in0=t["p2"][:], scalar1=nb[:, 2:3], scalar2=None,
            op0=ALU.mult))
        p.op("vector", lambda e: e.tensor_scalar(
            out=t["p3g"][:, 4:8], in0=t["p2"][:], scalar1=b[:, 2:3], scalar2=None,
            op0=ALU.mult))
        sv1 = sc[:, 7:15].rearrange("p (a b) -> p a b", b=2)[:, :, 0:1]
        sv2 = sc[:, 7:15].rearrange("p (a b) -> p a b", b=2)[:, :, 1:2]
        p.op("vector", lambda e, sv1=sv1: e.tensor_tensor(
            out=t["tmp8"][:, 0:4].rearrange("p (a b) -> p a b", b=1), in0=t["p3g"][:, 0:4].rearrange("p (a b) -> p a b", b=1),
            in1=sv1, op=ALU.mult))
        p.op("vector", lambda e, sv2=sv2: e.tensor_tensor(
            out=t["tmp8"][:, 4:8].rearrange("p (a b) -> p a b", b=1), in0=t["p3g"][:, 4:8].rearrange("p (a b) -> p a b", b=1),
            in1=sv2, op=ALU.mult))
        p.op("vector", lambda e: e.reduce_sum(t["sig"][:], t["tmp8"][:], axis=AX.X))
        p.op("vector", lambda e: e.tensor_scalar(
            out=b[:, 3:4], in0=t["sig"][:], scalar1=0.0, scalar2=None, op0=ALU.is_ge))
        p.op("vector", lambda e: e.tensor_scalar(
            out=nb[:, 3:4], in0=b[:, 3:4], scalar1=-1.0, scalar2=1.0,
            op0=ALU.mult, op1=ALU.add))
        p.op("vector", lambda e: e.tensor_scalar(
            out=t["sel16"][:, 0:8], in0=t["p3g"][:], scalar1=nb[:, 3:4], scalar2=None,
            op0=ALU.mult))
        p.op("vector", lambda e: e.tensor_scalar(
            out=t["sel16"][:, 8:16], in0=t["p3g"][:], scalar1=b[:, 3:4], scalar2=None,
            op0=ALU.mult))
        vv = p.op("vector", lambda e: e.tensor_copy(out=t["selbf"][:], in_=t["sel16"][:]))
        p.wait("tensor", "V", vv)
        p.wait("tensor", "V", p.val("V"))
        tv = p.op("tensor", lambda e: e.transpose(
            t["ptrb"][0:16, 0:128], t["selbf"][:], t["ident_bf"][:]), inc="T")
        p.wait("vector", "T", tv)
        dstc = (pidx * T + ti) * 128
        p.op("vector", lambda e, dstc=dstc: e.tensor_copy(
            out=t["seltr"][:, dstc:dstc + 128], in_=t["ptrb"][0:16, 0:128]), inc="V")

    def fff_apply(pidx, ti, out_kind):
        selc = (pidx * T + ti) * 128
        p.wait("tensor", "V", p.val("V"))
        tv = None
        for half in range(2):
            for cin in range(C):
                tv = p.op("tensor", lambda e, ti=ti, half=half, cin=cin: e.matmul(
                    t["pA"][:, 512 * half:512 * half + 512],
                    t["xfm"][:, SL * cin + 128 * ti: SL * cin + 128 * ti + 128],
                    t["W1"][:, D * cin + 512 * half: D * cin + 512 * half + 512],
                    start=(cin == 0), stop=(cin == C - 1)), inc="T")
        mv = None
        for half in range(2):
            mv = p.op("tensor", lambda e, half=half, selc=selc: e.matmul(
                t["pB"][:, 512 * half:512 * half + 512],
                t["seltr"][:, selc:selc + 128],
                t["Ebf"][:, 512 * half:512 * half + 512],
                start=True, stop=True), inc="T")
        p.wait("vector", "T", mv)
        p.op("vector", lambda e: e.tensor_tensor(out=t["hf32"][:], in0=t["pA"][:],
                                                 in1=t["b1bc"][:], op=ALU.add))
        p.op("vector", lambda e: e.tensor_scalar(out=t["Hbf"][:], in0=t["hf32"][:],
                                                 scalar1=0.0, scalar2=None, op0=ALU.max))
        p.op("vector", lambda e: e.tensor_copy(out=t["maskbf"][:], in_=t["pB"][:]))
        hv = p.op("vector", lambda e: e.tensor_tensor(out=t["Hm"][:], in0=t["Hbf"][:],
                                                      in1=t["maskbf"][:], op=ALU.mult),
                  inc="V")
        p.wait("tensor", "V", hv)
        tv = None
        for c in range(C):
            tv = p.op("tensor", lambda e, c=c: e.transpose(
                t["ptrb"][:, 128 * c:128 * c + 128],
                t["Hm"][:, 128 * c:128 * c + 128], t["ident_bf"][:]), inc="T")
        p.wait("vector", "T", tv)
        fv = p.op("vector", lambda e: e.tensor_copy(out=t["Hfm"][:], in_=t["ptrb"][:]),
                  inc="V")
        p.wait("tensor", "V", fv)
        tv = None
        for half in range(2):
            for c in range(C):
                p.op("tensor", lambda e, half=half, c=c: e.matmul(
                    t["pA"][:, 512 * half:512 * half + 512],
                    t["Hfm"][:, 128 * c:128 * c + 128],
                    t["W2"][:, D * c + 512 * half: D * c + 512 * half + 512],
                    start=(c == 0), stop=False), inc="T")
            tv = p.op("tensor", lambda e, half=half, selc=selc, pidx=pidx: e.matmul(
                t["pA"][:, 512 * half:512 * half + 512],
                t["seltr"][:, selc:selc + 128],
                t["b2bf"][:, D * pidx + 512 * half: D * pidx + 512 * half + 512],
                start=False, stop=True), inc="T")
        p.wait("vector", "T", tv)
        if out_kind == "vtm":
            p.op("vector", lambda e, ti=ti: e.tensor_copy(
                out=t["vtm"][:, D * ti:D * ti + D], in_=t["pA"][:]), inc="V")
        elif out_kind == "q":
            p.op("vector", lambda e: e.tensor_copy(out=t["ffout"][:], in_=t["pA"][:]),
                 inc="V")
        else:
            ev = p.op("vector", lambda e: e.tensor_copy(out=t["ffout"][:], in_=t["pA"][:]),
                      inc="V")
            p.wait("tensor", "V", ev)
            tv = None
            for c in range(C):
                tv = p.op("tensor", lambda e, c=c: e.transpose(
                    t["ptrb"][:, 128 * c:128 * c + 128],
                    t["ffout"][:, 128 * c:128 * c + 128], t["ident_bf"][:]), inc="T")
            p.wait("vector", "T", tv)
            dst = t["kfm"].rearrange("p (c w) -> p c w", w=SL)[:, :, 128 * ti:128 * ti + 128]
            src = t["ptrb"].rearrange("p (c q) -> p c q", q=128)
            p.op("vector", lambda e, dst=dst, src=src: e.tensor_copy(out=dst, in_=src),
                 inc="V")

    def load_fff_weights(pidx, pfx):
        tval = p.val("T")
        rw1 = emit_load_weight(p, t, dr[pfx + "_w1"], t["W1"], C * D,
                               ready=("T", tval))
        rw2 = emit_load_weight(p, t, dr[pfx + "_w2"], t["W2"], C * D,
                               ready=("T", tval))
        p.wait("sync", "T", p.val("T"))
        lr = p.op("sync", lambda e, pfx=pfx: e.dma_start(out=t["row"][:], in_=dr[pfx + "_b1"][:]),
                  inc="Lr", n=16)
        p.wait("tensor", "Lr", lr)
        p.wait("tensor", "S", p.val("S"))
        rb = emit_bcast_row(p, t, t["row"], t["b1bc"], D)
        return rw1, rw2, rb

    p.wait("tensor", "Lw", p.val("Lw"))
    p.wait("tensor", "S", p.val("S"))

    n_store = 0
    for seq in range(NSEQ):
        for ti in range(T):
            p.wait("sync", "S", p.val("S"))
            p.wait("sync", "V", p.val("V"))
            emit_xfm_tile(p, t, x_dram, seq, ti, t["xfm"], also_f32=t["xtr32"][:])
            for pidx in range(3):
                fff_scores_routing(pidx, ti)
        if dump == "sel":
            _dump_b(p, t, dbg_d, "seltr")
            return nc

        rw1, rw2, rb = load_fff_weights(1, "k")
        p.wait("tensor", rw1[0], rw1[1])
        p.wait("tensor", rw2[0], rw2[1])
        p.wait("vector", "V", rb)
        for ti in range(T):
            fff_apply(1, ti, "kfm")
        if dump == "kfm":
            _dump_b(p, t, dbg_d, "kfm")
            return nc

        rw1, rw2, rb = load_fff_weights(2, "v")
        p.wait("tensor", rw1[0], rw1[1])
        p.wait("tensor", rw2[0], rw2[1])
        p.wait("vector", "V", rb)
        for ti in range(T):
            fff_apply(2, ti, "vtm")
        if dump == "vtm":
            _dump_b(p, t, dbg_d, "vtm")
            return nc

        rw1, rw2, rb = load_fff_weights(0, "q")
        p.wait("tensor", rw1[0], rw1[1])
        p.wait("tensor", rw2[0], rw2[1])
        p.wait("vector", "V", rb)
        for ti in range(T):
            fff_apply(0, ti, "q")
            p.wait("tensor", "V", p.val("V"))
            tv = None
            for c in range(C):
                tv = p.op("tensor", lambda e, c=c: e.transpose(
                    t["ptrb"][:, 128 * c:128 * c + 128],
                    t["ffout"][:, 128 * c: 128 * c + 128],
                    t["ident_bf"][:]), inc="T")
            p.wait("vector", "T", tv)
            qv = p.op("vector", lambda e: e.tensor_copy(out=t["qfm_t"][:], in_=t["ptrb"][:]),
                      inc="V")
            p.wait("tensor", "V", qv)
            p.wait("tensor", "S", p.val("S"))
            tv = None
            for tc in range(4):
                dstp = t["pA"] if tc < 2 else t["pB"]
                off = 512 * (tc % 2)
                for c in range(C):
                    tv = p.op("tensor", lambda e, tc=tc, c=c, dstp=dstp, off=off: e.matmul(
                        dstp[:, off:off + 512],
                        t["qfm_t"][:, 128 * c:128 * c + 128],
                        t["kfm"][:, SL * c + 512 * tc: SL * c + 512 * tc + 512],
                        start=(c == 0), stop=(c == C - 1)), inc="T")
            p.wait("vector", "T", tv)
            rv = None
            for tc in range(4):
                srcp = t["pA"] if tc < 2 else t["pB"]
                off = 512 * (tc % 2)
                rv = p.op("vector", lambda e, tc=tc, srcp=srcp, off=off: e.reduce_max(
                    t["nmax4"][:, tc:tc + 1], srcp[:, off:off + 512],
                    axis=AX.X, negate=True), inc="V")
            p.op("vector", lambda e: e.tensor_reduce(
                t["nmax1"][:], t["nmax4"][:], axis=AX.X, op=ALU.min))
            rv2 = p.op("vector", lambda e: e.tensor_scalar(
                out=t["nmax1"][:], in0=t["nmax1"][:], scalar1=float(D) ** -0.5,
                scalar2=None, op0=ALU.mult), inc="V")
            p.wait("scalar", "V", rv2)
            p.op("scalar", lambda e: e.activation(
                out=t["attn_tm"][:, 0:1024], in_=t["pA"][:], func=AF.Exp,
                bias=t["nmax1"][:], scale=float(D) ** -0.5, accum_out=t["zc2"][:, 0:1]))
            sv = p.op("scalar", lambda e: e.activation(
                out=t["attn_tm"][:, 1024:2048], in_=t["pB"][:], func=AF.Exp,
                bias=t["nmax1"][:], scale=float(D) ** -0.5, accum_out=t["zc2"][:, 1:2]),
                inc="S")
            p.wait("vector", "S", sv)
            p.op("vector", lambda e: e.tensor_tensor(out=t["zc"][:], in0=t["zc2"][:, 0:1],
                                                     in1=t["zc2"][:, 1:2], op=ALU.add))
            p.op("vector", lambda e: e.reciprocal(t["rza"][:], t["zc"][:]))
            p.wait("tensor", "S", sv)
            av = None
            for r in range(2):
                p.wait("tensor", "V", p.val("V"))
                tv = None
                for j in range(8):
                    tv = p.op("tensor", lambda e, r=r, j=j: e.transpose(
                        t["ptrk"][:, 128 * j:128 * j + 128],
                        t["attn_tm"][:, 1024 * r + 128 * j: 1024 * r + 128 * j + 128],
                        t["ident_bf"][:]), inc="T")
                p.wait("vector", "T", tv)
                av = p.op("vector", lambda e, r=r: e.tensor_copy(
                    out=t["attn_km"][:, 1024 * r:1024 * r + 1024], in_=t["ptrk"][:]),
                    inc="V")
            p.wait("tensor", "V", av)
            tv = None
            for half in range(2):
                for kt in range(T):
                    tv = p.op("tensor", lambda e, half=half, kt=kt: e.matmul(
                        t["pA"][:, 512 * half:512 * half + 512],
                        t["attn_km"][:, 128 * kt:128 * kt + 128],
                        t["vtm"][:, D * kt + 512 * half: D * kt + 512 * half + 512],
                        start=(kt == 0), stop=(kt == T - 1)), inc="T")
            lv = p.op("sync", lambda e, seq=seq, ti=ti: e.dma_start(
                out=t["x_t"][:], in_=x_dram[seq * SL + 128 * ti: seq * SL + 128 * ti + 128, :]),
                inc="L", n=16)
            p.wait("vector", "T", tv)
            p.wait("vector", "L", lv)
            p.op("vector", lambda e: e.tensor_scalar(
                out=t["r1"][:], in0=t["pA"][:], scalar1=t["rza"][:], scalar2=None,
                op0=ALU.mult))
            p.op("vector", lambda e: e.tensor_tensor(out=t["r1"][:], in0=t["r1"][:],
                                                     in1=t["x_t"][:], op=ALU.add), inc="V")
            emit_ln_stats(p, t, t["r1"][:], t["mu2"][:], t["rstd2"][:], D)
            p.wait("vector", "O", p.val("O"))
            vv = p.op("vector", lambda e: e.tensor_scalar(
                out=t["xnew"][:], in0=t["r1"][:], scalar1=t["mu2"][:],
                scalar2=t["rstd2"][:], op0=ALU.subtract, op1=ALU.mult), inc="V")
            p.wait("gpsimd", "V", vv)
            p.op("gpsimd", lambda e, seq=seq, ti=ti: e.dma_start(
                out=y_dram[seq * SL + 128 * ti: seq * SL + 128 * ti + 128, :],
                in_=t["xnew"][:]), inc="O", n=16)
            n_store += 1
            p.wait("sync", "V", p.val("V"))

    p.wait("gpsimd", "O", 16 * n_store)
    p.finalize()
    return nc


def _dump_b(p, t, dbg_d, key):
    p.barrier(engines=("tensor", "vector", "scalar", "gpsimd"))
    p.wait("gpsimd", "V", p.val("V"))
    p.wait("gpsimd", "S", p.val("S"))
    p.op("gpsimd", lambda e: e.dma_start(out=dbg_d[:], in_=t[key]), inc="O", n=16)
    p.wait("gpsimd", "O", p.val("O"))
    p.finalize()


def host_inputs_B(full, x1):
    maps_common = {}
    for pfx in ("q", "k", "v"):
        nw = np.asarray(full[f"L0_ffn_{pfx}_node_w"], np.float32)
        nb = np.asarray(full[f"L0_ffn_{pfx}_node_b"], np.float32)
        w1 = np.asarray(full[f"L0_ffn_{pfx}_w1"], np.float32)
        b1 = np.asarray(full[f"L0_ffn_{pfx}_b1"], np.float32)
        w2 = np.asarray(full[f"L0_ffn_{pfx}_w2"], np.float32)
        b2 = np.asarray(full[f"L0_ffn_{pfx}_b2"], np.float32)
        w1h = np.concatenate([w1[LEAF_OF_H[h]] for h in range(16)], axis=1)
        b1h = np.concatenate([b1[LEAF_OF_H[h]] for h in range(16)])
        w2h = np.concatenate([w2[LEAF_OF_H[h]] for h in range(16)], axis=0)
        b2h = np.stack([b2[LEAF_OF_H[h]] for h in range(16)], axis=0)
        maps_common[pfx + "_nwT"] = _cm(np.ascontiguousarray(nw.T))
        maps_common[pfx + "_nb"] = _row(nb)
        maps_common[pfx + "_w1"] = _cm(w1h)
        maps_common[pfx + "_b1"] = _row(b1h)
        maps_common[pfx + "_w2"] = _cm(w2h)
        maps_common[pfx + "_b2"] = np.ascontiguousarray(b2h)
    maps = []
    for core in range(NCORES):
        m = dict(maps_common)
        m["x"] = np.ascontiguousarray(x1[core * NSEQ * SL:(core + 1) * NSEQ * SL])
        maps.append(m)
    return maps



def build_launch_C(dump=None):
    nc = bass.Bass("TRN2", target_bir_lowering=False)
    x_dram = nc.dram_tensor("x", [NSEQ * SL, D], F32, kind="ExternalInput").ap()
    y_dram = nc.dram_tensor("y", [NSEQ * SL, D], F32, kind="ExternalOutput").ap()
    wq_d = nc.dram_tensor("wqT", [128, C * D], F32, kind="ExternalInput").ap()
    wk_d = nc.dram_tensor("wkT", [128, C * D], F32, kind="ExternalInput").ap()
    wv_d = nc.dram_tensor("wvT", [128, C * D], F32, kind="ExternalInput").ap()
    wo_d = nc.dram_tensor("woT", [128, C * D], F32, kind="ExternalInput").ap()
    SCALE = 0.125

    t = {}
    A = nc.alloc_sbuf_tensor
    t["stage"] = A("stage", [128, 2048], F32)[:]
    t["W"] = A("W", [128, C * D], BF16)[:]
    t["xfm"] = A("xfm", [128, C * SL], BF16)[:]
    t["qfm"] = A("qfm", [128, C * SL], BF16)[:]
    t["kfm"] = A("kfm", [128, C * SL], BF16)[:]
    t["vtm"] = A("vtm", [128, T * D], BF16)[:]
    t["attn_tm"] = A("attn_tm", [128, SL], BF16)[:]
    t["attn_km"] = A("attn_km", [128, SL], BF16)[:]
    t["out_sb"] = A("out_sb", [128, D], BF16)[:]
    t["ofm"] = A("ofm", [128, D], BF16)[:]
    t["x_t"] = A("x_t", [128, D], F32)[:]
    t["xbf"] = A("xbf", [128, D], BF16)[:]
    t["r1"] = A("r1", [128, D], F32)[:]
    t["sqd"] = A("sqd", [128, D], F32)[:]
    t["xnew"] = A("xnew", [128, D], F32)[:]
    t["mask_bfm"] = A("mask_bfm", [128, 128], BF16)[:]
    t["ident_bf"] = A("ident_bf", [128, 128], BF16)[:]
    t["ident_f32"] = A("ident_f32", [128, 128], F32)[:]
    t["ones_f32"] = A("ones_f32", [1, 128], F32)[:]
    t["eps_col"] = A("eps_col", [128, 1], F32)[:]
    t["nmax4"] = A("nmax4", [128, 4], F32)[:]
    t["nmax1"] = A("nmax1", [128, 1], F32)[:]
    t["zc4"] = A("zc4", [128, 4], F32)[:]
    t["zc"] = A("zc", [128, 1], F32)[:]
    t["rza"] = A("rza", [128, 1], F32)[:]
    t["mu2"] = A("mu2", [128, 1], F32)[:]
    t["rstd2"] = A("rstd2", [128, 1], F32)[:]
    t["red1"] = A("red1", [128, 1], F32)[:]
    t["red2"] = A("red2", [128, 1], F32)[:]
    t["red3"] = A("red3", [128, 1], F32)[:]

    AP_ = nc.alloc_psum_tensor
    t["pA"] = AP_("pA", [128, D], F32)[:]
    t["pB"] = AP_("pB", [128, D], F32)[:]
    t["pO"] = AP_("pO", [128, D], F32)[:]
    t["ptrk"] = AP_("ptrk", [128, D], BF16)[:]
    t["ptrb"] = AP_("ptrb", [128, D], BF16)[:]

    p = P(nc)
    for s in ("L", "Lw", "Lr", "O", "T", "V", "S", "G"):
        p.sem(s)

    emit_consts(p, t)
    mv = p.op("gpsimd", lambda e: e.memset(t["mask_sb"][:], 0.0), inc="G")
    p.wait("gpsimd", "G", mv)
    p.op("gpsimd", lambda e: e.affine_select(out=t["mask_sb"][:], in_=t["mask_sb"][:],
                                             compare_op=ALU.is_ge, fill=-10000.0, base=0,
                                             pattern=[[-1, 128]], channel_multiplier=1),
         inc="G")
    gv = p.val("G")
    for eng in ("tensor", "scalar", "vector"):
        p.wait(eng, "G", gv)

    def produce_fm(w_dram, dst, ready_T):
        rw = emit_load_weight(p, t, w_dram, t["W"], C * D, ready=("T", ready_T))
        p.wait("tensor", rw[0], rw[1])
        for jc in range(C):
            for tc in range(4):
                dstp = t["pA"] if (jc * 4 + tc) % 2 == 0 else t["pB"]
                p.wait("tensor", "V", p.val("V"))
                tv = None
                for cin in range(C):
                    tv = p.op("tensor", lambda e, jc=jc, tc=tc, cin=cin, dstp=dstp: e.matmul(
                        dstp[:, 0:512],
                        t["W"][:, D * cin + 128 * jc: D * cin + 128 * jc + 128],
                        t["xfm"][:, SL * cin + 512 * tc: SL * cin + 512 * tc + 512],
                        start=(cin == 0), stop=(cin == C - 1)), inc="T")
                p.wait("vector", "T", tv)
                p.op("vector", lambda e, jc=jc, tc=tc, dstp=dstp: e.tensor_copy(
                    out=dst[:, SL * jc + 512 * tc: SL * jc + 512 * tc + 512],
                    in_=dstp[:, 0:512]), inc="V")
        return p.val("T")

    def produce_vtm(w_dram, ready_T):
        rw = emit_load_weight(p, t, w_dram, t["W"], C * D, ready=("T", ready_T))
        p.wait("tensor", rw[0], rw[1])
        for ti in range(T):
            p.wait("tensor", "V", p.val("V"))
            tv = None
            for half in range(2):
                for cin in range(C):
                    tv = p.op("tensor", lambda e, ti=ti, half=half, cin=cin: e.matmul(
                        t["pA"][:, 512 * half:512 * half + 512],
                        t["xfm"][:, SL * cin + 128 * ti: SL * cin + 128 * ti + 128],
                        t["W"][:, D * cin + 512 * half: D * cin + 512 * half + 512],
                        start=(cin == 0), stop=(cin == C - 1)), inc="T")
            p.wait("vector", "T", tv)
            p.op("vector", lambda e, ti=ti: e.tensor_copy(
                out=t["vtm"][:, D * ti:D * ti + D], in_=t["pA"][:]), inc="V")
        return p.val("T")

    n_store = 0
    for seq in range(NSEQ):
        for ti in range(T):
            p.wait("sync", "S", p.val("S"))
            p.wait("sync", "V", p.val("V"))
            emit_xfm_tile(p, t, x_dram, seq, ti, t["xfm"])
        tT = produce_fm(wq_d, t["qfm"], p.val("T"))
        tT = produce_fm(wk_d, t["kfm"], tT)
        tT = produce_vtm(wv_d, tT)
        rwo = emit_load_weight(p, t, wo_d, t["W"], C * D, ready=("T", tT))

        for qt in range(T):
            nk = 128 * (qt + 1)
            nkc = (nk + 511) // 512
            for h in range(16):
                jc, hp = h // 2, 64 * (h % 2)
                p.wait("tensor", "S", p.val("S"))
                p.wait("tensor", "V", p.val("V"))
                tv = None
                for tc in range(nkc):
                    w = min(512, nk - 512 * tc)
                    dstp = t["pA"] if tc < 2 else t["pB"]
                    off = 512 * (tc % 2)
                    tv = p.op("tensor", lambda e, qt=qt, tc=tc, w=w, dstp=dstp, off=off,
                              jc=jc, hp=hp: e.matmul(
                        dstp[:, off:off + w],
                        t["qfm"][hp:hp + 64, SL * jc + 128 * qt: SL * jc + 128 * qt + 128],
                        t["kfm"][hp:hp + 64, SL * jc + 512 * tc: SL * jc + 512 * tc + w],
                        start=True, stop=True), inc="T")
                p.wait("vector", "T", tv)
                mtc = (128 * qt) // 512
                moff = 512 * (mtc % 2) + (128 * qt - 512 * mtc)
                mp = t["pA"] if mtc < 2 else t["pB"]
                p.op("vector", lambda e, mp=mp, moff=moff: e.tensor_tensor(
                    out=mp[:, moff:moff + 128], in0=mp[:, moff:moff + 128],
                    in1=t["mask_sb"][:], op=ALU.add))
                for tc in range(nkc):
                    w = min(512, nk - 512 * tc)
                    srcp = t["pA"] if tc < 2 else t["pB"]
                    off = 512 * (tc % 2)
                    p.op("vector", lambda e, tc=tc, w=w, srcp=srcp, off=off: e.reduce_max(
                        t["nmax4"][:, tc:tc + 1], srcp[:, off:off + w],
                        axis=AX.X, negate=True))
                p.op("vector", lambda e, nkc=nkc: e.tensor_reduce(
                    t["nmax1"][:], t["nmax4"][:, 0:nkc], axis=AX.X, op=ALU.min))
                rv2 = p.op("vector", lambda e: e.tensor_scalar(
                    out=t["nmax1"][:], in0=t["nmax1"][:], scalar1=SCALE, scalar2=None,
                    op0=ALU.mult), inc="V")
                p.wait("scalar", "V", rv2)
                sv = None
                for tc in range(nkc):
                    w = min(512, nk - 512 * tc)
                    srcp = t["pA"] if tc < 2 else t["pB"]
                    off = 512 * (tc % 2)
                    sv = p.op("scalar", lambda e, tc=tc, w=w, srcp=srcp, off=off: e.activation(
                        out=t["attn_tm"][:, 512 * tc:512 * tc + w], in_=srcp[:, off:off + w],
                        func=AF.Exp, bias=t["nmax1"][:], scale=SCALE,
                        accum_out=t["zc4"][:, tc:tc + 1]), inc="S")
                p.wait("vector", "S", sv)
                p.op("vector", lambda e, nkc=nkc: e.reduce_sum(
                    t["zc"][:], t["zc4"][:, 0:nkc], axis=AX.X))
                p.op("vector", lambda e: e.reciprocal(t["rza"][:], t["zc"][:]))
                p.wait("tensor", "S", sv)
                nb = qt + 1
                for r in range((nb + 7) // 8):
                    p.wait("tensor", "V", p.val("V"))
                    w8 = min(8, nb - 8 * r)
                    tv = None
                    for j in range(w8):
                        tv = p.op("tensor", lambda e, r=r, j=j: e.transpose(
                            t["ptrk"][:, 128 * j:128 * j + 128],
                            t["attn_tm"][:, 1024 * r + 128 * j: 1024 * r + 128 * j + 128],
                            t["ident_bf"][:]), inc="T")
                    p.wait("vector", "T", tv)
                    p.op("vector", lambda e, r=r, w8=w8: e.tensor_copy(
                        out=t["attn_km"][:, 1024 * r:1024 * r + 128 * w8],
                        in_=t["ptrk"][:, 0:128 * w8]), inc="V")
                p.wait("tensor", "V", p.val("V"))
                tv = None
                for kt in range(qt + 1):
                    tv = p.op("tensor", lambda e, kt=kt, qt=qt, h=h: e.matmul(
                        t["pO"][:, 64 * h:64 * h + 64],
                        t["attn_km"][:, 128 * kt:128 * kt + 128],
                        t["vtm"][:, D * kt + 64 * h: D * kt + 64 * h + 64],
                        start=(kt == 0), stop=(kt == qt)), inc="T")
                p.wait("vector", "T", tv)
                p.op("vector", lambda e, h=h: e.tensor_scalar(
                    out=t["out_sb"][:, 64 * h:64 * h + 64], in0=t["pO"][:, 64 * h:64 * h + 64],
                    scalar1=t["rza"][:], scalar2=None, op0=ALU.mult), inc="V")

            p.wait("tensor", "V", p.val("V"))
            tv = None
            for c in range(C):
                tv = p.op("tensor", lambda e, c=c: e.transpose(
                    t["ptrb"][:, 128 * c:128 * c + 128],
                    t["out_sb"][:, 128 * c:128 * c + 128], t["ident_bf"][:]), inc="T")
            p.wait("vector", "T", tv)
            fv = p.op("vector", lambda e: e.tensor_copy(out=t["ofm"][:], in_=t["ptrb"][:]),
                      inc="V")
            p.wait("tensor", "V", fv)
            if qt == 0:
                p.wait("tensor", rwo[0], rwo[1])
            tv = None
            for half in range(2):
                for c in range(C):
                    tv = p.op("tensor", lambda e, half=half, c=c: e.matmul(
                        t["pA"][:, 512 * half:512 * half + 512],
                        t["ofm"][:, 128 * c:128 * c + 128],
                        t["W"][:, D * c + 512 * half: D * c + 512 * half + 512],
                        start=(c == 0), stop=(c == C - 1)), inc="T")
            lv = p.op("sync", lambda e, seq=seq, qt=qt: e.dma_start(
                out=t["x_t"][:], in_=x_dram[seq * SL + 128 * qt: seq * SL + 128 * qt + 128, :]),
                inc="L", n=16)
            p.wait("vector", "T", tv)
            p.wait("vector", "L", lv)
            p.op("vector", lambda e: e.tensor_tensor(out=t["r1"][:], in0=t["pA"][:],
                                                     in1=t["x_t"][:], op=ALU.add), inc="V")
            emit_ln_stats(p, t, t["r1"][:], t["mu2"][:], t["rstd2"][:], D)
            p.wait("vector", "O", p.val("O"))
            vv = p.op("vector", lambda e: e.tensor_scalar(
                out=t["xnew"][:], in0=t["r1"][:], scalar1=t["mu2"][:],
                scalar2=t["rstd2"][:], op0=ALU.subtract, op1=ALU.mult), inc="V")
            p.wait("gpsimd", "V", vv)
            p.op("gpsimd", lambda e, seq=seq, qt=qt: e.dma_start(
                out=y_dram[seq * SL + 128 * qt: seq * SL + 128 * qt + 128, :],
                in_=t["xnew"][:]), inc="O", n=16)
            n_store += 1
            p.wait("sync", "V", p.val("V"))

    p.wait("gpsimd", "O", 16 * n_store)
    p.finalize()
    return nc


def host_inputs_C(full, x2):
    w_qkv = np.asarray(full["L1_attn_w_qkv"], np.float32)
    w_out = np.asarray(full["L1_attn_w_out"], np.float32)
    m_common = {
        "wqT": _cm(np.ascontiguousarray(w_qkv[0:1024].T)),
        "wkT": _cm(np.ascontiguousarray(w_qkv[1024:2048].T)),
        "wvT": _cm(np.ascontiguousarray(w_qkv[2048:3072].T)),
        "woT": _cm(np.ascontiguousarray(w_out.T)),
    }
    maps = []
    for core in range(NCORES):
        m = dict(m_common)
        m["x"] = np.ascontiguousarray(x2[core * NSEQ * SL:(core + 1) * NSEQ * SL])
        maps.append(m)
    return maps



D_FCS = [(0, 1024), (1024, 1024), (2048, 768)]

def build_launch_D(dump=None):
    nc = bass.Bass("TRN2", target_bir_lowering=False)
    x_dram = nc.dram_tensor("x", [NSEQ * SL, D], F32, kind="ExternalInput").ap()
    dbg_d = None
    if dump is not None:
        dbg_d = nc.dram_tensor("dbg", [128, T * D], F32, kind="ExternalOutput").ap()
    y_dram = nc.dram_tensor("y", [NSEQ * SL, D], F32, kind="ExternalOutput").ap()
    gw_d = nc.dram_tensor("gwT", [128, C * 16], F32, kind="ExternalInput").ap()
    sgw_d = nc.dram_tensor("sgwT", [128, C], F32, kind="ExternalInput").ap()
    FCS = D_FCS
    swg_d = [nc.dram_tensor(f"swgT{i}", [128, C * w], F32, kind="ExternalInput").ap()
             for i, (_, w) in enumerate(FCS)]
    swu_d = [nc.dram_tensor(f"swuT{i}", [128, C * w], F32, kind="ExternalInput").ap()
             for i, (_, w) in enumerate(FCS)]
    swd_d = [nc.dram_tensor(f"swdT{i}", [128, (w // 128) * D], F32, kind="ExternalInput").ap()
             for i, (_, w) in enumerate(FCS)]
    wgu_d = [nc.dram_tensor(f"wgu{e}", [128, C * D], F32, kind="ExternalInput").ap()
             for e in range(16)]
    wdn_d = [nc.dram_tensor(f"wdn{e}", [128, 4 * D], F32, kind="ExternalInput").ap()
             for e in range(16)]

    t = {}
    A = nc.alloc_sbuf_tensor
    t["stage"] = A("stage", [128, 2048], F32)[:]
    t["xfm"] = A("xfm", [128, C * SL], BF16)[:]
    t["acc"] = A("acc", [128, T * D], F32)[:]
    t["Wg"] = A("Wg", [128, C * 1024], BF16)[:]
    t["Wu"] = A("Wu", [128, C * 1024], BF16)[:]
    t["Wd"] = A("Wd", [128, 8 * D], BF16)[:]
    t["gwbf"] = A("gwbf", [128, C * 16], BF16)[:]
    t["sgwbf"] = A("sgwbf", [128, C], BF16)[:]
    t["exps"] = A("exps", [128, T * 16], F32)[:]
    t["combine"] = A("combine", [128, T * 16], F32)[:]
    t["sglog"] = A("sglog", [128, T], F32)[:]
    t["sgate"] = A("sgate", [128, T], F32)[:]
    t["gs_bf"] = A("gs_bf", [128, D], BF16)[:]
    t["U_bf"] = A("U_bf", [128, D], BF16)[:]
    t["Ufm"] = A("Ufm", [128, D], BF16)[:]
    t["x_t"] = A("x_t", [128, D], F32)[:]
    t["xbf"] = A("xbf", [128, D], BF16)[:]
    t["r1"] = A("r1", [128, D], F32)[:]
    t["sqd"] = A("sqd", [128, D], F32)[:]
    t["xnew"] = A("xnew", [128, D], F32)[:]
    t["m1"] = A("m1", [128, 1], F32)[:]
    t["m2"] = A("m2", [128, 1], F32)[:]
    t["mask16"] = A("mask16", [128, 16], F32)[:]
    t["pmsk"] = A("pmsk", [128, 16], F32)[:]
    t["s12"] = A("s12", [128, 1], F32)[:]
    t["rs12"] = A("rs12", [128, 1], F32)[:]
    t["ident_bf"] = A("ident_bf", [128, 128], BF16)[:]
    t["ident_f32"] = A("ident_f32", [128, 128], F32)[:]
    t["ones_f32"] = A("ones_f32", [1, 128], F32)[:]
    t["eps_col"] = A("eps_col", [128, 1], F32)[:]
    t["nmax1"] = A("nmax1", [128, 1], F32)[:]
    t["zc"] = A("zc", [128, 1], F32)[:]
    t["mu2"] = A("mu2", [128, 1], F32)[:]
    t["rstd2"] = A("rstd2", [128, 1], F32)[:]
    t["red1"] = A("red1", [128, 1], F32)[:]
    t["red2"] = A("red2", [128, 1], F32)[:]
    t["red3"] = A("red3", [128, 1], F32)[:]

    AP_ = nc.alloc_psum_tensor
    t["pA"] = AP_("pA", [128, D], F32)[:]
    t["pB"] = AP_("pB", [128, D], F32)[:]
    t["pO"] = AP_("pO", [128, D], F32)[:]
    t["ptrb"] = AP_("ptrb", [128, D], BF16)[:]

    p = P(nc)
    for s in ("L", "Lw", "Lr", "O", "T", "V", "S", "G"):
        p.sem(s)

    emit_consts(p, t)
    gv = p.val("G")
    for eng in ("tensor", "scalar", "vector"):
        p.wait(eng, "G", gv)

    lv = p.op("sync", lambda e: e.dma_start(out=t["stage"][:, 0:C * 16], in_=gw_d[:]),
              inc="Lw", n=16)
    p.wait("scalar", "Lw", lv)
    sv = p.op("scalar", lambda e: e.copy(out=t["gwbf"][:], in_=t["stage"][:, 0:C * 16]), inc="S")
    p.wait("sync", "S", sv)
    lv = p.op("sync", lambda e: e.dma_start(out=t["stage"][:, 0:C], in_=sgw_d[:]), inc="Lw", n=16)
    p.wait("scalar", "Lw", lv)
    t["_stage_last"] = p.op("scalar", lambda e: e.copy(out=t["sgwbf"][:], in_=t["stage"][:, 0:C]),
                            inc="S")
    p.wait("tensor", "S", p.val("S"))

    n_store = 0
    for seq in range(NSEQ):
        for ti in range(T):
            p.wait("sync", "S", p.val("S"))
            p.wait("sync", "V", p.val("V"))
            emit_xfm_tile(p, t, x_dram, seq, ti, t["xfm"])
            p.wait("tensor", "S", p.val("S"))
            p.wait("tensor", "V", p.val("V"))
            tv = None
            for cin in range(C):
                tv = p.op("tensor", lambda e, ti=ti, cin=cin: e.matmul(
                    t["pA"][:, 0:16],
                    t["xfm"][:, SL * cin + 128 * ti: SL * cin + 128 * ti + 128],
                    t["gwbf"][:, 16 * cin:16 * cin + 16],
                    start=(cin == 0), stop=(cin == C - 1)), inc="T")
            for cin in range(C):
                tv = p.op("tensor", lambda e, ti=ti, cin=cin: e.matmul(
                    t["pA"][:, 128:129],
                    t["xfm"][:, SL * cin + 128 * ti: SL * cin + 128 * ti + 128],
                    t["sgwbf"][:, cin:cin + 1],
                    start=(cin == 0), stop=(cin == C - 1)), inc="T")
            p.wait("vector", "T", tv)
            rv = p.op("vector", lambda e: e.reduce_max(
                t["nmax1"][:], t["pA"][:, 0:16], axis=AX.X, negate=True), inc="V")
            p.op("vector", lambda e, ti=ti: e.tensor_copy(
                out=t["sglog"][:, ti:ti + 1], in_=t["pA"][:, 128:129]))
            p.wait("scalar", "V", p.val("V"))
            p.op("scalar", lambda e, ti=ti: e.activation(
                out=t["exps"][:, 16 * ti:16 * ti + 16], in_=t["pA"][:, 0:16], func=AF.Exp,
                bias=t["nmax1"][:], scale=1.0), inc="S")
        p.wait("scalar", "V", p.val("V"))
        p.op("scalar", lambda e: e.activation(out=t["sgate"][:], in_=t["sglog"][:],
                                              func=AF.Sigmoid), inc="S")
        p.wait("vector", "S", p.val("S"))
        for ti in range(T):
            ex = t["exps"][:, 16 * ti:16 * ti + 16]
            cb = t["combine"][:, 16 * ti:16 * ti + 16]
            p.op("vector", lambda e, ex=ex: e.reduce_max(t["m1"][:], ex, axis=AX.X))
            p.op("vector", lambda e, ex=ex: e.tensor_scalar(
                out=t["mask16"][:], in0=ex, scalar1=t["m1"][:], scalar2=None, op0=ALU.is_ge))
            p.op("vector", lambda e, ex=ex: e.scalar_tensor_tensor(
                out=t["pmsk"][:], in0=t["mask16"][:], scalar=-1e30, in1=ex,
                op0=ALU.mult, op1=ALU.add))
            p.op("vector", lambda e: e.reduce_max(t["m2"][:], t["pmsk"][:], axis=AX.X))
            p.op("vector", lambda e, ex=ex: e.tensor_scalar(
                out=t["mask16"][:], in0=ex, scalar1=t["m2"][:], scalar2=None, op0=ALU.is_ge))
            p.op("vector", lambda e: e.tensor_tensor(
                out=t["s12"][:], in0=t["m1"][:], in1=t["m2"][:], op=ALU.add))
            p.op("vector", lambda e: e.reciprocal(t["rs12"][:], t["s12"][:]))
            p.op("vector", lambda e, ex=ex: e.tensor_tensor(
                out=t["pmsk"][:], in0=ex, in1=t["mask16"][:], op=ALU.mult))
            p.op("vector", lambda e, cb=cb: e.tensor_scalar(
                out=cb, in0=t["pmsk"][:], scalar1=t["rs12"][:], scalar2=None, op0=ALU.mult))

        for fci, (f0, wfc) in enumerate(FCS):
            tval = p.val("T")
            rg = emit_load_weight(p, t, swg_d[fci], t["Wg"], C * wfc, ready=("T", tval))
            ru = emit_load_weight(p, t, swu_d[fci], t["Wu"], C * wfc, ready=("T", tval))
            rd = emit_load_weight(p, t, swd_d[fci], t["Wd"], (wfc // 128) * D,
                                  ready=("T", tval))
            p.wait("tensor", rg[0], rg[1])
            p.wait("tensor", ru[0], ru[1])
            p.wait("tensor", rd[0], rd[1])
            for ti in range(T):
                p.wait("tensor", "S", p.val("S"))
                p.wait("tensor", "V", p.val("V"))
                tv = None
                for half in range(wfc // 512 + (1 if wfc % 512 else 0)):
                    w = min(512, wfc - 512 * half)
                    for cin in range(C):
                        tv = p.op("tensor", lambda e, ti=ti, half=half, w=w, cin=cin, wfc=wfc: e.matmul(
                            t["pA"][:, 512 * half:512 * half + w],
                            t["xfm"][:, SL * cin + 128 * ti: SL * cin + 128 * ti + 128],
                            t["Wg"][:, wfc * cin + 512 * half: wfc * cin + 512 * half + w],
                            start=(cin == 0), stop=(cin == C - 1)), inc="T")
                for half in range(wfc // 512 + (1 if wfc % 512 else 0)):
                    w = min(512, wfc - 512 * half)
                    for cin in range(C):
                        tv = p.op("tensor", lambda e, ti=ti, half=half, w=w, cin=cin, wfc=wfc: e.matmul(
                            t["pB"][:, 512 * half:512 * half + w],
                            t["xfm"][:, SL * cin + 128 * ti: SL * cin + 128 * ti + 128],
                            t["Wu"][:, wfc * cin + 512 * half: wfc * cin + 512 * half + w],
                            start=(cin == 0), stop=(cin == C - 1)), inc="T")
                p.wait("scalar", "T", tv)
                sv = p.op("scalar", lambda e, wfc=wfc: e.activation(
                    out=t["gs_bf"][:, 0:wfc], in_=t["pA"][:, 0:wfc], func=AF.Silu), inc="S")
                p.wait("vector", "S", sv)
                uv = p.op("vector", lambda e, wfc=wfc: e.tensor_tensor(
                    out=t["U_bf"][:, 0:wfc], in0=t["gs_bf"][:, 0:wfc],
                    in1=t["pB"][:, 0:wfc], op=ALU.mult), inc="V")
                p.wait("tensor", "V", uv)
                tv = None
                for c2 in range(wfc // 128):
                    tv = p.op("tensor", lambda e, c2=c2: e.transpose(
                        t["ptrb"][:, 128 * c2:128 * c2 + 128],
                        t["U_bf"][:, 128 * c2:128 * c2 + 128], t["ident_bf"][:]), inc="T")
                p.wait("vector", "T", tv)
                fv = p.op("vector", lambda e, wfc=wfc: e.tensor_copy(
                    out=t["Ufm"][:, 0:wfc], in_=t["ptrb"][:, 0:wfc]), inc="V")
                p.wait("tensor", "V", fv)
                tv = None
                for half in range(2):
                    for c2 in range(wfc // 128):
                        tv = p.op("tensor", lambda e, half=half, c2=c2, wfc=wfc: e.matmul(
                            t["pO"][:, 512 * half:512 * half + 512],
                            t["Ufm"][:, 128 * c2:128 * c2 + 128],
                            t["Wd"][:, D * c2 + 512 * half: D * c2 + 512 * half + 512],
                            start=(c2 == 0), stop=(c2 == wfc // 128 - 1)), inc="T")
                p.wait("vector", "T", tv)
                sgc = t["sgate"][:, ti:ti + 1]
                at = t["acc"][:, D * ti:D * ti + D]
                if fci == 0:
                    p.op("vector", lambda e, at=at, sgc=sgc: e.tensor_scalar(
                        out=at, in0=t["pO"][:], scalar1=sgc, scalar2=None, op0=ALU.mult),
                        inc="V")
                else:
                    p.op("vector", lambda e, at=at, sgc=sgc: e.scalar_tensor_tensor(
                        out=at, in0=t["pO"][:], scalar=sgc, in1=at,
                        op0=ALU.mult, op1=ALU.add), inc="V")
                if dump == f"fc{fci}t{ti}":
                    p.barrier(engines=("tensor", "vector", "scalar", "gpsimd"))
                    p.wait("gpsimd", "V", p.val("V"))
                    p.op("gpsimd", lambda e: e.dma_start(out=dbg_d[:], in_=t["acc"]), inc="O", n=16)
                    p.wait("gpsimd", "O", p.val("O"))
                    p.finalize()
                    return nc
            _tistop = None
            if dump is not None and dump.startswith("acct"):
                _tistop = int(dump[4:])
            if dump == f"accfc{fci}" or (fci == len(FCS) - 1 and _tistop is not None and False):
                p.barrier(engines=("tensor", "vector", "scalar", "gpsimd"))
                p.wait("gpsimd", "V", p.val("V"))
                p.op("gpsimd", lambda e: e.dma_start(out=dbg_d[:], in_=t["acc"]), inc="O", n=16)
                p.wait("gpsimd", "O", p.val("O"))
                p.finalize()
                return nc

        if dump == "acc1":
            p.barrier(engines=("tensor", "vector", "scalar", "gpsimd"))
            p.wait("gpsimd", "V", p.val("V"))
            p.op("gpsimd", lambda e: e.dma_start(out=dbg_d[:], in_=t["acc"]), inc="O", n=16)
            p.wait("gpsimd", "O", p.val("O"))
            p.finalize()
            return nc

        for ex in range(16):
            tval = p.val("T")
            rg = emit_load_weight(p, t, wgu_d[ex], t["Wg"], C * D, ready=("T", tval))
            rd = emit_load_weight(p, t, wdn_d[ex], t["Wd"], 4 * D, ready=("T", tval))
            p.wait("tensor", rg[0], rg[1])
            p.wait("tensor", rd[0], rd[1])
            for ti in range(T):
                p.wait("tensor", "S", p.val("S"))
                p.wait("tensor", "V", p.val("V"))
                tv = None
                for half in range(2):
                    for cin in range(C):
                        tv = p.op("tensor", lambda e, ti=ti, half=half, cin=cin: e.matmul(
                            t["pA"][:, 512 * half:512 * half + 512],
                            t["xfm"][:, SL * cin + 128 * ti: SL * cin + 128 * ti + 128],
                            t["Wg"][:, D * cin + 512 * half: D * cin + 512 * half + 512],
                            start=(cin == 0), stop=(cin == C - 1)), inc="T")
                p.wait("scalar", "T", tv)
                sv = p.op("scalar", lambda e: e.activation(
                    out=t["gs_bf"][:, 0:512], in_=t["pA"][:, 0:512], func=AF.Silu), inc="S")
                p.wait("vector", "S", sv)
                p.op("vector", lambda e: e.tensor_tensor(
                    out=t["U_bf"][:, 0:512], in0=t["gs_bf"][:, 0:512],
                    in1=t["pA"][:, 512:1024], op=ALU.mult))
                cbc = t["combine"][:, 16 * ti + ex:16 * ti + ex + 1]
                uv = p.op("vector", lambda e, cbc=cbc: e.tensor_scalar(
                    out=t["U_bf"][:, 0:512], in0=t["U_bf"][:, 0:512], scalar1=cbc,
                    scalar2=None, op0=ALU.mult), inc="V")
                p.wait("tensor", "V", uv)
                tv = None
                for c2 in range(4):
                    tv = p.op("tensor", lambda e, c2=c2: e.transpose(
                        t["ptrb"][:, 128 * c2:128 * c2 + 128],
                        t["U_bf"][:, 128 * c2:128 * c2 + 128], t["ident_bf"][:]), inc="T")
                p.wait("vector", "T", tv)
                fv = p.op("vector", lambda e: e.tensor_copy(
                    out=t["Ufm"][:, 0:512], in_=t["ptrb"][:, 0:512]), inc="V")
                p.wait("tensor", "V", fv)
                tv = None
                for half in range(2):
                    for c2 in range(4):
                        tv = p.op("tensor", lambda e, half=half, c2=c2: e.matmul(
                            t["pO"][:, 512 * half:512 * half + 512],
                            t["Ufm"][:, 128 * c2:128 * c2 + 128],
                            t["Wd"][:, D * c2 + 512 * half: D * c2 + 512 * half + 512],
                            start=(c2 == 0), stop=(c2 == 3)), inc="T")
                p.wait("vector", "T", tv)
                at = t["acc"][:, D * ti:D * ti + D]
                p.op("vector", lambda e, at=at: e.tensor_tensor(
                    out=at, in0=at, in1=t["pO"][:], op=ALU.add), inc="V")

        if dump == "acc2":
            p.barrier(engines=("tensor", "vector", "scalar", "gpsimd"))
            p.wait("gpsimd", "V", p.val("V"))
            p.op("gpsimd", lambda e: e.dma_start(out=dbg_d[:], in_=t["acc"]), inc="O", n=16)
            p.wait("gpsimd", "O", p.val("O"))
            p.finalize()
            return nc

        for ti in range(T):
            lv = p.op("sync", lambda e, seq=seq, ti=ti: e.dma_start(
                out=t["x_t"][:], in_=x_dram[seq * SL + 128 * ti: seq * SL + 128 * ti + 128, :]),
                inc="L", n=16)
            p.wait("vector", "L", lv)
            p.op("vector", lambda e, ti=ti: e.tensor_tensor(
                out=t["r1"][:], in0=t["acc"][:, D * ti:D * ti + D], in1=t["x_t"][:],
                op=ALU.add), inc="V")
            emit_ln_stats(p, t, t["r1"][:], t["mu2"][:], t["rstd2"][:], D)
            p.wait("vector", "O", p.val("O"))
            vv = p.op("vector", lambda e: e.tensor_scalar(
                out=t["xnew"][:], in0=t["r1"][:], scalar1=t["mu2"][:],
                scalar2=t["rstd2"][:], op0=ALU.subtract, op1=ALU.mult), inc="V")
            p.wait("gpsimd", "V", vv)
            p.op("gpsimd", lambda e, seq=seq, ti=ti: e.dma_start(
                out=y_dram[seq * SL + 128 * ti: seq * SL + 128 * ti + 128, :],
                in_=t["xnew"][:]), inc="O", n=16)
            n_store += 1
            p.wait("sync", "V", p.val("V"))

    p.wait("gpsimd", "O", 16 * n_store)
    p.finalize()
    return nc


def host_inputs_D(full, x3):
    gw = np.asarray(full["L1_ffn_gate_w"], np.float32)
    sgw = np.asarray(full["L1_ffn_shared_gate_w"], np.float32)
    swg = np.asarray(full["L1_ffn_shared_w_gate"], np.float32)
    swu = np.asarray(full["L1_ffn_shared_w_up"], np.float32)
    swd = np.asarray(full["L1_ffn_shared_w_down"], np.float32)
    wgu = np.asarray(full["L1_ffn_w_gate_up"], np.float32)
    wdn = np.asarray(full["L1_ffn_w_down"], np.float32)
    FCS = [(0, 1024), (1024, 1024), (2048, 768)]
    m_common = {"gwT": _cm(np.ascontiguousarray(gw.T)),
                "sgwT": _cm(np.ascontiguousarray(sgw.T))}
    swgT = np.ascontiguousarray(swg.T)
    swuT = np.ascontiguousarray(swu.T)
    swdT = np.ascontiguousarray(swd.T)
    for i, (f0, w) in enumerate(FCS):
        m_common[f"swgT{i}"] = _cm(np.ascontiguousarray(swgT[:, f0:f0 + w]))
        m_common[f"swuT{i}"] = _cm(np.ascontiguousarray(swuT[:, f0:f0 + w]))
        m_common[f"swdT{i}"] = _cm(np.ascontiguousarray(swdT[f0:f0 + w, :]))
    for e in range(16):
        m_common[f"wgu{e}"] = _cm(wgu[e])
        m_common[f"wdn{e}"] = _cm(wdn[e])
    maps = []
    for core in range(NCORES):
        m = dict(m_common)
        m["x"] = np.ascontiguousarray(x3[core * NSEQ * SL:(core + 1) * NSEQ * SL])
        maps.append(m)
    return maps



import contextlib


def build_fused():
    nc = bass.Bass("TRN2", target_bir_lowering=False)
    x_dram = nc.dram_tensor("x", [NSEQ * SL, D], F32, kind="ExternalInput").ap()
    y_dram = nc.dram_tensor("y", [NSEQ * SL, D], F32, kind="ExternalOutput").ap()
    wq_d = nc.dram_tensor("wq", [128, C * D], F32, kind="ExternalInput").ap()
    wk_d = nc.dram_tensor("wk", [128, C * D], F32, kind="ExternalInput").ap()
    wv_d = nc.dram_tensor("wv", [128, C * D], F32, kind="ExternalInput").ap()
    wo_d = nc.dram_tensor("wo", [128, C * D], F32, kind="ExternalInput").ap()
    bo_d = nc.dram_tensor("bo", [1, D], F32, kind="ExternalInput").ap()
    gam_d = nc.dram_tensor("gam", [1, D], F32, kind="ExternalInput").ap()
    bet_d = nc.dram_tensor("bet", [1, D], F32, kind="ExternalInput").ap()
    dr = {}
    for pfx in ("q", "k", "v"):
        dr[pfx + "_nwT"] = nc.dram_tensor(pfx + "_nwT", [128, C * 15], F32, kind="ExternalInput").ap()
        dr[pfx + "_nb"] = nc.dram_tensor(pfx + "_nb", [1, 15], F32, kind="ExternalInput").ap()
        dr[pfx + "_w1"] = nc.dram_tensor(pfx + "_w1", [128, C * D], F32, kind="ExternalInput").ap()
        dr[pfx + "_b1"] = nc.dram_tensor(pfx + "_b1", [1, D], F32, kind="ExternalInput").ap()
        dr[pfx + "_w2"] = nc.dram_tensor(pfx + "_w2", [128, C * D], F32, kind="ExternalInput").ap()
        dr[pfx + "_b2"] = nc.dram_tensor(pfx + "_b2", [16, D], F32, kind="ExternalInput").ap()
    wqT_d = nc.dram_tensor("wqT", [128, C * D], F32, kind="ExternalInput").ap()
    wkT_d = nc.dram_tensor("wkT", [128, C * D], F32, kind="ExternalInput").ap()
    wvT_d = nc.dram_tensor("wvT", [128, C * D], F32, kind="ExternalInput").ap()
    woT_d = nc.dram_tensor("woT", [128, C * D], F32, kind="ExternalInput").ap()
    gw_d = nc.dram_tensor("gwT", [128, C * 16], F32, kind="ExternalInput").ap()
    sgw_d = nc.dram_tensor("sgwT", [128, C], F32, kind="ExternalInput").ap()
    FCS = [(0, 1024), (1024, 1024), (2048, 768)]
    swg_d = [nc.dram_tensor(f"swgT{i}", [128, C * w], F32, kind="ExternalInput").ap()
             for i, (_, w) in enumerate(FCS)]
    swu_d = [nc.dram_tensor(f"swuT{i}", [128, C * w], F32, kind="ExternalInput").ap()
             for i, (_, w) in enumerate(FCS)]
    swd_d = [nc.dram_tensor(f"swdT{i}", [128, (w // 128) * D], F32, kind="ExternalInput").ap()
             for i, (_, w) in enumerate(FCS)]
    wgu_d = [nc.dram_tensor(f"wgu{e}", [128, C * D], F32, kind="ExternalInput").ap()
             for e in range(16)]
    wdn_d = [nc.dram_tensor(f"wdn{e}", [128, 4 * D], F32, kind="ExternalInput").ap()
             for e in range(16)]

    t = {}
    A = nc.alloc_sbuf_tensor
    t["x_bf"] = A("x_bf", [128, T * D], BF16)[:]
    t["xfm"] = A("xfm", [128, C * SL], BF16)[:]
    t["stage"] = A("stage", [128, 1024], F32)[:]
    t["W0"] = A("W0", [128, C * D], BF16)[:]
    t["W1"] = A("W1", [128, C * D], BF16)[:]
    t["row"] = A("rowbuf", [1, D], F32)[:]
    t["ident_bf"] = A("ident_bf", [128, 128], BF16)[:]
    t["ones_f32"] = A("ones_f32", [1, 128], F32)[:]
    t["eps_col"] = A("eps_col", [128, 1], F32)[:]
    t["shift48"] = A("shift48", [128, 1], F32)[:]
    t["shift24"] = A("shift24", [128, 1], F32)[:]
    t["mask_bfm"] = A("mask_bfm", [128, 128], BF16)[:]
    t["Ebf"] = A("Ebf", [16, D], BF16)[:]
    t["nmax4"] = A("nmax4", [128, 4], F32)[:]
    t["nmax1"] = A("nmax1", [128, 1], F32)[:]
    t["zc2"] = A("zc2", [128, 4], F32)[:]
    t["zc"] = A("zc", [128, 1], F32)[:]
    t["rzk"] = A("rzk", [128, C], F32)[:]
    t["rzq"] = A("rzq", [128, 1], F32)[:]
    t["rza"] = A("rza", [128, 1], F32)[:]
    t["mu1"] = A("mu1", [128, 1], F32)[:]
    t["rstd1"] = A("rstd1", [128, 1], F32)[:]
    t["mu2"] = A("mu2", [128, 1], F32)[:]
    t["rstd2"] = A("rstd2", [128, 1], F32)[:]
    t["red1"] = A("red1", [128, 1], F32)[:]
    t["red2"] = A("red2", [128, 1], F32)[:]
    t["red3"] = A("red3", [128, 1], F32)[:]
    t["zc4"] = t["zc2"]

    AP_ = nc.alloc_psum_tensor
    t["pA"] = AP_("pA", [128, D], F32)[:]
    t["pB"] = AP_("pB", [128, D], F32)[:]
    t["pO"] = AP_("pO", [128, D], F32)[:]
    t["ptrb"] = AP_("ptrb", [128, D], BF16)[:]
    t["ptrk"] = AP_("ptrk", [128, D], BF16)[:]

    p = P(nc)
    for s in ("L", "Lw", "Lr", "O", "T", "V", "S", "G", "Ls"):
        p.sem(s)

    FCS_W = [w for (_, w) in FCS]
    wspec = [("wq", wq_d, 128, C * D), ("wk", wk_d, 128, C * D),
             ("wv", wv_d, 128, C * D), ("wo", wo_d, 128, C * D),
             ("wqT", wqT_d, 128, C * D), ("wkT", wkT_d, 128, C * D),
             ("wvT", wvT_d, 128, C * D), ("woT", woT_d, 128, C * D),
             ("gwT", gw_d, 128, C * 16), ("sgwT", sgw_d, 128, C)]
    for pfx in ("q", "k", "v"):
        wspec += [(pfx + "_w1", dr[pfx + "_w1"], 128, C * D),
                  (pfx + "_w2", dr[pfx + "_w2"], 128, C * D),
                  (pfx + "_b2", dr[pfx + "_b2"], 16, D),
                  (pfx + "_nwT", dr[pfx + "_nwT"], 128, C * 15)]
    for i in range(3):
        wspec += [(f"swgT{i}", swg_d[i], 128, C * FCS_W[i]),
                  (f"swuT{i}", swu_d[i], 128, C * FCS_W[i]),
                  (f"swdT{i}", swd_d[i], 128, (FCS_W[i] // 128) * D)]
    for e_ in range(16):
        wspec += [(f"wgu{e_}", wgu_d[e_], 128, C * D), (f"wdn{e_}", wdn_d[e_], 128, 4 * D)]
    scr = {}
    for nm, _, npart, ncols in wspec:
        scr[nm] = nc.dram_tensor("s_" + nm, [npart, ncols], BF16, kind="Internal").ap()

    emit_consts(p, t, f32id=False)
    mv = p.op("gpsimd", lambda e: e.memset(t["mask_bfm"][:], 0.0), inc="G")
    p.wait("gpsimd", "G", mv)
    p.op("gpsimd", lambda e: e.affine_select(out=t["mask_bfm"][:], in_=t["mask_bfm"][:],
                                             compare_op=ALU.is_ge, fill=-10000.0, base=0,
                                             pattern=[[-1, 128]], channel_multiplier=1), inc="G")
    mv = p.op("gpsimd", lambda e: e.memset(t["Ebf"][:], 1.0), inc="G")
    p.wait("gpsimd", "G", mv)
    p.op("gpsimd", lambda e: e.affine_select(out=t["Ebf"][:], in_=t["Ebf"][:],
                                             compare_op=ALU.is_ge, fill=0.0, base=0,
                                             pattern=[[1, D]], channel_multiplier=-64), inc="G")
    p.op("gpsimd", lambda e: e.affine_select(out=t["Ebf"][:], in_=t["Ebf"][:],
                                             compare_op=ALU.is_ge, fill=0.0, base=63,
                                             pattern=[[-1, D]], channel_multiplier=64), inc="G")
    gv = p.val("G")
    for eng in ("tensor", "scalar", "vector"):
        p.wait(eng, "G", gv)

    def emit_phase0():
        with contextlib.ExitStack() as st_:
            stf = [st_.enter_context(nc.sbuf_tensor(f"ph0f{i}", [128, 2048], F32))[:]
                   for i in range(2)]
            stb = [st_.enter_context(nc.sbuf_tensor(f"ph0b{i}", [128, 2048], BF16))[:]
                   for i in range(2)]
            cvt_mark = [0, 0]
            out_mark = [0, 0]
            k = 0
            for nm, src_ap, npart, ncols in wspec:
                for j in range(0, ncols, 2048):
                    w = min(2048, ncols - j)
                    par = k % 2
                    k += 1
                    if cvt_mark[par]:
                        p.wait("sync", "S", cvt_mark[par])
                    lv = p.op("sync", lambda e, src_ap=src_ap, npart=npart, j=j, w=w, par=par:
                              e.dma_start(out=stf[par][0:npart, 0:w],
                                          in_=src_ap[:, j:j + w]), inc="Lw", n=16)
                    p.wait("scalar", "Lw", lv)
                    if out_mark[par]:
                        p.wait("scalar", "Ls", out_mark[par])
                    cvt_mark[par] = p.op("scalar", lambda e, npart=npart, w=w, par=par: e.copy(
                        out=stb[par][0:npart, 0:w], in_=stf[par][0:npart, 0:w]), inc="S")
                    p.wait("gpsimd", "S", cvt_mark[par])
                    out_mark[par] = p.op("gpsimd", lambda e, nm=nm, npart=npart, j=j, w=w,
                                         par=par: e.dma_start(
                        out=scr[nm][:, j:j + w], in_=stb[par][0:npart, 0:w]), inc="Ls", n=16)
            p.wait("sync", "Ls", p.val("Ls"))

    def wload(nm, dst_ap, ncols, war_T=None):
        if war_T:
            p.wait("sync", "T", war_T)
        return p.op("sync", lambda e, nm=nm, ncols=ncols, dst_ap=dst_ap: e.dma_start(
            out=dst_ap[:, 0:ncols], in_=scr[nm][:, 0:ncols]), inc="Lw", n=16)

    def load_x(seq):
        for ti in range(T):
            if t.get("_stage_last") is not None:
                p.wait("sync", "S", t["_stage_last"])
            lv = p.op("sync", lambda e, seq=seq, ti=ti: e.dma_start(
                out=t["stage"][:], in_=x_dram[seq * SL + 128 * ti: seq * SL + 128 * ti + 128, :]),
                inc="L", n=16)
            p.wait("scalar", "L", lv)
            p.wait("scalar", "V", p.val("V"))
            t["_stage_last"] = p.op("scalar", lambda e, ti=ti: e.copy(
                out=t["x_bf"][:, D * ti:D * ti + D], in_=t["stage"][:]), inc="S")

    def build_xfm():
        p.wait("tensor", "S", p.val("S"))
        for ti in range(T):
            p.wait("tensor", "V", p.val("V"))
            tv = None
            for c in range(C):
                tv = p.op("tensor", lambda e, ti=ti, c=c: e.transpose(
                    t["ptrb"][:, 128 * c:128 * c + 128],
                    t["x_bf"][:, D * ti + 128 * c: D * ti + 128 * c + 128],
                    t["ident_bf"][:]), inc="T")
            p.wait("vector", "T", tv)
            dst = t["xfm"].rearrange("p (c w) -> p c w", w=SL)[:, :, 128 * ti:128 * ti + 128]
            src = t["ptrb"].rearrange("p (c q) -> p c q", q=128)
            p.op("vector", lambda e, dst=dst, src=src: e.tensor_copy(out=dst, in_=src), inc="V")

    def xbf_tile(ti):
        return t["x_bf"][:, D * ti:D * ti + D]

    def bcast_to(row_src_d, dst_ap, n, bf=True):
        p.wait("sync", "T", p.val("T"))
        lv = p.op("sync", lambda e: e.dma_start(out=t["row"][:, 0:n], in_=row_src_d[:]),
                  inc="Lr", n=16)
        p.wait("tensor", "Lr", lv)
        p.wait("tensor", "V", p.val("V"))
        tv = None
        for j in range(0, n, 512):
            w = min(512, n - j)
            tv = p.op("tensor", lambda e, j=j, w=w: e.matmul(
                t["pA"][:, j:j + w], t["ones_f32"][:], t["row"][:, j:j + w],
                start=True, stop=True), inc="T")
        p.wait("vector", "T", tv)
        return p.op("vector", lambda e: e.tensor_copy(out=dst_ap[:, 0:n], in_=t["pA"][:, 0:n]),
                    inc="V")

    def emit_A(seq):
        p.mark(f'A{seq}')
        with contextlib.ExitStack() as st_:
            a = {}
            for nm, shape, dt in (("ktm", [128, T * D], BF16), ("vtm", [128, T * D], BF16),
                                  ("ctx", [128, C * D], BF16), ("kexp_f", [128, SL], BF16),
                                  ("qexp", [128, D], BF16), ("qfm", [128, D], BF16),
                                  ("afm", [128, D], BF16),
                                  ("r1", [128, D], F32),
                                  ("bo_bc", [128, D], BF16), ("gam_bc", [128, D], BF16),
                                  ("bet_bc", [128, D], BF16)):
                a[nm] = st_.enter_context(nc.sbuf_tensor(f"A{seq}_{nm}", shape, dt))[:]
            a["attnb"] = a["qexp"]
            p.barrier()
            for dram, dst in ((bo_d, "bo_bc"), (gam_d, "gam_bc"), (bet_d, "bet_bc")):
                bcast_to(dram, a[dst], D)
            wk_ready = wload("wk", t["W0"], C * D, war_T=p.val("T"))
            wv_ready = wload("wv", t["W1"], C * D)
            build_xfm()
            p.mark(f'A{seq}.K')
            p.wait("tensor", "Lw", wk_ready)
            for c in range(C):
                p.wait("tensor", "S", p.val("S"))
                p.wait("tensor", "V", p.val("V"))
                tv = None
                for tc in range(4):
                    dstp = t["pA"] if tc < 2 else t["pB"]
                    off = 512 * (tc % 2)
                    for cin in range(C):
                        tv = p.op("tensor", lambda e, c=c, tc=tc, cin=cin, dstp=dstp, off=off:
                                  e.matmul(dstp[:, off:off + 512],
                                           t["W0"][:, D * cin + 128 * c: D * cin + 128 * c + 128],
                                           t["xfm"][:, SL * cin + 512 * tc: SL * cin + 512 * tc + 512],
                                           start=(cin == 0), stop=(cin == C - 1)), inc="T")
                p.wait("scalar", "T", tv)
                p.op("scalar", lambda e, a=a: e.activation(
                    out=a["kexp_f"][:, 0:1024], in_=t["pA"][:], func=AF.Exp,
                    bias=t["shift24"][:], scale=1.0, accum_out=t["zc2"][:, 0:1]))
                sv = p.op("scalar", lambda e, a=a: e.activation(
                    out=a["kexp_f"][:, 1024:2048], in_=t["pB"][:], func=AF.Exp,
                    bias=t["shift24"][:], scale=1.0, accum_out=t["zc2"][:, 1:2]), inc="S")
                p.wait("vector", "S", sv)
                p.op("vector", lambda e: e.tensor_tensor(out=t["zc"][:], in0=t["zc2"][:, 0:1],
                                                         in1=t["zc2"][:, 1:2], op=ALU.add))
                p.op("vector", lambda e, c=c: e.reciprocal(t["rzk"][:, c:c + 1], t["zc"][:]))
                p.wait("tensor", "S", sv)
                for r in range(2):
                    p.wait("tensor", "V", p.val("V"))
                    tv = None
                    for j in range(8):
                        tv = p.op("tensor", lambda e, a=a, r=r, j=j: e.transpose(
                            t["ptrb"][:, 128 * j:128 * j + 128],
                            a["kexp_f"][:, 1024 * r + 128 * j: 1024 * r + 128 * j + 128],
                            t["ident_bf"][:]), inc="T")
                    p.wait("vector", "T", tv)
                    dst = a["ktm"].rearrange("p (t w) -> p t w", w=D)[:, 8 * r:8 * r + 8,
                                                                     128 * c:128 * c + 128]
                    src = t["ptrb"].rearrange("p (q w) -> p q w", w=128)
                    p.op("vector", lambda e, dst=dst, src=src: e.tensor_copy(out=dst, in_=src),
                         inc="V")
            wq_ready = wload("wq", t["W0"], C * D, war_T=p.val("T"))
            p.mark(f'A{seq}.V')
            p.wait("tensor", "Lw", wv_ready)
            for ti in range(T):
                p.wait("tensor", "V", p.val("V"))
                p.wait("tensor", "S", p.val("S"))
                tv = None
                for half in range(2):
                    for cin in range(C):
                        tv = p.op("tensor", lambda e, ti=ti, half=half, cin=cin: e.matmul(
                            t["pA"][:, 512 * half:512 * half + 512],
                            t["xfm"][:, SL * cin + 128 * ti: SL * cin + 128 * ti + 128],
                            t["W1"][:, D * cin + 512 * half: D * cin + 512 * half + 512],
                            start=(cin == 0), stop=(cin == C - 1)), inc="T")
                p.wait("vector", "T", tv)
                p.op("vector", lambda e, a=a, ti=ti: e.tensor_copy(
                    out=a["vtm"][:, D * ti:D * ti + D], in_=t["pA"][:]), inc="V")
            wo_ready = wload("wo", t["W1"], C * D, war_T=p.val("T"))
            p.mark(f'A{seq}.ctx')
            for c in range(C):
                p.wait("tensor", "V", p.val("V"))
                p.wait("tensor", "S", p.val("S"))
                tv = None
                for half in range(2):
                    for ti in range(T):
                        tv = p.op("tensor", lambda e, a=a, c=c, half=half, ti=ti: e.matmul(
                            t["pB"][:, 512 * half:512 * half + 512],
                            a["ktm"][:, D * ti + 128 * c: D * ti + 128 * c + 128],
                            a["vtm"][:, D * ti + 512 * half: D * ti + 512 * half + 512],
                            start=(ti == 0), stop=(ti == T - 1)), inc="T")
                p.wait("vector", "T", tv)
                p.op("vector", lambda e, a=a, c=c: e.tensor_scalar(
                    out=a["ctx"][:, D * c:D * c + D], in0=t["pB"][:],
                    scalar1=t["rzk"][:, c:c + 1], scalar2=None, op0=ALU.mult), inc="V")
            p.wait("tensor", "Lw", wq_ready)
            p.wait("tensor", "Lw", wo_ready)
            p.mark(f'A{seq}.A4')
            for ti in range(T):
                p.wait("tensor", "S", p.val("S"))
                p.wait("tensor", "V", p.val("V"))
                tv = None
                for half in range(2):
                    for cin in range(C):
                        tv = p.op("tensor", lambda e, ti=ti, half=half, cin=cin: e.matmul(
                            t["pA"][:, 512 * half:512 * half + 512],
                            t["xfm"][:, SL * cin + 128 * ti: SL * cin + 128 * ti + 128],
                            t["W0"][:, D * cin + 512 * half: D * cin + 512 * half + 512],
                            start=(cin == 0), stop=(cin == C - 1)), inc="T")
                p.wait("scalar", "T", tv)
                sv = p.op("scalar", lambda e, a=a: e.activation(
                    out=a["qexp"][:], in_=t["pA"][:], func=AF.Exp,
                    bias=t["shift24"][:], scale=1.0, accum_out=t["zc"][:]), inc="S")
                p.wait("vector", "S", sv)
                p.op("vector", lambda e: e.reciprocal(t["rzq"][:], t["zc"][:]))
                p.wait("tensor", "S", sv)
                p.wait("tensor", "V", p.val("V"))
                tv = None
                for c in range(C):
                    tv = p.op("tensor", lambda e, a=a, c=c: e.transpose(
                        t["ptrb"][:, 128 * c:128 * c + 128],
                        a["qexp"][:, 128 * c:128 * c + 128], t["ident_bf"][:]), inc="T")
                p.wait("vector", "T", tv)
                qv = p.op("vector", lambda e, a=a: e.tensor_copy(out=a["qfm"][:], in_=t["ptrb"][:]),
                          inc="V")
                p.wait("tensor", "V", qv)
                tv = None
                for half in range(2):
                    for c in range(C):
                        tv = p.op("tensor", lambda e, a=a, half=half, c=c: e.matmul(
                            t["pB"][:, 512 * half:512 * half + 512],
                            a["qfm"][:, 128 * c:128 * c + 128],
                            a["ctx"][:, D * c + 512 * half: D * c + 512 * half + 512],
                            start=(c == 0), stop=(c == C - 1)), inc="T")
                p.wait("vector", "T", tv)
                av = p.op("vector", lambda e, a=a: e.tensor_scalar(
                    out=a["attnb"][:], in0=t["pB"][:], scalar1=t["rzq"][:],
                    scalar2=float(D) ** -0.5, op0=ALU.mult, op1=ALU.mult), inc="V")
                p.wait("tensor", "V", av)
                tv = None
                for c in range(C):
                    tv = p.op("tensor", lambda e, a=a, c=c: e.transpose(
                        t["ptrb"][:, 128 * c:128 * c + 128],
                        a["attnb"][:, 128 * c:128 * c + 128], t["ident_bf"][:]), inc="T")
                p.wait("vector", "T", tv)
                fv = p.op("vector", lambda e, a=a: e.tensor_copy(out=a["afm"][:], in_=t["ptrb"][:]),
                          inc="V")
                p.wait("tensor", "V", fv)
                tv = None
                for half in range(2):
                    for c in range(C):
                        tv = p.op("tensor", lambda e, a=a, half=half, c=c: e.matmul(
                            t["pA"][:, 512 * half:512 * half + 512],
                            a["afm"][:, 128 * c:128 * c + 128],
                            t["W1"][:, D * c + 512 * half: D * c + 512 * half + 512],
                            start=(c == 0), stop=(c == C - 1)), inc="T")
                p.wait("vector", "T", tv)
                p.op("vector", lambda e, a=a: e.tensor_tensor(out=a["r1"][:], in0=t["pA"][:],
                                                              in1=a["bo_bc"][:], op=ALU.add))
                p.op("vector", lambda e, a=a, ti=ti: e.tensor_tensor(
                    out=a["r1"][:], in0=a["r1"][:], in1=xbf_tile(ti), op=ALU.add), inc="V")
                emit_ln_stats2(p, t, a["r1"][:], t["mu1"][:], t["rstd1"][:], D, a["afm"][:])
                p.op("vector", lambda e, a=a: e.tensor_scalar(
                    out=a["attnb"][:], in0=a["r1"][:], scalar1=t["mu1"][:],
                    scalar2=t["rstd1"][:], op0=ALU.subtract, op1=ALU.mult))
                p.op("vector", lambda e, a=a: e.tensor_tensor(out=a["attnb"][:], in0=a["attnb"][:],
                                                              in1=a["gam_bc"][:], op=ALU.mult))
                p.op("vector", lambda e, a=a: e.tensor_tensor(out=a["attnb"][:], in0=a["attnb"][:],
                                                              in1=a["bet_bc"][:], op=ALU.add))
                p.op("vector", lambda e, a=a, ti=ti: e.tensor_tensor(
                    out=a["r1"][:], in0=a["attnb"][:], in1=xbf_tile(ti), op=ALU.add), inc="V")
                emit_ln_stats2(p, t, a["r1"][:], t["mu2"][:], t["rstd2"][:], D, a["afm"][:])
                p.op("vector", lambda e, a=a, ti=ti: e.tensor_scalar(
                    out=xbf_tile(ti), in0=a["r1"][:], scalar1=t["mu2"][:],
                    scalar2=t["rstd2"][:], op0=ALU.subtract, op1=ALU.mult), inc="V")
            p.barrier()

    def emit_B(seq):
        p.mark(f'B{seq}')
        with contextlib.ExitStack() as st_:
            b = {}
            for nm, shape, dt in (("kfm", [128, C * SL], BF16), ("vtm", [128, T * D], BF16),
                                  ("seltr", [16, T * 128], BF16), ("b2bf", [16, D], BF16),
                                  ("nwT", [128, 3 * C * 15], BF16), ("nbbc", [128, 45], F32),
                                  ("b1bc", [128, D], BF16), ("attn_tm", [128, SL], BF16),
                                  ("attn_km", [128, SL], BF16), ("Hbf", [128, D], BF16),
                                  ("maskbf", [128, D], BF16), ("Hm", [128, D], BF16),
                                  ("Hfm", [128, D], BF16), ("ffout", [128, D], BF16),
                                  ("qfm_t", [128, D], BF16), ("r1", [128, D], F32),
                                  ("scoreb", [128, 15], F32), ("bits", [128, 4], F32),
                                  ("nbits", [128, 4], F32), ("sig", [128, 1], F32),
                                  ("tmp1", [128, 1], F32), ("p2", [128, 4], F32),
                                  ("p3g", [128, 8], F32), ("tmp8", [128, 8], F32),
                                  ("sel16", [128, 16], F32), ("selbf", [128, 16], BF16)):
                b[nm] = st_.enter_context(nc.sbuf_tensor(f"B{seq}_{nm}", shape, dt))[:]
            p.barrier()
            build_xfm()
            nw_ready = None
            p.wait("sync", "V", p.val("V"))
            p.wait("sync", "S", p.val("S"))
            for i, pfx in enumerate(("q", "k", "v")):
                nw_ready = wload(pfx + "_nwT", b["nwT"][:, C * 15 * i:C * 15 * (i + 1)],
                                 C * 15, war_T=(p.val("T") if i == 0 else None))
                bcast_to(dr[pfx + "_nb"], b["nbbc"][:, 15 * i:], 15)
            p.wait("tensor", "Lw", nw_ready)

            def fff_scores_routing(b, pidx, ti):
                sc = b["scoreb"]
                p.wait("tensor", "V", p.val("V"))
                p.wait("tensor", "S", p.val("S"))
                tv = None
                for cin in range(C):
                    tv = p.op("tensor", lambda e, b=b, cin=cin, ti=ti, pidx=pidx: e.matmul(
                        t["pO"][:, 0:15],
                        t["xfm"][:, SL * cin + 128 * ti: SL * cin + 128 * ti + 128],
                        b["nwT"][:, C * 15 * pidx + 15 * cin: C * 15 * pidx + 15 * cin + 15],
                        start=(cin == 0), stop=(cin == C - 1)), inc="T")
                p.wait("vector", "T", tv)
                p.op("vector", lambda e, b=b, pidx=pidx: e.tensor_tensor(
                    out=sc, in0=t["pO"][:, 0:15],
                    in1=b["nbbc"][:, 15 * pidx:15 * pidx + 15], op=ALU.add))
                bb = b["bits"]; nb = b["nbits"]
                p.op("vector", lambda e, sc=sc, bb=bb: e.tensor_scalar(
                    out=bb[:, 0:1], in0=sc[:, 0:1], scalar1=0.0, scalar2=None, op0=ALU.is_ge))
                p.op("vector", lambda e, sc=sc, b=b: e.tensor_tensor(
                    out=b["tmp1"][:], in0=sc[:, 2:3], in1=sc[:, 1:2], op=ALU.subtract))
                p.op("vector", lambda e, sc=sc, b=b, bb=bb: e.scalar_tensor_tensor(
                    out=b["sig"][:], in0=b["tmp1"][:], scalar=bb[:, 0:1], in1=sc[:, 1:2],
                    op0=ALU.mult, op1=ALU.add))
                p.op("vector", lambda e, b=b, bb=bb: e.tensor_scalar(
                    out=bb[:, 1:2], in0=b["sig"][:], scalar1=0.0, scalar2=None, op0=ALU.is_ge))
                p.op("vector", lambda e, bb=bb, nb=nb: e.tensor_scalar(
                    out=nb[:, 0:2], in0=bb[:, 0:2], scalar1=-1.0, scalar2=1.0,
                    op0=ALU.mult, op1=ALU.add))
                p.op("vector", lambda e, b=b, bb=bb, nb=nb: e.tensor_tensor(
                    out=b["p2"][:, 0:1], in0=nb[:, 0:1], in1=nb[:, 1:2], op=ALU.mult))
                p.op("vector", lambda e, b=b, bb=bb, nb=nb: e.tensor_tensor(
                    out=b["p2"][:, 1:2], in0=nb[:, 0:1], in1=bb[:, 1:2], op=ALU.mult))
                p.op("vector", lambda e, b=b, bb=bb, nb=nb: e.tensor_tensor(
                    out=b["p2"][:, 2:3], in0=bb[:, 0:1], in1=nb[:, 1:2], op=ALU.mult))
                p.op("vector", lambda e, b=b, bb=bb: e.tensor_tensor(
                    out=b["p2"][:, 3:4], in0=bb[:, 0:1], in1=bb[:, 1:2], op=ALU.mult))
                p.op("vector", lambda e, sc=sc, b=b: e.tensor_tensor(
                    out=b["tmp8"][:, 0:4], in0=b["p2"][:], in1=sc[:, 3:7], op=ALU.mult))
                p.op("vector", lambda e, b=b: e.reduce_sum(b["sig"][:], b["tmp8"][:, 0:4],
                                                           axis=AX.X))
                p.op("vector", lambda e, b=b, bb=bb: e.tensor_scalar(
                    out=bb[:, 2:3], in0=b["sig"][:], scalar1=0.0, scalar2=None, op0=ALU.is_ge))
                p.op("vector", lambda e, bb=bb, nb=nb: e.tensor_scalar(
                    out=nb[:, 2:3], in0=bb[:, 2:3], scalar1=-1.0, scalar2=1.0,
                    op0=ALU.mult, op1=ALU.add))
                p.op("vector", lambda e, b=b, nb=nb: e.tensor_scalar(
                    out=b["p3g"][:, 0:4], in0=b["p2"][:], scalar1=nb[:, 2:3], scalar2=None,
                    op0=ALU.mult))
                p.op("vector", lambda e, b=b, bb=bb: e.tensor_scalar(
                    out=b["p3g"][:, 4:8], in0=b["p2"][:], scalar1=bb[:, 2:3], scalar2=None,
                    op0=ALU.mult))
                sv1 = sc[:, 7:15].rearrange("p (a b) -> p a b", b=2)[:, :, 0:1]
                sv2 = sc[:, 7:15].rearrange("p (a b) -> p a b", b=2)[:, :, 1:2]
                p.op("vector", lambda e, b=b, sv1=sv1: e.tensor_tensor(
                    out=b["tmp8"][:, 0:4].rearrange("p (a b) -> p a b", b=1),
                    in0=b["p3g"][:, 0:4].rearrange("p (a b) -> p a b", b=1),
                    in1=sv1, op=ALU.mult))
                p.op("vector", lambda e, b=b, sv2=sv2: e.tensor_tensor(
                    out=b["tmp8"][:, 4:8].rearrange("p (a b) -> p a b", b=1),
                    in0=b["p3g"][:, 4:8].rearrange("p (a b) -> p a b", b=1),
                    in1=sv2, op=ALU.mult))
                p.op("vector", lambda e, b=b: e.reduce_sum(b["sig"][:], b["tmp8"][:], axis=AX.X))
                p.op("vector", lambda e, b=b, bb=bb: e.tensor_scalar(
                    out=bb[:, 3:4], in0=b["sig"][:], scalar1=0.0, scalar2=None, op0=ALU.is_ge))
                p.op("vector", lambda e, bb=bb, nb=nb: e.tensor_scalar(
                    out=nb[:, 3:4], in0=bb[:, 3:4], scalar1=-1.0, scalar2=1.0,
                    op0=ALU.mult, op1=ALU.add))
                p.op("vector", lambda e, b=b, nb=nb: e.tensor_scalar(
                    out=b["sel16"][:, 0:8], in0=b["p3g"][:], scalar1=nb[:, 3:4], scalar2=None,
                    op0=ALU.mult))
                p.op("vector", lambda e, b=b, bb=bb: e.tensor_scalar(
                    out=b["sel16"][:, 8:16], in0=b["p3g"][:], scalar1=bb[:, 3:4], scalar2=None,
                    op0=ALU.mult))
                vv = p.op("vector", lambda e, b=b: e.tensor_copy(out=b["selbf"][:],
                                                                 in_=b["sel16"][:]))
                p.wait("tensor", "V", p.val("V"))
                tv = p.op("tensor", lambda e, b=b: e.transpose(
                    t["ptrb"][0:16, 0:128], b["selbf"][:], t["ident_bf"][:]), inc="T")
                p.wait("vector", "T", tv)
                p.op("vector", lambda e, b=b, ti=ti: e.tensor_copy(
                    out=b["seltr"][:, 128 * ti:128 * ti + 128], in_=t["ptrb"][0:16, 0:128]),
                    inc="V")

            def fff_apply(b, pidx, ti, out_kind):
                selc = 128 * ti
                p.wait("tensor", "V", p.val("V"))
                tv = None
                for half in range(2):
                    for cin in range(C):
                        tv = p.op("tensor", lambda e, ti=ti, half=half, cin=cin: e.matmul(
                            t["pA"][:, 512 * half:512 * half + 512],
                            t["xfm"][:, SL * cin + 128 * ti: SL * cin + 128 * ti + 128],
                            t["W0"][:, D * cin + 512 * half: D * cin + 512 * half + 512],
                            start=(cin == 0), stop=(cin == C - 1)), inc="T")
                mv = None
                for half in range(2):
                    mv = p.op("tensor", lambda e, b=b, half=half, selc=selc: e.matmul(
                        t["pB"][:, 512 * half:512 * half + 512],
                        b["seltr"][:, selc:selc + 128],
                        t["Ebf"][:, 512 * half:512 * half + 512],
                        start=True, stop=True), inc="T")
                p.wait("vector", "T", mv)
                p.op("vector", lambda e, b=b: e.tensor_tensor(out=b["Hbf"][:], in0=t["pA"][:],
                                                              in1=b["b1bc"][:], op=ALU.add))
                p.op("vector", lambda e, b=b: e.tensor_scalar(out=b["Hbf"][:], in0=b["Hbf"][:],
                                                              scalar1=0.0, scalar2=None,
                                                              op0=ALU.max))
                p.op("vector", lambda e, b=b: e.tensor_copy(out=b["maskbf"][:], in_=t["pB"][:]))
                hv = p.op("vector", lambda e, b=b: e.tensor_tensor(
                    out=b["Hm"][:], in0=b["Hbf"][:], in1=b["maskbf"][:], op=ALU.mult), inc="V")
                p.wait("tensor", "V", hv)
                tv = None
                for c in range(C):
                    tv = p.op("tensor", lambda e, b=b, c=c: e.transpose(
                        t["ptrb"][:, 128 * c:128 * c + 128],
                        b["Hm"][:, 128 * c:128 * c + 128], t["ident_bf"][:]), inc="T")
                p.wait("vector", "T", tv)
                fv = p.op("vector", lambda e, b=b: e.tensor_copy(out=b["Hfm"][:], in_=t["ptrb"][:]),
                          inc="V")
                p.wait("tensor", "V", fv)
                tv = None
                for half in range(2):
                    for c in range(C):
                        p.op("tensor", lambda e, b=b, half=half, c=c: e.matmul(
                            t["pA"][:, 512 * half:512 * half + 512],
                            b["Hfm"][:, 128 * c:128 * c + 128],
                            t["W1"][:, D * c + 512 * half: D * c + 512 * half + 512],
                            start=(c == 0), stop=False), inc="T")
                    tv = p.op("tensor", lambda e, b=b, half=half, selc=selc: e.matmul(
                        t["pA"][:, 512 * half:512 * half + 512],
                        b["seltr"][:, selc:selc + 128],
                        b["b2bf"][:, 512 * half:512 * half + 512],
                        start=False, stop=True), inc="T")
                p.wait("vector", "T", tv)
                if out_kind == "vtm":
                    p.op("vector", lambda e, b=b, ti=ti: e.tensor_copy(
                        out=b["vtm"][:, D * ti:D * ti + D], in_=t["pA"][:]), inc="V")
                elif out_kind == "q":
                    p.op("vector", lambda e, b=b: e.tensor_copy(out=b["ffout"][:], in_=t["pA"][:]),
                         inc="V")
                else:
                    ev = p.op("vector", lambda e, b=b: e.tensor_copy(out=b["ffout"][:],
                                                                     in_=t["pA"][:]), inc="V")
                    p.wait("tensor", "V", ev)
                    tv = None
                    for c in range(C):
                        tv = p.op("tensor", lambda e, b=b, c=c: e.transpose(
                            t["ptrb"][:, 128 * c:128 * c + 128],
                            b["ffout"][:, 128 * c:128 * c + 128], t["ident_bf"][:]), inc="T")
                    p.wait("vector", "T", tv)
                    dst = b["kfm"].rearrange("p (c w) -> p c w", w=SL)[:, :, 128 * ti:128 * ti + 128]
                    src = t["ptrb"].rearrange("p (c q) -> p c q", q=128)
                    p.op("vector", lambda e, dst=dst, src=src: e.tensor_copy(out=dst, in_=src),
                         inc="V")

            def load_fff_weights(pfx):
                tval = p.val("T")
                p.wait("sync", "V", p.val("V"))
                p.wait("sync", "S", p.val("S"))
                rw1 = wload(pfx + "_w1", t["W0"], C * D, war_T=tval)
                rw2 = wload(pfx + "_w2", t["W1"], C * D)
                rb2 = wload(pfx + "_b2", b["b2bf"], D)
                rb = bcast_to(dr[pfx + "_b1"], b["b1bc"], D)
                p.wait("tensor", "Lw", rb2)
                p.wait("vector", "V", rb)

            p.mark(f'B{seq}.kv')
            for pidx, pfx, kind in ((1, "k", "kfm"), (2, "v", "vtm")):
                load_fff_weights(pfx)
                for ti in range(T):
                    fff_scores_routing(b, pidx, ti)
                    fff_apply(b, pidx, ti, kind)
            p.mark(f'B{seq}.qattn')
            load_fff_weights("q")
            for ti in range(T):
                fff_scores_routing(b, 0, ti)
                fff_apply(b, 0, ti, "q")
                p.wait("tensor", "V", p.val("V"))
                tv = None
                for c in range(C):
                    tv = p.op("tensor", lambda e, b=b, c=c: e.transpose(
                        t["ptrb"][:, 128 * c:128 * c + 128],
                        b["ffout"][:, 128 * c: 128 * c + 128], t["ident_bf"][:]), inc="T")
                p.wait("vector", "T", tv)
                qv = p.op("vector", lambda e, b=b: e.tensor_copy(out=b["qfm_t"][:],
                                                                 in_=t["ptrb"][:]), inc="V")
                p.wait("tensor", "V", qv)
                p.wait("tensor", "S", p.val("S"))
                tv = None
                for tc in range(4):
                    dstp = t["pA"] if tc < 2 else t["pB"]
                    off = 512 * (tc % 2)
                    for c in range(C):
                        tv = p.op("tensor", lambda e, b=b, tc=tc, c=c, dstp=dstp, off=off:
                                  e.matmul(dstp[:, off:off + 512],
                                           b["qfm_t"][:, 128 * c:128 * c + 128],
                                           b["kfm"][:, SL * c + 512 * tc: SL * c + 512 * tc + 512],
                                           start=(c == 0), stop=(c == C - 1)), inc="T")
                p.wait("scalar", "T", tv)
                p.op("scalar", lambda e, b=b: e.activation(
                    out=b["attn_tm"][:, 0:1024], in_=t["pA"][:], func=AF.Exp,
                    bias=t["shift48"][:], scale=float(D) ** -0.5, accum_out=t["zc2"][:, 0:1]))
                sv = p.op("scalar", lambda e, b=b: e.activation(
                    out=b["attn_tm"][:, 1024:2048], in_=t["pB"][:], func=AF.Exp,
                    bias=t["shift48"][:], scale=float(D) ** -0.5, accum_out=t["zc2"][:, 1:2]),
                    inc="S")
                p.wait("vector", "S", sv)
                p.op("vector", lambda e: e.tensor_tensor(out=t["zc"][:], in0=t["zc2"][:, 0:1],
                                                         in1=t["zc2"][:, 1:2], op=ALU.add))
                p.op("vector", lambda e: e.reciprocal(t["rza"][:], t["zc"][:]))
                p.wait("tensor", "S", sv)
                for r in range(2):
                    p.wait("tensor", "V", p.val("V"))
                    tv = None
                    for j in range(8):
                        tv = p.op("tensor", lambda e, b=b, r=r, j=j: e.transpose(
                            t["ptrk"][:, 128 * j:128 * j + 128],
                            b["attn_tm"][:, 1024 * r + 128 * j: 1024 * r + 128 * j + 128],
                            t["ident_bf"][:]), inc="T")
                    p.wait("vector", "T", tv)
                    p.op("vector", lambda e, b=b, r=r: e.tensor_copy(
                        out=b["attn_km"][:, 1024 * r:1024 * r + 1024], in_=t["ptrk"][:]), inc="V")
                p.wait("tensor", "V", p.val("V"))
                tv = None
                for half in range(2):
                    for kt in range(T):
                        tv = p.op("tensor", lambda e, b=b, half=half, kt=kt: e.matmul(
                            t["pO"][:, 512 * half:512 * half + 512],
                            b["attn_km"][:, 128 * kt:128 * kt + 128],
                            b["vtm"][:, D * kt + 512 * half: D * kt + 512 * half + 512],
                            start=(kt == 0), stop=(kt == T - 1)), inc="T")
                p.wait("vector", "T", tv)
                p.op("vector", lambda e, b=b: e.tensor_scalar(
                    out=b["r1"][:], in0=t["pO"][:], scalar1=t["rza"][:], scalar2=None,
                    op0=ALU.mult))
                p.op("vector", lambda e, b=b, ti=ti: e.tensor_tensor(
                    out=b["r1"][:], in0=b["r1"][:], in1=xbf_tile(ti), op=ALU.add), inc="V")
                emit_ln_stats2(p, t, b["r1"][:], t["mu2"][:], t["rstd2"][:], D, b["Hbf"][:])
                p.op("vector", lambda e, b=b, ti=ti: e.tensor_scalar(
                    out=xbf_tile(ti), in0=b["r1"][:], scalar1=t["mu2"][:],
                    scalar2=t["rstd2"][:], op0=ALU.subtract, op1=ALU.mult), inc="V")
            p.barrier()

    def emit_C(seq):
        p.mark(f'C{seq}')
        with contextlib.ExitStack() as st_:
            cvars = {}
            for nm, shape, dt in (("kfm", [128, C * SL], BF16), ("vtm", [128, T * D], BF16),
                                  ("qtmp", [128, D], BF16), ("qfm_t", [128, D], BF16),
                                  ("attn_tm", [128, SL], BF16), ("attn_km", [128, SL], BF16),
                                  ("attn_tm2", [128, SL], BF16), ("attn_km2", [128, SL], BF16),
                                  ("out_sb", [128, D], BF16), ("ofm", [128, D], BF16),
                                  ("rza16", [128, 16], F32),
                                  ("r1", [128, D], F32)):
                cvars[nm] = st_.enter_context(nc.sbuf_tensor(f"C{seq}_{nm}", shape, dt))[:]
            cv = cvars
            p.barrier()
            build_xfm()
            SCALE = 0.125
            rq = wload("wqT", t["W0"], C * D, war_T=p.val("T"))
            rk = wload("wkT", t["W1"], C * D)
            p.wait("tensor", "Lw", rk)
            for jc in range(C):
                for tc in range(4):
                    dstp = t["pA"] if (jc * 4 + tc) % 2 == 0 else t["pB"]
                    p.wait("tensor", "V", p.val("V"))
                    tv = None
                    for cin in range(C):
                        tv = p.op("tensor", lambda e, jc=jc, tc=tc, cin=cin, dstp=dstp: e.matmul(
                            dstp[:, 0:512],
                            t["W1"][:, D * cin + 128 * jc: D * cin + 128 * jc + 128],
                            t["xfm"][:, SL * cin + 512 * tc: SL * cin + 512 * tc + 512],
                            start=(cin == 0), stop=(cin == C - 1)), inc="T")
                    p.wait("vector", "T", tv)
                    p.op("vector", lambda e, cv=cv, jc=jc, tc=tc, dstp=dstp: e.tensor_copy(
                        out=cv["kfm"][:, SL * jc + 512 * tc: SL * jc + 512 * tc + 512],
                        in_=dstp[:, 0:512]), inc="V")
            rv_ = wload("wvT", t["W1"], C * D, war_T=p.val("T"))
            p.wait("tensor", "Lw", rv_)
            for ti in range(T):
                p.wait("tensor", "V", p.val("V"))
                tv = None
                for half in range(2):
                    for cin in range(C):
                        tv = p.op("tensor", lambda e, ti=ti, half=half, cin=cin: e.matmul(
                            t["pA"][:, 512 * half:512 * half + 512],
                            t["xfm"][:, SL * cin + 128 * ti: SL * cin + 128 * ti + 128],
                            t["W1"][:, D * cin + 512 * half: D * cin + 512 * half + 512],
                            start=(cin == 0), stop=(cin == C - 1)), inc="T")
                p.wait("vector", "T", tv)
                p.op("vector", lambda e, cv=cv, ti=ti: e.tensor_copy(
                    out=cv["vtm"][:, D * ti:D * ti + D], in_=t["pA"][:]), inc="V")
            ro = wload("woT", t["W1"], C * D, war_T=p.val("T"))
            p.wait("tensor", "Lw", rq)
            p.wait("tensor", "Lw", ro)
            p.mark(f'C{seq}.qt')
            for qt in range(T):
                p.wait("tensor", "V", p.val("V"))
                tv = None
                for half in range(2):
                    for cin in range(C):
                        tv = p.op("tensor", lambda e, qt=qt, half=half, cin=cin: e.matmul(
                            t["pO"][:, 512 * half:512 * half + 512],
                            t["xfm"][:, SL * cin + 128 * qt: SL * cin + 128 * qt + 128],
                            t["W0"][:, D * cin + 512 * half: D * cin + 512 * half + 512],
                            start=(cin == 0), stop=(cin == C - 1)), inc="T")
                p.wait("vector", "T", tv)
                ev = p.op("vector", lambda e, cv=cv: e.tensor_copy(out=cv["qtmp"][:],
                                                                   in_=t["pO"][:]), inc="V")
                p.wait("tensor", "V", ev)
                tv = None
                for c in range(C):
                    tv = p.op("tensor", lambda e, cv=cv, c=c: e.transpose(
                        t["ptrb"][:, 128 * c:128 * c + 128],
                        cv["qtmp"][:, 128 * c:128 * c + 128], t["ident_bf"][:]), inc="T")
                p.wait("vector", "T", tv)
                qv = p.op("vector", lambda e, cv=cv: e.tensor_copy(out=cv["qfm_t"][:],
                                                                   in_=t["ptrb"][:]), inc="V")
                p.wait("tensor", "V", qv)
                nk = 128 * (qt + 1)
                nkc = (nk + 511) // 512
                fmarks = {}

                def c3_front(h):
                    jc, hp = h // 2, 64 * (h % 2)
                    atm = cv["attn_tm"] if h % 2 == 0 else cv["attn_tm2"]
                    p.wait("tensor", "S", p.val("S"))
                    p.wait("tensor", "V", p.val("V"))
                    mtc = (128 * qt) // 512
                    tv = None
                    for tc in range(nkc):
                        w = min(512, nk - 512 * tc)
                        dstp = t["pA"] if tc < 2 else t["pB"]
                        off = 512 * (tc % 2)
                        tv = p.op("tensor", lambda e, cv=cv, qt=qt, tc=tc, w=w, dstp=dstp,
                                  off=off, jc=jc, hp=hp, mtc=mtc: e.matmul(
                            dstp[:, off:off + w],
                            cv["qfm_t"][hp:hp + 64, 128 * jc:128 * jc + 128],
                            cv["kfm"][hp:hp + 64, SL * jc + 512 * tc: SL * jc + 512 * tc + w],
                            start=True, stop=(tc != mtc)), inc="T")
                        if tc == mtc:
                            moff = 512 * (mtc % 2) + (128 * qt - 512 * mtc)
                            tv = p.op("tensor", lambda e, dstp=dstp, moff=moff: e.matmul(
                                dstp[:, moff:moff + 128], t["ident_bf"][:],
                                t["mask_bfm"][:], start=False, stop=True), inc="T")
                    p.wait("scalar", "T", tv)
                    if ("tr", h - 2) in fmarks:
                        p.wait("scalar", "T", fmarks[("tr", h - 2)])
                    sv = None
                    for tc in range(nkc):
                        w = min(512, nk - 512 * tc)
                        srcp = t["pA"] if tc < 2 else t["pB"]
                        off = 512 * (tc % 2)
                        sv = p.op("scalar", lambda e, atm=atm, tc=tc, w=w, srcp=srcp, off=off:
                                  e.activation(
                            out=atm[:, 512 * tc:512 * tc + w], in_=srcp[:, off:off + w],
                            func=AF.Exp, bias=t["shift48"][:], scale=SCALE,
                            accum_out=t["zc4"][:, tc:tc + 1]), inc="S")
                    fmarks[("exp", h)] = sv
                    p.wait("vector", "S", sv)
                    p.op("vector", lambda e, nkc=nkc: e.reduce_sum(
                        t["zc"][:], t["zc4"][:, 0:nkc], axis=AX.X))
                    fmarks[("z", h)] = p.op("vector", lambda e, cv=cv, h=h: e.reciprocal(
                        cv["rza16"][:, h:h + 1], t["zc"][:]), inc="V")

                def c3_back(h):
                    atm = cv["attn_tm"] if h % 2 == 0 else cv["attn_tm2"]
                    akm = cv["attn_km"] if h % 2 == 0 else cv["attn_km2"]
                    p.wait("tensor", "S", fmarks[("exp", h)])
                    nb_ = qt + 1
                    tv = None
                    for r in range((nb_ + 7) // 8):
                        p.wait("tensor", "V", p.val("V"))
                        w8 = min(8, nb_ - 8 * r)
                        tv = None
                        for j in range(w8):
                            tv = p.op("tensor", lambda e, atm=atm, r=r, j=j: e.transpose(
                                t["ptrk"][:, 128 * j:128 * j + 128],
                                atm[:, 1024 * r + 128 * j: 1024 * r + 128 * j + 128],
                                t["ident_bf"][:]), inc="T")
                        p.wait("vector", "T", tv)
                        if ("outmm", h - 2) in fmarks:
                            p.wait("vector", "T", fmarks[("outmm", h - 2)])
                        p.op("vector", lambda e, akm=akm, r=r, w8=w8: e.tensor_copy(
                            out=akm[:, 1024 * r:1024 * r + 128 * w8],
                            in_=t["ptrk"][:, 0:128 * w8]), inc="V")
                    fmarks[("tr", h)] = tv
                    p.wait("tensor", "V", p.val("V"))
                    tv = None
                    for kt in range(qt + 1):
                        tv = p.op("tensor", lambda e, cv=cv, akm=akm, kt=kt, qt=qt, h=h: e.matmul(
                            t["pO"][:, 64 * h:64 * h + 64],
                            akm[:, 128 * kt:128 * kt + 128],
                            cv["vtm"][:, D * kt + 64 * h: D * kt + 64 * h + 64],
                            start=(kt == 0), stop=(kt == qt)), inc="T")
                    fmarks[("outmm", h)] = tv
                    p.wait("scalar", "T", tv)
                    p.op("scalar", lambda e, cv=cv, h=h: e.activation(
                        out=cv["out_sb"][:, 64 * h:64 * h + 64],
                        in_=t["pO"][:, 64 * h:64 * h + 64], func=AF.Copy,
                        scale=cv["rza16"][:, h:h + 1]), inc="S")

                c3_front(0)
                for h in range(1, 16):
                    c3_front(h)
                    c3_back(h - 1)
                c3_back(15)
                p.wait("tensor", "V", p.val("V"))
                p.wait("tensor", "S", p.val("S"))
                tv = None
                for c in range(C):
                    tv = p.op("tensor", lambda e, cv=cv, c=c: e.transpose(
                        t["ptrb"][:, 128 * c:128 * c + 128],
                        cv["out_sb"][:, 128 * c:128 * c + 128], t["ident_bf"][:]), inc="T")
                p.wait("vector", "T", tv)
                fv = p.op("vector", lambda e, cv=cv: e.tensor_copy(out=cv["ofm"][:],
                                                                   in_=t["ptrb"][:]), inc="V")
                p.wait("tensor", "V", fv)
                tv = None
                for half in range(2):
                    for c in range(C):
                        tv = p.op("tensor", lambda e, cv=cv, half=half, c=c: e.matmul(
                            t["pA"][:, 512 * half:512 * half + 512],
                            cv["ofm"][:, 128 * c:128 * c + 128],
                            t["W1"][:, D * c + 512 * half: D * c + 512 * half + 512],
                            start=(c == 0), stop=(c == C - 1)), inc="T")
                p.wait("vector", "T", tv)
                p.op("vector", lambda e, cv=cv, qt=qt: e.tensor_tensor(
                    out=cv["r1"][:], in0=t["pA"][:], in1=xbf_tile(qt), op=ALU.add), inc="V")
                emit_ln_stats2(p, t, cv["r1"][:], t["mu2"][:], t["rstd2"][:], D, cv["out_sb"][:])
                p.op("vector", lambda e, cv=cv, qt=qt: e.tensor_scalar(
                    out=xbf_tile(qt), in0=cv["r1"][:], scalar1=t["mu2"][:],
                    scalar2=t["rstd2"][:], op0=ALU.subtract, op1=ALU.mult), inc="V")
            p.barrier()

    def emit_D(seq, n_store_ref):
        p.mark(f'D{seq}')
        with contextlib.ExitStack() as st_:
            d = {}
            for nm, shape, dt in (("acc", [128, T * D], F32), ("Wd", [128, 8 * D], BF16),
                                  ("gwbf", [128, C * 16], BF16), ("sgwbf", [128, C], BF16),
                                  ("exps", [128, T * 16], F32), ("combine", [128, T * 16], F32),
                                  ("sglog", [128, T], F32), ("sgate", [128, T], F32),
                                  ("gs_bf", [128, D], BF16), ("U_bf", [128, D], BF16),
                                  ("U_bf2", [128, D], BF16), ("Ufm2", [128, D], BF16),
                                  ("Ufm", [128, D], BF16), ("r1", [128, D], F32),
                                  ("m1", [128, 1], F32), ("m2", [128, 1], F32),
                                  ("mask16", [128, 16], F32), ("pmsk", [128, 16], F32),
                                  ("s12", [128, 1], F32), ("rs12", [128, 1], F32)):
                d[nm] = st_.enter_context(nc.sbuf_tensor(f"D{seq}_{nm}", shape, dt))[:]
            p.barrier()
            build_xfm()
            p.wait("sync", "V", p.val("V"))
            p.wait("sync", "S", p.val("S"))
            wload("gwT", d["gwbf"], C * 16, war_T=p.val("T"))
            rsg = wload("sgwT", d["sgwbf"], C)
            p.wait("tensor", "Lw", rsg)
            for ti in range(T):
                p.wait("tensor", "S", p.val("S"))
                p.wait("tensor", "V", p.val("V"))
                tv = None
                for cin in range(C):
                    tv = p.op("tensor", lambda e, d=d, ti=ti, cin=cin: e.matmul(
                        t["pA"][:, 0:16],
                        t["xfm"][:, SL * cin + 128 * ti: SL * cin + 128 * ti + 128],
                        d["gwbf"][:, 16 * cin:16 * cin + 16],
                        start=(cin == 0), stop=(cin == C - 1)), inc="T")
                for cin in range(C):
                    tv = p.op("tensor", lambda e, d=d, ti=ti, cin=cin: e.matmul(
                        t["pA"][:, 128:129],
                        t["xfm"][:, SL * cin + 128 * ti: SL * cin + 128 * ti + 128],
                        d["sgwbf"][:, cin:cin + 1],
                        start=(cin == 0), stop=(cin == C - 1)), inc="T")
                p.wait("vector", "T", tv)
                rv = p.op("vector", lambda e: e.reduce_max(
                    t["nmax1"][:], t["pA"][:, 0:16], axis=AX.X, negate=True), inc="V")
                p.op("vector", lambda e, d=d, ti=ti: e.tensor_copy(
                    out=d["sglog"][:, ti:ti + 1], in_=t["pA"][:, 128:129]))
                p.wait("scalar", "V", p.val("V"))
                p.op("scalar", lambda e, d=d, ti=ti: e.activation(
                    out=d["exps"][:, 16 * ti:16 * ti + 16], in_=t["pA"][:, 0:16], func=AF.Exp,
                    bias=t["nmax1"][:], scale=1.0), inc="S")
            p.wait("scalar", "V", p.val("V"))
            p.op("scalar", lambda e, d=d: e.activation(out=d["sgate"][:], in_=d["sglog"][:],
                                                       func=AF.Sigmoid), inc="S")
            p.wait("vector", "S", p.val("S"))
            for ti in range(T):
                ex = d["exps"][:, 16 * ti:16 * ti + 16]
                cb = d["combine"][:, 16 * ti:16 * ti + 16]
                p.op("vector", lambda e, ex=ex, d=d: e.reduce_max(d["m1"][:], ex, axis=AX.X))
                p.op("vector", lambda e, ex=ex, d=d: e.tensor_scalar(
                    out=d["mask16"][:], in0=ex, scalar1=d["m1"][:], scalar2=None, op0=ALU.is_ge))
                p.op("vector", lambda e, ex=ex, d=d: e.scalar_tensor_tensor(
                    out=d["pmsk"][:], in0=d["mask16"][:], scalar=-1e30, in1=ex,
                    op0=ALU.mult, op1=ALU.add))
                p.op("vector", lambda e, d=d: e.reduce_max(d["m2"][:], d["pmsk"][:], axis=AX.X))
                p.op("vector", lambda e, ex=ex, d=d: e.tensor_scalar(
                    out=d["mask16"][:], in0=ex, scalar1=d["m2"][:], scalar2=None, op0=ALU.is_ge))
                p.op("vector", lambda e, d=d: e.tensor_tensor(
                    out=d["s12"][:], in0=d["m1"][:], in1=d["m2"][:], op=ALU.add))
                p.op("vector", lambda e, d=d: e.reciprocal(d["rs12"][:], d["s12"][:]))
                p.op("vector", lambda e, ex=ex, d=d: e.tensor_tensor(
                    out=d["pmsk"][:], in0=ex, in1=d["mask16"][:], op=ALU.mult))
                p.op("vector", lambda e, cb=cb, d=d: e.tensor_scalar(
                    out=cb, in0=d["pmsk"][:], scalar1=d["rs12"][:], scalar2=None, op0=ALU.mult))
            p.mark(f'D{seq}.D1')
            for fci, (f0, wfc) in enumerate(FCS):
                tval = p.val("T")
                rg = wload(f"swgT{fci}", t["W0"], C * wfc, war_T=tval)
                ru = wload(f"swuT{fci}", t["W1"], C * wfc)
                rd = wload(f"swdT{fci}", d["Wd"], (wfc // 128) * D)
                p.wait("tensor", "Lw", rd)
                nh = wfc // 512 + (1 if wfc % 512 else 0)
                nc2 = wfc // 128
                sm = {}

                def d1_front(ti, nh=nh, nc2=nc2, wfc=wfc, sm=sm):
                    ub = d["U_bf"] if ti % 2 == 0 else d["U_bf2"]
                    gb = d["gs_bf"] if ti % 2 == 0 else d["Ufm2"]
                    if ti == 0:
                        p.wait("tensor", "S", p.val("S"))
                        p.wait("tensor", "V", p.val("V"))
                    else:
                        p.wait("tensor", "S", sm[("silu", ti - 1)])
                        p.wait("tensor", "V", sm[("mult", ti - 1)])
                    gstop = None
                    for half in range(nh):
                        w = min(512, wfc - 512 * half)
                        for cin in range(C):
                            gstop = p.op("tensor", lambda e, ti=ti, half=half, w=w, cin=cin,
                                         wfc=wfc: e.matmul(
                                t["pA"][:, 512 * half:512 * half + w],
                                t["xfm"][:, SL * cin + 128 * ti: SL * cin + 128 * ti + 128],
                                t["W0"][:, wfc * cin + 512 * half: wfc * cin + 512 * half + w],
                                start=(cin == 0), stop=(cin == C - 1)), inc="T")
                    ustop = None
                    for half in range(nh):
                        w = min(512, wfc - 512 * half)
                        for cin in range(C):
                            ustop = p.op("tensor", lambda e, ti=ti, half=half, w=w, cin=cin,
                                         wfc=wfc: e.matmul(
                                t["pB"][:, 512 * half:512 * half + w],
                                t["xfm"][:, SL * cin + 128 * ti: SL * cin + 128 * ti + 128],
                                t["W1"][:, wfc * cin + 512 * half: wfc * cin + 512 * half + w],
                                start=(cin == 0), stop=(cin == C - 1)), inc="T")
                    if ti >= 2:
                        p.wait("scalar", "V", sm[("mult", ti - 2)])
                    p.wait("scalar", "T", gstop)
                    sm[("silu", ti)] = p.op("scalar", lambda e, d=d, gb=gb, wfc=wfc:
                        e.activation(out=gb[:, 0:wfc], in_=t["pA"][:, 0:wfc],
                                     func=AF.Silu), inc="S")
                    p.wait("vector", "S", sm[("silu", ti)])
                    p.wait("vector", "T", ustop)
                    sm[("mult", ti)] = p.op("vector", lambda e, d=d, gb=gb, ub=ub, wfc=wfc:
                        e.tensor_tensor(out=ub[:, 0:wfc], in0=gb[:, 0:wfc],
                                        in1=t["pB"][:, 0:wfc], op=ALU.mult), inc="V")

                def d1_back(ti, nc2=nc2, wfc=wfc, sm=sm, fci=fci):
                    ub = d["U_bf"] if ti % 2 == 0 else d["U_bf2"]
                    if ("copy", ti - 1) in sm:
                        p.wait("tensor", "V", sm[("copy", ti - 1)])
                    else:
                        p.wait("tensor", "V", sm[("mult", ti)])
                    tv = None
                    for c2 in range(nc2):
                        tv = p.op("tensor", lambda e, ub=ub, c2=c2: e.transpose(
                            t["ptrb"][:, 128 * c2:128 * c2 + 128],
                            ub[:, 128 * c2:128 * c2 + 128], t["ident_bf"][:]), inc="T")
                    p.wait("vector", "T", tv)
                    sm[("copy", ti)] = p.op("vector", lambda e, d=d, wfc=wfc: e.tensor_copy(
                        out=d["Ufm"][:, 0:wfc], in_=t["ptrb"][:, 0:wfc]), inc="V")
                    p.wait("tensor", "V", sm[("copy", ti)])
                    tv = None
                    for half in range(2):
                        for c2 in range(nc2):
                            tv = p.op("tensor", lambda e, d=d, half=half, c2=c2, wfc=wfc,
                                      nc2=nc2: e.matmul(
                                t["pO"][:, 512 * half:512 * half + 512],
                                d["Ufm"][:, 128 * c2:128 * c2 + 128],
                                d["Wd"][:, D * c2 + 512 * half: D * c2 + 512 * half + 512],
                                start=(c2 == 0), stop=(c2 == nc2 - 1)), inc="T")
                    p.wait("vector", "T", tv)
                    sgc = d["sgate"][:, ti:ti + 1]
                    at = d["acc"][:, D * ti:D * ti + D]
                    if fci == 0:
                        p.op("vector", lambda e, at=at, sgc=sgc: e.tensor_scalar(
                            out=at, in0=t["pO"][:], scalar1=sgc, scalar2=None, op0=ALU.mult),
                            inc="V")
                    else:
                        p.op("vector", lambda e, at=at, sgc=sgc: e.scalar_tensor_tensor(
                            out=at, in0=t["pO"][:], scalar=sgc, in1=at,
                            op0=ALU.mult, op1=ALU.add), inc="V")

                d1_front(0)
                for ti in range(1, T):
                    d1_front(ti)
                    d1_back(ti - 1)
                d1_back(T - 1)
            p.wait("tensor", "V", p.val("V"))
            p.mark(f'D{seq}.D2')
            wg_ar = {0: t["W0"], 1: t["W1"]}
            wd_ar = {0: d["Wd"][:, 0:4 * D], 1: d["Wd"][:, 4 * D:8 * D]}
            t_done = {0: p.val("T"), 1: p.val("T")}
            eloads = {}

            def load_expert(ex_, ready_T):
                par = ex_ % 2
                rg = wload(f"wgu{ex_}", wg_ar[par], C * D, war_T=ready_T)
                rd = wload(f"wdn{ex_}", wd_ar[par], 4 * D)
                return rg, rd

            eloads[0] = load_expert(0, t_done[0])
            eloads[1] = load_expert(1, t_done[1])
            for ex_ in range(16):
                par = ex_ % 2
                wg, wd = wg_ar[par], wd_ar[par]
                rg, rd = eloads[ex_]
                p.wait("tensor", "Lw", rd)
                marks = {}

                def emit_gu2(ti, wg=wg):
                    ppar = ti % 2
                    pg = t["pA"] if ppar == 0 else t["pB"]
                    if ti >= 2:
                        p.wait("tensor", "S", marks[("silu", ti - 2)])
                        p.wait("tensor", "V", marks[("u", ti - 2)])
                    tv = None
                    for half in range(2):
                        for cin in range(C):
                            tv = p.op("tensor", lambda e, ti=ti, half=half, cin=cin, wg=wg,
                                      pg=pg: e.matmul(
                                pg[:, 512 * half:512 * half + 512],
                                t["xfm"][:, SL * cin + 128 * ti: SL * cin + 128 * ti + 128],
                                wg[:, D * cin + 512 * half: D * cin + 512 * half + 512],
                                start=(cin == 0), stop=(cin == C - 1)), inc="T")
                    marks[("gu", ti)] = p.val("T")

                p.wait("tensor", "V", p.val("V"))
                emit_gu2(0)
                for ti in range(T):
                    ppar = ti % 2
                    pg = t["pA"] if ppar == 0 else t["pB"]
                    if ti + 1 < T:
                        emit_gu2(ti + 1)
                    p.wait("scalar", "T", marks[("gu", ti)])
                    if ti >= 1:
                        p.wait("scalar", "V", marks[("u", ti - 1)])
                    marks[("silu", ti)] = p.op("scalar", lambda e, d=d, pg=pg: e.activation(
                        out=d["gs_bf"][:, 0:512], in_=pg[:, 0:512], func=AF.Silu), inc="S")
                    p.wait("vector", "S", marks[("silu", ti)])
                    ub = d["U_bf"] if ppar == 0 else d["U_bf2"]
                    cbc = d["combine"][:, 16 * ti + ex_:16 * ti + ex_ + 1]
                    p.op("vector", lambda e, d=d, pg=pg, ub=ub: e.tensor_tensor(
                        out=ub[:, 0:512], in0=d["gs_bf"][:, 0:512],
                        in1=pg[:, 512:1024], op=ALU.mult))
                    marks[("u", ti)] = p.op("vector", lambda e, cbc=cbc, ub=ub: e.tensor_scalar(
                        out=ub[:, 0:512], in0=ub[:, 0:512], scalar1=cbc,
                        scalar2=None, op0=ALU.mult), inc="V")
                    p.wait("tensor", "V", marks[("u", ti)])
                    if ti >= 1:
                        p.wait("tensor", "V", marks[("ufm", ti - 1)])
                    tv = None
                    for c2 in range(4):
                        tv = p.op("tensor", lambda e, c2=c2, ub=ub: e.transpose(
                            t["ptrb"][:, 128 * c2:128 * c2 + 128],
                            ub[:, 128 * c2:128 * c2 + 128], t["ident_bf"][:]), inc="T")
                    p.wait("vector", "T", tv)
                    uf = d["Ufm"] if ppar == 0 else d["Ufm2"]
                    marks[("ufm", ti)] = p.op("vector", lambda e, uf=uf: e.tensor_copy(
                        out=uf[:, 0:512], in_=t["ptrb"][:, 0:512]), inc="V")
                    p.wait("tensor", "V", marks[("ufm", ti)])
                    tv = None
                    for half in range(2):
                        for c2 in range(4):
                            tv = p.op("tensor", lambda e, d=d, half=half, c2=c2, wd=wd,
                                      uf=uf: e.matmul(
                                t["pO"][:, 512 * half:512 * half + 512],
                                uf[:, 128 * c2:128 * c2 + 128],
                                wd[:, D * c2 + 512 * half: D * c2 + 512 * half + 512],
                                start=(c2 == 0), stop=(c2 == 3)), inc="T")
                    p.wait("vector", "T", tv)
                    at = d["acc"][:, D * ti:D * ti + D]
                    p.op("vector", lambda e, at=at: e.tensor_tensor(
                        out=at, in0=at, in1=t["pO"][:], op=ALU.add), inc="V")
                t_done[par] = p.val("T")
                if ex_ + 2 < 16:
                    eloads[ex_ + 2] = load_expert(ex_ + 2, t_done[par])
            p.mark(f'D{seq}.D3')
            for ti in range(T):
                p.wait("vector", "O", p.val("O"))
                rm = p.op("vector", lambda e, d=d, ti=ti: e.tensor_tensor(
                    out=d["r1"][:], in0=d["acc"][:, D * ti:D * ti + D], in1=xbf_tile(ti),
                    op=ALU.add), inc="V")
                if seq + 1 < NSEQ:
                    if t.get("_stage_last") is not None:
                        p.wait("sync", "S", t["_stage_last"])
                    lv = p.op("sync", lambda e, seq=seq, ti=ti: e.dma_start(
                        out=t["stage"][:],
                        in_=x_dram[(seq + 1) * SL + 128 * ti: (seq + 1) * SL + 128 * ti + 128, :]),
                        inc="L", n=16)
                    p.wait("scalar", "L", lv)
                    p.wait("scalar", "V", rm)
                    t["_stage_last"] = p.op("scalar", lambda e, ti=ti: e.copy(
                        out=t["x_bf"][:, D * ti:D * ti + D], in_=t["stage"][:]), inc="S")
                emit_ln_stats2(p, t, d["r1"][:], t["mu2"][:], t["rstd2"][:], D, d["gs_bf"][:])
                vv = p.op("vector", lambda e, d=d: e.tensor_scalar(
                    out=d["r1"][:], in0=d["r1"][:], scalar1=t["mu2"][:],
                    scalar2=t["rstd2"][:], op0=ALU.subtract, op1=ALU.mult), inc="V")
                p.wait("gpsimd", "V", vv)
                p.op("gpsimd", lambda e, d=d, seq=seq, ti=ti: e.dma_start(
                    out=y_dram[seq * SL + 128 * ti: seq * SL + 128 * ti + 128, :],
                    in_=d["r1"][:]), inc="O", n=16)
                n_store_ref[0] += 1
            p.barrier()

    emit_phase0()

    n_store = [0]
    for seq in range(NSEQ):
        if seq == 0:
            load_x(seq)
        emit_A(seq)
        emit_B(seq)
        emit_C(seq)
        emit_D(seq, n_store)

    p.wait("gpsimd", "O", 16 * n_store[0])
    p.finalize()
    return nc


def emit_ln_stats2(p, t, src_ap, mu_col, rstd_col, nfree, sq_dump):
    p.wait("scalar", "V", p.val("V"))
    rv = p.op("vector", lambda e: e.reduce_sum(t["red1"][:], src_ap, axis=AX.X), inc="V")
    p.wait("vector", "V", rv)
    p.op("vector", lambda e: e.tensor_scalar(out=mu_col, in0=t["red1"][:],
                                             scalar1=1.0 / nfree, scalar2=None, op0=ALU.mult))
    sv = p.op("scalar", lambda e: e.activation(out=sq_dump, in_=src_ap, func=AF.Square,
                                               accum_out=t["red2"][:]), inc="S")
    p.wait("vector", "S", sv)
    p.op("vector", lambda e: e.tensor_scalar(out=t["red2"][:], in0=t["red2"][:],
                                             scalar1=1.0 / nfree, scalar2=None, op0=ALU.mult))
    p.op("vector", lambda e: e.tensor_tensor(out=t["red3"][:], in0=mu_col, in1=mu_col,
                                             op=ALU.mult))
    vv = p.op("vector", lambda e: e.tensor_tensor(out=t["red2"][:], in0=t["red2"][:],
                                                  in1=t["red3"][:], op=ALU.subtract), inc="V")
    p.wait("scalar", "V", vv)
    p.op("scalar", lambda e: e.activation(out=rstd_col, in_=t["red2"][:], func=AF.Ln,
                                          bias=t["eps_col"][:], scale=1.0))
    sv2 = p.op("scalar", lambda e: e.activation(out=rstd_col, in_=rstd_col, func=AF.Exp,
                                                scale=-0.5), inc="S")
    p.wait("vector", "S", sv2)


def host_inputs_fused(full):
    x = np.asarray(full["L0_attn_x"], dtype=np.float32).reshape(32 * SL, D)
    m = {}
    m.update({k: v for k, v in host_inputs_A(full)[0].items() if k != "x"})
    m.update({k: v for k, v in host_inputs_B(full, x)[0].items() if k != "x"})
    m.update({k: v for k, v in host_inputs_C(full, x)[0].items() if k != "x"})
    m.update({k: v for k, v in host_inputs_D(full, x)[0].items() if k != "x"})
    maps = []
    for core in range(NCORES):
        mm = dict(m)
        mm["x"] = np.ascontiguousarray(x[core * NSEQ * SL:(core + 1) * NSEQ * SL])
        maps.append(mm)
    return maps


LEAF_OF_H = []
for _h in range(16):
    _b3, _b2, _b0, _b1 = (_h >> 3) & 1, (_h >> 2) & 1, (_h >> 1) & 1, _h & 1
    LEAF_OF_H.append(8 * _b0 + 4 * _b1 + 2 * _b2 + _b3)


FUSED = True


def kernel(**inputs):
    cores = list(range(NCORES))
    if FUSED:
        res = run_bass_kernel_spmd(build_fused(), host_inputs_fused(inputs), core_ids=cores)
        return np.concatenate([r["y"] for r in res.results], axis=0).reshape(32, SL, D)
    resA = run_bass_kernel_spmd(build_launch_A(), host_inputs_A(inputs), core_ids=cores)
    x1 = np.concatenate([r["y"] for r in resA.results], axis=0)
    resB = run_bass_kernel_spmd(build_launch_B(), host_inputs_B(inputs, x1), core_ids=cores)
    x2 = np.concatenate([r["y"] for r in resB.results], axis=0)
    resC = run_bass_kernel_spmd(build_launch_C(), host_inputs_C(inputs, x2), core_ids=cores)
    x3 = np.concatenate([r["y"] for r in resC.results], axis=0)
    resD = run_bass_kernel_spmd(build_launch_D(), host_inputs_D(inputs, x3), core_ids=cores)
    x4 = np.concatenate([r["y"] for r in resD.results], axis=0)
    return x4.reshape(32, SL, D)
